# version 03
# baseline (speedup 1.0000x reference)
import numpy as np
import ml_dtypes
import concourse.bass as bass
import concourse.mybir as mybir
from concourse.bass_utils import run_bass_kernel_spmd

F32 = mybir.dt.float32
BF16 = mybir.dt.bfloat16
I32 = mybir.dt.int32
AF = mybir.ActivationFunctionType
ALU = mybir.AluOpType
AX = mybir.AxisListType

N_CORES = 8
D = 2048
T_LOC = 1024
NT = T_LOC // 128
SEQ = 4096
B = 2
DH = 128
NH = 16
EPS_LN = 1e-5
EPS_Q = 1e-5
EPS_RMS = 1e-6
INV_SQRT_DH = float(1.0 / np.sqrt(128.0))

F_L0 = 8192
F_L1 = 8192
EL0 = 16384

QW_SHAPES = {
    "L0gu": (EL0, D),
    "L0dn": (D, F_L0),
    "L1w1": (F_L1, D),
    "L1w2": (D, F_L1),
}
QW_NAMES = ["L0gu", "L0dn", "L1w1", "L1w2"]


def owned_tiles(core):
    k = core % 4
    return [k + 4 * j for j in range(NT)]



class Sem:

    def __init__(self, handle):
        self.h = handle
        self.n = 0

    def inc(self, instr, k=1):
        instr.then_inc(self.h, k)
        self.n += k
        return self.n


class Ctx:
    def __init__(self, nc, block):
        self.nc = nc
        self.block = block
        self.sems = {}
        self._stack = None

    def sem(self, name):
        return self.sems[name]


def ceil_div(a, b):
    return (a + b - 1) // b


_UID = [0]


def _u(name):
    _UID[0] += 1
    return f"{name}_{_UID[0]}" 



def build_p0():
    nc = bass.Bass("TRN2", target_bir_lowering=False)
    ins = {}
    for name, (r, c) in QW_SHAPES.items():
        ins[name] = nc.dram_tensor("w_" + name, [r // 8, c], F32, kind="ExternalInput").ap()
    out = nc.dram_tensor("partials", [1, len(QW_NAMES)], F32, kind="ExternalOutput").ap()

    CH = 4096
    with (
        nc.sbuf_tensor("wbuf", [128, 2 * CH], F32) as wbuf,
        nc.sbuf_tensor("acc", [128, len(QW_NAMES)], F32) as acc,
        nc.sbuf_tensor("accs", [128, 1], F32) as accs,
        nc.sbuf_tensor("ones", [128, 1], F32) as ones,
        nc.sbuf_tensor("red", [128, 2], F32) as red,
        nc.sbuf_tensor("outt", [1, len(QW_NAMES)], F32) as outt,
        nc.psum_tensor("ps1", [1, len(QW_NAMES)], F32) as ps1,
        nc.semaphore("ld0") as ld0_h,
        nc.semaphore("ld1") as ld1_h,
        nc.semaphore("cm") as cm_h,
        nc.semaphore("ch") as ch_h,
        nc.semaphore("st") as st_h,
        nc.Block() as block,
    ):
        ld = [Sem(ld0_h), Sem(ld1_h)]
        cm = Sem(cm_h)
        ch = Sem(ch_h)
        st = Sem(st_h)

        loads = []
        for wi, name in enumerate(QW_NAMES):
            r, c = QW_SHAPES[name]
            r //= 8
            flat = ins[name].rearrange("r c -> (r c)")
            total = r * c
            assert total % 128 == 0
            per_row = total // 128
            view = flat.rearrange("(p f) -> p f", p=128)
            for off in range(0, per_row, CH):
                w = min(CH, per_row - off)
                loads.append((wi, view[:, off:off + w], w))

        @block.sync
        def _(sync):
            for i, (wi, ap, w) in enumerate(loads):
                buf = wbuf[:, (i % 2) * CH:(i % 2) * CH + w]
                if i >= 2:
                    sync.wait_ge(cm.h, i - 1)
                ld[i % 2].inc(sync.dma_start(out=buf, in_=ap), 16)

        @block.vector
        def _(vector):
            ch.inc(vector.memset(acc[:], 0.0), 1)
            vector.memset(ones[:], 1.0)
            for i, (wi, ap, w) in enumerate(loads):
                buf = wbuf[:, (i % 2) * CH:(i % 2) * CH + w]
                vector.wait_ge(ld[i % 2].h, 16 * (i // 2 + 1))
                rr = red[:, (i % 2):(i % 2) + 1]
                ch.inc(vector.tensor_reduce(out=rr, in_=buf, axis=AX.X, op=ALU.add,
                                            apply_absolute_value=True), 1)
                vector.wait_ge(ch.h, ch.n)
                cm.inc(vector.tensor_tensor(out=acc[:, wi:wi + 1], in0=acc[:, wi:wi + 1],
                                            in1=rr, op=ALU.add), 1)
            vector.wait_ge(cm.h, cm.n)
            cm.inc(vector.tensor_copy(accs[:], acc[:, 0:1]), 1)

        @block.tensor
        def _(tensor):
            tensor.wait_ge(cm.h, len(loads) + 1)
            tensor.matmul(ps1[:, :], ones[:, :], acc[:, :], start=True, stop=True).then_inc(cm.h, 1)
        cm.n += 1

        @block.vector
        def _(vector):
            vector.wait_ge(cm.h, len(loads) + 2)
            cm.inc(vector.tensor_copy(outt[:], ps1[:, :]), 1)

        @block.sync
        def _(sync):
            sync.wait_ge(cm.h, len(loads) + 3)
            st.inc(sync.dma_start(out=out[:, :], in_=outt[:]), 16)
            sync.wait_ge(st.h, 16)
    return nc




def emit_scales(nc, block, sems, partials_ap, consts, ones_col, ones_row, counts):
    ld, ch = sems["ld0"], sems["ch_v"]
    mm = sems["mm"]
    NW = len(QW_NAMES)

    with (
        nc.sbuf_tensor(_u("sc_part"), [8, NW], F32) as part,
        nc.sbuf_tensor(_u("sc_row"), [1, NW], F32) as row,
        nc.sbuf_tensor(_u("sc_cnt"), [1, NW], F32) as cnt,
        nc.psum_tensor(_u("sc_ps"), [1, NW], F32) as ps,
        nc.psum_tensor(_u("sc_psb"), [128, NW], F32) as psb,
    ):
        @block.sync
        def _(sync):
            ld.inc(sync.dma_start(out=part[:, :], in_=partials_ap), 16)

        @block.vector
        def _(vector):
            for wi, name in enumerate(QW_NAMES):
                ch.inc(vector.memset(cnt[:, wi:wi + 1], 1.0 / counts[wi]), 1)
            ch.inc(vector.memset(ones_col[:], 1.0), 1)
            ch.inc(vector.memset(ones_row[:], 1.0), 1)

        @block.tensor
        def _(tensor):
            tensor.wait_ge(ld.h, ld.n)
            tensor.wait_ge(ch.h, ch.n)
            mm.inc(tensor.matmul(ps[:, :], ones_col[0:8, :], part[:, :],
                                 start=True, stop=True), 1)

        @block.vector
        def _(vector):
            vector.wait_ge(mm.h, mm.n)
            ch.inc(vector.tensor_tensor(out=row[:, :], in0=ps[:, :], in1=cnt[:, :],
                                        op=ALU.mult), 1)
            vector.wait_ge(ch.h, ch.n)
            ch.inc(vector.tensor_scalar(out=row[:, :], in0=row[:, :], scalar1=EPS_Q,
                                        scalar2=None, op0=ALU.max), 1)

        @block.tensor
        def _(tensor):
            tensor.wait_ge(ch.h, ch.n)
            mm.inc(tensor.matmul(psb[:, :], ones_row[:, 0:128], row[:, :],
                                 start=True, stop=True), 1)

        @block.vector
        def _(vector):
            vector.wait_ge(mm.h, mm.n)
            for wi in range(NW):
                src = psb[:, wi:wi + 1]
                ch.inc(vector.tensor_scalar(out=consts[:, 4 * wi:4 * wi + 1], in0=src,
                                            scalar1=0.5, scalar2=None, op0=ALU.mult), 1)
                ch.inc(vector.tensor_scalar(out=consts[:, 4 * wi + 1:4 * wi + 2], in0=src,
                                            scalar1=-0.5, scalar2=None, op0=ALU.mult), 1)
                ch.inc(vector.tensor_scalar(out=consts[:, 4 * wi + 2:4 * wi + 3], in0=src,
                                            scalar1=1.0 / 254.0, scalar2=None, op0=ALU.mult), 1)
                ch.inc(vector.tensor_copy(consts[:, 4 * wi + 3:4 * wi + 4], src), 1)


def emit_weight_prep(nc, block, sems, w_in, wnat_dram, wt_out, R, C, ternary,
                     consts, wi, bufs, flat_rows, passA_cols, wnat_geo,
                     do_transpose=True):
    ld = [sems["wp_ld0"], sems["wp_ld1"]]
    sg = sems["wp_sg"]
    cv = [sems["wp_cv0"], sems["wp_cv1"]]
    stg = [sems["wp_st0"], sems["wp_st1"]]
    tt = [sems["wp_tt0"], sems["wp_tt1"]]
    wo = [sems["wp_wo0"], sems["wp_wo1"]]
    wpf, sg1, sg2, wpb, wtt = (bufs["wpf"], bufs["sg1"], bufs["sg2"],
                               bufs["wpb"], bufs["wtt"])
    CW = passA_cols
    nrb = flat_rows // 128
    nct = C // 128
    wtt_cols = wtt.shape[1]
    dbl_tt = (2 * R <= wtt_cols)

    ldb = [ld[0].n, ld[1].n]
    sgb = sg.n
    cvb = [cv[0].n, cv[1].n]
    stb = [stg[0].n, stg[1].n]
    ttb = [tt[0].n, tt[1].n]
    wob = [wo[0].n, wo[1].n]

    @block.sync
    def _(sync):
        for rb in range(nrb):
            p = rb % 2
            if rb >= 2:
                sync.wait_ge(cv[p].h, cvb[p] + (rb // 2))
            sync.dma_start(out=wpf[:, p * CW:p * CW + CW],
                           in_=w_in[rb * 128:(rb + 1) * 128, :]).then_inc(ld[p].h, 16)
    ld[0].n += 16 * ((nrb + 1) // 2)
    ld[1].n += 16 * (nrb // 2)

    if ternary:
        @block.scalar
        def _(scalar):
            for rb in range(nrb):
                p = rb % 2
                scalar.wait_ge(ld[p].h, ldb[p] + 16 * (rb // 2 + 1))
                if rb >= 1:
                    scalar.wait_ge(cv[0].h if (rb - 1) % 2 == 0 else cv[1].h,
                                   (cvb[0] + (rb - 1) // 2 + 1) if (rb - 1) % 2 == 0
                                   else (cvb[1] + (rb - 1) // 2 + 1))
                src = wpf[:, p * CW:p * CW + CW]
                scalar.activation(out=sg1[:, :], in_=src, func=AF.Sign,
                                  bias=consts[:, 4 * wi:4 * wi + 1],
                                  scale=consts[:, CC["neg1"]:CC["neg1"] + 1])
                scalar.activation(out=sg2[:, :], in_=src, func=AF.Sign,
                                  bias=consts[:, 4 * wi + 1:4 * wi + 2],
                                  scale=consts[:, CC["neg1"]:CC["neg1"] + 1]
                                  ).then_inc(sg.h, 1)
        sg.n += nrb

        @block.vector
        def _(vector):
            for rb in range(nrb):
                p = rb % 2
                vector.wait_ge(sg.h, sgb + rb + 1)
                if rb >= 2:
                    vector.wait_ge(stg[p].h, stb[p] + 16 * (rb // 2))
                vector.tensor_tensor(out=wpb[:, p * CW:p * CW + CW],
                                     in0=sg1[:, :], in1=sg2[:, :],
                                     op=ALU.add).then_inc(cv[p].h, 1)
    else:
        @block.vector
        def _(vector):
            for rb in range(nrb):
                p = rb % 2
                vector.wait_ge(ld[p].h, ldb[p] + 16 * (rb // 2 + 1))
                if rb >= 2:
                    vector.wait_ge(stg[p].h, stb[p] + 16 * (rb // 2))
                vector.tensor_copy(wpb[:, p * CW:p * CW + CW],
                                   wpf[:, p * CW:p * CW + CW]).then_inc(cv[p].h, 1)
    cv[0].n += (nrb + 1) // 2
    cv[1].n += nrb // 2

    @block.gpsimd
    def _(gpsimd):
        for rb in range(nrb):
            p = rb % 2
            gpsimd.wait_ge(cv[p].h, cvb[p] + (rb // 2 + 1))
            gpsimd.dma_start(out=wnat_dram[rb * 128:(rb + 1) * 128, :],
                             in_=wpb[:, p * CW:p * CW + CW]).then_inc(stg[p].h, 16)
    stg[0].n += 16 * ((nrb + 1) // 2)
    stg[1].n += 16 * (nrb // 2)

    if not do_transpose:
        return

    @block.sync
    def _(sync):
        sync.wait_ge(stg[0].h, stg[0].n)
        sync.wait_ge(stg[1].h, stg[1].n)
        for ct in range(nct):
            sync.wait_ge(wo[0].h, wob[0] + 16 * ct)
            sync.dma_start_transpose(out=wtt[:, 0:R],
                                     in_=wnat_geo[:, ct * 128:(ct + 1) * 128]
                                     ).then_inc(tt[0].h, 16)
    tt[0].n += 16 * nct

    @block.gpsimd
    def _(gpsimd):
        for ct in range(nct):
            gpsimd.wait_ge(tt[0].h, ttb[0] + 16 * (ct + 1))
            gpsimd.dma_start(out=wt_out[ct * 128:(ct + 1) * 128, :],
                             in_=wtt[:, 0:R]).then_inc(wo[0].h, 16)
    wo[0].n += 16 * nct


def emit_lnq_tiles(nc, block, sems, xs, k1nat, consts, tc, sqs, ybuf, kibuf,
                   dbuf, t1buf, nkbuf, ready, quant, sc_out, sc_wcol, xn_nat):
    ch_v, ch_s, kn = sems["ch_v"], sems["ch_s"], sems["kn"]
    rlist = ready if isinstance(ready, list) else [ready]
    invD = 1.0 / D

    for t in range(NT):
        xt = xs[:, t * D:(t + 1) * D]

        @block.vector
        def _(vector, t=t, xt=xt):
            for rsem, rcnt in rlist:
                vector.wait_ge(rsem.h, rcnt)
            ch_v.inc(vector.tensor_reduce(out=tc[:, 0:1], in_=xt, axis=AX.X,
                                          op=ALU.add), 1)
            ch_v.inc(vector.tensor_reduce(out=tc[:, 3:4], in_=xt, axis=AX.X,
                                          op=ALU.max), 1)
            ch_v.inc(vector.tensor_reduce(out=tc[:, 4:5], in_=xt, axis=AX.X,
                                          op=ALU.min), 1)
            vector.wait_ge(ch_v.h, ch_v.n)
            ch_v.inc(vector.tensor_scalar(out=tc[:, 0:1], in0=tc[:, 0:1],
                                          scalar1=invD, scalar2=None,
                                          op0=ALU.mult), 1)
            ch_v.inc(vector.memset(tc[:, 1:2], 0.0), 1)

        @block.scalar
        def _(scalar, t=t, xt=xt):
            for rsem, rcnt in rlist:
                scalar.wait_ge(rsem.h, rcnt)
            scalar.wait_ge(ch_v.h, ch_v.n)
            ch_s.inc(scalar.activation(out=sqs[:, :], in_=xt, func=AF.Square,
                                       accum_out=tc[:, 1:2]), 1)

        @block.vector
        def _(vector, t=t):
            vector.wait_ge(ch_s.h, ch_s.n)
            vector.wait_ge(ch_v.h, ch_v.n)
            ch_v.inc(vector.tensor_tensor(out=tc[:, 5:6], in0=tc[:, 0:1],
                                          in1=tc[:, 0:1], op=ALU.mult), 1)
            ch_v.inc(vector.tensor_scalar(out=tc[:, 1:2], in0=tc[:, 1:2],
                                          scalar1=invD, scalar2=None,
                                          op0=ALU.mult), 1)
            vector.wait_ge(ch_v.h, ch_v.n)
            ch_v.inc(vector.tensor_tensor(out=tc[:, 1:2], in0=tc[:, 1:2],
                                          in1=tc[:, 5:6], op=ALU.subtract), 1)

        @block.scalar
        def _(scalar, t=t):
            scalar.wait_ge(ch_v.h, ch_v.n)
            ch_s.inc(scalar.activation(out=tc[:, 5:6], in_=tc[:, 1:2],
                                       func=AF.Sqrt,
                                       bias=consts[:, CC["epsln"]:CC["epsln"] + 1]), 1)

        @block.vector
        def _(vector, t=t, xt=xt):
            vector.wait_ge(ch_s.h, ch_s.n)
            ch_v.inc(vector.reciprocal(tc[:, 2:3], tc[:, 5:6]), 1)
            vector.wait_ge(ch_v.h, ch_v.n)
            if quant:
                ch_v.inc(vector.tensor_tensor(out=tc[:, 3:4], in0=tc[:, 3:4],
                                              in1=tc[:, 0:1], op=ALU.subtract), 1)
                ch_v.inc(vector.tensor_tensor(out=tc[:, 4:5], in0=tc[:, 0:1],
                                              in1=tc[:, 4:5], op=ALU.subtract), 1)
                vector.wait_ge(ch_v.h, ch_v.n)
                ch_v.inc(vector.tensor_tensor(out=tc[:, 3:4], in0=tc[:, 3:4],
                                              in1=tc[:, 4:5], op=ALU.max), 1)
                vector.wait_ge(ch_v.h, ch_v.n)
                ch_v.inc(vector.tensor_tensor(out=tc[:, 3:4], in0=tc[:, 3:4],
                                              in1=tc[:, 2:3], op=ALU.mult), 1)
                vector.wait_ge(ch_v.h, ch_v.n)
                ch_v.inc(vector.tensor_scalar(out=tc[:, 3:4], in0=tc[:, 3:4],
                                              scalar1=EPS_Q, scalar2=None,
                                              op0=ALU.max), 1)
                vector.wait_ge(ch_v.h, ch_v.n)
                ch_v.inc(vector.tensor_tensor(out=sc_out[:, t:t + 1],
                                              in0=tc[:, 3:4],
                                              in1=consts[:, sc_wcol:sc_wcol + 1],
                                              op=ALU.mult), 1)
                ch_v.inc(vector.reciprocal(tc[:, 6:7], tc[:, 3:4]), 1)
                vector.wait_ge(ch_v.h, ch_v.n)
                ch_v.inc(vector.tensor_scalar(out=tc[:, 6:7], in0=tc[:, 6:7],
                                              scalar1=127.0, scalar2=None,
                                              op0=ALU.mult), 1)
                vector.wait_ge(ch_v.h, ch_v.n)
                ch_v.inc(vector.tensor_tensor(out=tc[:, 7:8], in0=tc[:, 2:3],
                                              in1=tc[:, 6:7], op=ALU.mult), 1)
                vector.wait_ge(ch_v.h, ch_v.n)
                ch_v.inc(vector.scalar_tensor_tensor(out=tc[:, 8:9],
                                                     in0=tc[:, 0:1], scalar=-1.0,
                                                     in1=tc[:, 7:8],
                                                     op0=ALU.mult, op1=ALU.mult), 1)
            else:
                ch_v.inc(vector.tensor_copy(tc[:, 7:8], tc[:, 2:3]), 1)
                ch_v.inc(vector.scalar_tensor_tensor(out=tc[:, 8:9],
                                                     in0=tc[:, 0:1], scalar=-1.0,
                                                     in1=tc[:, 2:3],
                                                     op0=ALU.mult, op1=ALU.mult), 1)

        @block.scalar
        def _(scalar, t=t, xt=xt):
            scalar.wait_ge(ch_v.h, ch_v.n)
            out = ybuf[:, :] if quant else nkbuf[:, :]
            ch_s.inc(scalar.activation(out=out, in_=xt, func=AF.Identity,
                                       scale=tc[:, 7:8], bias=tc[:, 8:9]), 1)

        if quant:
            @block.vector
            def _(vector, t=t):
                vector.wait_ge(ch_s.h, ch_s.n)
                if t >= 1:
                    vector.wait_ge(kn.h, kn.n)
                ch_v.inc(vector.tensor_copy(kibuf[:, :], ybuf[:, :]), 1)
                vector.wait_ge(ch_v.h, ch_v.n)
                ch_v.inc(vector.tensor_tensor(out=dbuf[:, :], in0=ybuf[:, :],
                                              in1=kibuf[:, :], op=ALU.subtract), 1)
                vector.wait_ge(ch_v.h, ch_v.n)
                ch_v.inc(vector.scalar_tensor_tensor(out=t1buf[:, :], in0=dbuf[:, :],
                                                     scalar=0.5, in1=kibuf[:, :],
                                                     op0=ALU.is_ge, op1=ALU.add), 1)
                vector.wait_ge(ch_v.h, ch_v.n)
                ch_v.inc(vector.scalar_tensor_tensor(out=nkbuf[:, :], in0=dbuf[:, :],
                                                     scalar=-0.5, in1=t1buf[:, :],
                                                     op0=ALU.is_le, op1=ALU.subtract), 1)

            @block.gpsimd
            def _(gpsimd, t=t):
                gpsimd.wait_ge(ch_v.h, ch_v.n)
                kn.inc(gpsimd.dma_start(out=k1nat[t * 128:(t + 1) * 128, :],
                                        in_=nkbuf[:, :]), 16)
        else:
            @block.gpsimd
            def _(gpsimd, t=t):
                gpsimd.wait_ge(ch_s.h, ch_s.n)
                kn.inc(gpsimd.dma_start(out=xn_nat[t * 128:(t + 1) * 128, :],
                                        in_=nkbuf[:, :]), 16)


def emit_ffn_up_glu(nc, block, sems, k1T, wt_up, g_dram, scU, m2, Fh, glu,
                    silu, bufs):
    wl = [sems["wl0"], sems["wl1"]]
    mm, ue_s, ue_v, gst = sems["mm"], sems["ue_s"], sems["ue_v"], sems["gst"]
    ch_v = sems["ch_v"]
    nch = Fh // 512
    KT = 16
    wlb = [wl[0].n, wl[1].n]
    mmb = mm.n
    usb, uvb = ue_s.n, ue_v.n
    gsb = gst.n
    wua, wub, ga, gb, rmax = (bufs["wua"], bufs["wub"], bufs["ga"], bufs["gb"],
                              bufs["rmax"])

    with (
        nc.psum_tensor(_u("ps_upA"), [128, 2 * 512], F32) as psA,
        nc.psum_tensor(_u("ps_upB"), [128, 2 * 512], F32) as psB,
    ):
        @block.vector
        def _(vector):
            ch_v.inc(vector.memset(m2[:, :], 0.0), 1)
        m2_init = ch_v.n

        per_c = 32 if glu else 16

        @block.sync
        def _(sync):
            sync.wait_ge(sems["tt0"].h, sems["tt0"].n)
            for c in range(nch):
                p = c % 2
                if c >= 2:
                    sync.wait_ge(ue_v.h, uvb + (c - 1) * NT)
                ap_a = wt_up[:, c * 512:(c + 1) * 512].rearrange(
                    "(dt q) f -> q dt f", q=128)
                sync.dma_start(out=wua[:, p * KT * 512:(p + 1) * KT * 512]
                               .rearrange("q (dt f) -> q dt f", f=512),
                               in_=ap_a).then_inc(wl[p].h, 16)
                if glu:
                    ap_b = wt_up[:, Fh + c * 512:Fh + (c + 1) * 512].rearrange(
                        "(dt q) f -> q dt f", q=128)
                    sync.dma_start(out=wub[:, p * KT * 512:(p + 1) * KT * 512]
                                   .rearrange("q (dt f) -> q dt f", f=512),
                                   in_=ap_b).then_inc(wl[p].h, 16)
        wl[0].n += per_c * ((nch + 1) // 2)
        wl[1].n += per_c * (nch // 2)

        @block.tensor
        def _(tensor):
            tensor.wait_ge(sems["ch_v"].h, m2_init)
            for c in range(nch):
                p = c % 2
                tensor.wait_ge(wl[p].h, wlb[p] + per_c * (c // 2 + 1))
                for t in range(NT):
                    q = t % 2
                    u = c * NT + t
                    if u >= 2:
                        tensor.wait_ge(ue_s.h, usb + u - 1)
                        tensor.wait_ge(ue_v.h, uvb + u - 1)
                    last = None
                    for dt in range(KT):
                        stat = k1T[:, dt * T_LOC + t * 128:dt * T_LOC + (t + 1) * 128]
                        mva = wua[:, p * KT * 512 + dt * 512:p * KT * 512 + (dt + 1) * 512]
                        last = tensor.matmul(psA[:, q * 512:(q + 1) * 512], stat,
                                             mva, start=(dt == 0), stop=(dt == KT - 1))
                    if glu:
                        for dt in range(KT):
                            stat = k1T[:, dt * T_LOC + t * 128:dt * T_LOC + (t + 1) * 128]
                            mvb = wub[:, p * KT * 512 + dt * 512:p * KT * 512 + (dt + 1) * 512]
                            last = tensor.matmul(psB[:, q * 512:(q + 1) * 512], stat,
                                                 mvb, start=(dt == 0), stop=(dt == KT - 1))
                    mm.inc(last, 1)

        @block.scalar
        def _(scalar):
            for c in range(nch):
                for t in range(NT):
                    q = t % 2
                    u = c * NT + t
                    scalar.wait_ge(mm.h, mmb + u + 1)
                    if u >= 2:
                        scalar.wait_ge(ue_v.h, uvb + u - 1)
                    fn = AF.Silu if silu else AF.Gelu
                    scalar.activation(out=ga[:, q * 512:(q + 1) * 512],
                                      in_=psA[:, q * 512:(q + 1) * 512],
                                      func=fn,
                                      scale=scU[:, t:t + 1]).then_inc(ue_s.h, 1)
        ue_s.n += nch * NT

        @block.vector
        def _(vector):
            vector.wait_ge(ch_v.h, m2_init)
            for c in range(nch):
                for t in range(NT):
                    q = t % 2
                    u = c * NT + t
                    vector.wait_ge(ue_s.h, usb + u + 1)
                    if u >= 2:
                        vector.wait_ge(gst.h, gsb + 16 * (u - 1))
                    gbq = gb[:, q * 512:(q + 1) * 512]
                    if glu:
                        vector.tensor_scalar(out=gbq,
                                             in0=psB[:, q * 512:(q + 1) * 512],
                                             scalar1=scU[:, t:t + 1], scalar2=None,
                                             op0=ALU.mult).then_inc(ch_v.h, 1)
                        ch_v.n += 1
                        vector.wait_ge(ch_v.h, ch_v.n)
                        vector.tensor_tensor(out=gbq,
                                             in0=ga[:, q * 512:(q + 1) * 512],
                                             in1=gbq,
                                             op=ALU.mult).then_inc(ch_v.h, 1)
                        ch_v.n += 1
                    else:
                        vector.tensor_copy(gbq, ga[:, q * 512:(q + 1) * 512]
                                           ).then_inc(ch_v.h, 1)
                        ch_v.n += 1
                    vector.wait_ge(ch_v.h, ch_v.n)
                    vector.tensor_reduce(out=rmax[:, q:q + 1], in_=gbq,
                                         axis=AX.X, op=ALU.max,
                                         apply_absolute_value=True).then_inc(ch_v.h, 1)
                    ch_v.n += 1
                    vector.wait_ge(ch_v.h, ch_v.n)
                    vector.tensor_tensor(out=m2[:, t:t + 1], in0=m2[:, t:t + 1],
                                         in1=rmax[:, q:q + 1],
                                         op=ALU.max).then_inc(ue_v.h, 1)
        ue_v.n += nch * NT

        @block.gpsimd
        def _(gpsimd):
            for c in range(nch):
                for t in range(NT):
                    q = t % 2
                    u = c * NT + t
                    gpsimd.wait_ge(ue_v.h, uvb + u + 1)
                    gpsimd.dma_start(out=g_dram[t * 128:(t + 1) * 128,
                                                c * 512:(c + 1) * 512],
                                     in_=gb[:, q * 512:(q + 1) * 512]
                                     ).then_inc(gst.h, 16)
        gst.n += 16 * nch * NT


def emit_quant_g(nc, block, sems, g_dram, k2nat, m2, s2c, sc2, consts, grl,
                 y2, ki2, d2, t12, nk2, dn_wi, Fh, extra_half):
    ch_v, ch_s, kn = sems["ch_v"], sems["ch_s"], sems["kn"]
    gr = [sems["gr0"], sems["gr1"]]
    gst = sems["gst"]

    @block.vector
    def _(vector):
        vector.wait_ge(sems["ue_v"].h, sems["ue_v"].n)
        ch_v.inc(vector.tensor_scalar(out=m2[:, :], in0=m2[:, :], scalar1=EPS_Q,
                                      scalar2=None, op0=ALU.max), 1)
        vector.wait_ge(ch_v.h, ch_v.n)
        ch_v.inc(vector.reciprocal(s2c[:, :], m2[:, :]), 1)
        vector.wait_ge(ch_v.h, ch_v.n)
        ch_v.inc(vector.tensor_scalar(out=s2c[:, :], in0=s2c[:, :], scalar1=127.0,
                                      scalar2=None, op0=ALU.mult), 1)
        hf = 0.5 if extra_half else 1.0
        ch_v.inc(vector.tensor_scalar(out=sc2[:, :], in0=m2[:, :], scalar1=hf,
                                      scalar2=None, op0=ALU.mult), 1)
        vector.wait_ge(ch_v.h, ch_v.n)
        for t in range(NT):
            ch_v.inc(vector.tensor_tensor(out=sc2[:, t:t + 1], in0=sc2[:, t:t + 1],
                                          in1=consts[:, 4 * dn_wi + 2:4 * dn_wi + 3],
                                          op=ALU.mult), 1)

    grb = [gr[0].n, gr[1].n]
    knb = kn.n
    nsec = Fh // 1024
    units = [(t, sec) for t in range(NT) for sec in range(nsec)]

    @block.sync
    def _(sync):
        sync.wait_ge(gst.h, gst.n)
        for i, (t, sec) in enumerate(units):
            p = i % 2
            if i >= 2:
                sync.wait_ge(kn.h, knb + 16 * (i - 1))
            sync.dma_start(out=grl[:, p * 1024:(p + 1) * 1024],
                           in_=g_dram[t * 128:(t + 1) * 128,
                                      sec * 1024:(sec + 1) * 1024]
                           ).then_inc(gr[p].h, 16)
    gr[0].n += 16 * ((len(units) + 1) // 2)
    gr[1].n += 16 * (len(units) // 2)

    for i, (t, sec) in enumerate(units):
        p = i % 2

        @block.scalar
        def _(scalar, i=i, t=t, p=p):
            scalar.wait_ge(gr[p].h, grb[p] + 16 * (i // 2 + 1))
            scalar.wait_ge(ch_v.h, ch_v.n)
            scalar.activation(out=y2[:, :], in_=grl[:, p * 1024:(p + 1) * 1024],
                              func=AF.Copy, scale=s2c[:, t:t + 1]).then_inc(ch_s.h, 1)
        ch_s.n += 1

        @block.vector
        def _(vector, i=i):
            vector.wait_ge(ch_s.h, ch_s.n)
            if i >= 1:
                vector.wait_ge(kn.h, knb + 16 * i)
            ch_v.inc(vector.tensor_copy(ki2[:, :], y2[:, :]), 1)
            vector.wait_ge(ch_v.h, ch_v.n)
            ch_v.inc(vector.tensor_tensor(out=d2[:, :], in0=y2[:, :], in1=ki2[:, :],
                                          op=ALU.subtract), 1)
            vector.wait_ge(ch_v.h, ch_v.n)
            ch_v.inc(vector.scalar_tensor_tensor(out=t12[:, :], in0=d2[:, :],
                                                 scalar=0.5, in1=ki2[:, :],
                                                 op0=ALU.is_ge, op1=ALU.add), 1)
            vector.wait_ge(ch_v.h, ch_v.n)
            ch_v.inc(vector.scalar_tensor_tensor(out=nk2[:, :], in0=d2[:, :],
                                                 scalar=-0.5, in1=t12[:, :],
                                                 op0=ALU.is_le, op1=ALU.subtract), 1)

        @block.gpsimd
        def _(gpsimd, i=i, t=t, sec=sec):
            gpsimd.wait_ge(ch_v.h, ch_v.n)
            kn.inc(gpsimd.dma_start(out=k2nat[t * 128:(t + 1) * 128,
                                              sec * 1024:(sec + 1) * 1024],
                                    in_=nk2[:, :]), 16)


def emit_ffn_down(nc, block, sems, k2nat, wt_dn, xs, sc2, k2T, wdn, Fh,
                  Eo=D):
    k2t = [sems["k2t0"], sems["k2t1"]]
    wd = [sems["wd0"], sems["wd1"]]
    mm, dn_ev, kn = sems["mm"], sems["dn_ev"], sems["kn"]
    ch_v = sems["ch_v"]
    nsec = Fh // 1024
    nec = Eo // 512
    k2b = [k2t[0].n, k2t[1].n]
    wdb = [wd[0].n, wd[1].n]
    mmb = mm.n
    evb = dn_ev.n
    NU = nsec * 4

    with nc.psum_tensor(_u("ps_dn"), [128, 8 * 512], F32) as psD:
        @block.sync
        def _(sync):
            sync.wait_ge(kn.h, kn.n)
            for sec in range(nsec):
                p = sec % 2
                if sec >= 2:
                    sync.wait_ge(dn_ev.h, evb + 8 * 4 * (sec - 1))
                for ft in range(8):
                    sync.dma_start_transpose(
                        out=k2T[:, p * Fh + ft * T_LOC:p * Fh + (ft + 1) * T_LOC],
                        in_=k2nat[:, (sec * 8 + ft) * 128:(sec * 8 + ft + 1) * 128]
                    ).then_inc(k2t[p].h, 16)
        k2t[0].n += 16 * 8 * ((nsec + 1) // 2)
        k2t[1].n += 16 * 8 * (nsec // 2)

        @block.gpsimd
        def _(gpsimd):
            gpsimd.wait_ge(kn.h, kn.n)
            for sec in range(nsec):
                if sec >= 1:
                    gpsimd.wait_ge(mm.h, mmb + 32 * sec)
                rows = wt_dn[sec * 1024:(sec + 1) * 1024, :]
                gpsimd.dma_start(out=wdn[:, :].rearrange("q (ft e) -> q ft e", e=Eo),
                                 in_=rows.rearrange("(ft q) e -> q ft e", q=128)
                                 ).then_inc(wd[0].h, 16)
        wd[0].n += 16 * nsec

        @block.tensor
        def _(tensor):
            gu = 0
            for sec in range(nsec):
                p = sec % 2
                tensor.wait_ge(k2t[p].h, k2b[p] + 16 * 8 * (sec // 2 + 1))
                tensor.wait_ge(wd[0].h, wdb[0] + 16 * (sec + 1))
                for tg in range(4):
                    un = sec * 4 + tg
                    if un >= 2:
                        tensor.wait_ge(dn_ev.h, evb + 8 * (un - 1))
                    for t2 in range(2):
                        t = tg * 2 + t2
                        for ec in range(nec):
                            bank = (tg % 2) * 4 + t2 * nec // 1 * 0 + t2 * 2 + ec                                 if nec == 2 else (tg % 2) * 4 + ec
                            if nec == 4:
                                bank = (tg % 2) * 4 + ec
                            last = None
                            for ft in range(8):
                                stat = k2T[:, p * Fh + ft * T_LOC + t * 128:
                                           p * Fh + ft * T_LOC + (t + 1) * 128]
                                mv = wdn[:, ft * Eo + ec * 512:
                                         ft * Eo + (ec + 1) * 512]
                                last = tensor.matmul(
                                    psD[:, bank * 512:(bank + 1) * 512], stat, mv,
                                    start=(ft == 0), stop=(ft == 7))
                            mm.inc(last, 1)
                        if nec == 4 and t2 == 0:
                            tensor.wait_ge(dn_ev.h, evb + un * 8 + 4)

        @block.vector
        def _(vector):
            g = 0
            for sec in range(nsec):
                for tg in range(4):
                    for t2 in range(2):
                        t = tg * 2 + t2
                        for ec in range(nec):
                            bank = (tg % 2) * 4 + (t2 * 2 + ec if nec == 2 else ec)
                            g += 1
                            vector.wait_ge(mm.h, mmb + g)
                            xv = xs[:, t * Eo + ec * 512:t * Eo + (ec + 1) * 512]
                            vector.scalar_tensor_tensor(
                                out=xv, in0=psD[:, bank * 512:(bank + 1) * 512],
                                scalar=sc2[:, t:t + 1], in1=xv,
                                op0=ALU.mult, op1=ALU.add).then_inc(dn_ev.h, 1)
                            dn_ev.n += 1


def emit_qkv_l0(nc, block, sems, xn2T, wt_q, wt_k, wt_v, qt_out, kt_out,
                v_out, consts, wqkv, pev):
    wl = [sems["wl0"], sems["wl1"]]
    mm, qe = sems["mm"], [sems["qe0"], sems["qe1"]]
    ch_s = sems["ch_s"]
    tt1 = sems["tt1"]
    wlb = [wl[0].n, wl[1].n]
    mmb = mm.n
    qeb = [qe[0].n, qe[1].n]
    KT = 16

    with nc.psum_tensor(_u("ps_qkv"), [128, 4 * 512], F32) as psQ:
        jobs = []
        jobs.append((wt_q, qt_out, None))
        jobs.append((wt_k, kt_out, CC["invsq"]))

        @block.sync
        def _(sync):
            sync.wait_ge(tt1.h, tt1.n)
            for ji, (wt, outd, sc) in enumerate(jobs):
                for et in range(16):
                    i = ji * 16 + et
                    p = i % 2
                    if i >= 2:
                        sync.wait_ge(mm.h, mmb + (i - 1) * 2)
                    ap = wt[:, et * 128:(et + 1) * 128].rearrange(
                        "(dt q) e -> q dt e", q=128)
                    sync.dma_start(out=wqkv[:, p * KT * 512:p * KT * 512 + KT * 128]
                                   .rearrange("q (dt e) -> q dt e", e=128),
                                   in_=ap).then_inc(wl[p].h, 16)
        wl[0].n += 16 * 16
        wl[1].n += 16 * 16
        ch_s.qkv_base = ch_s.n

        @block.tensor
        def _(tensor):
            for ji, (wt, outd, sc) in enumerate(jobs):
                for et in range(16):
                    i = ji * 16 + et
                    p = i % 2
                    tensor.wait_ge(wl[p].h, wlb[p] + 16 * (i // 2 + 1))
                    for tc in range(2):
                        bank = (i % 2) * 2 + tc
                        if i >= 2:
                            tensor.wait_ge(ch_s.h, sems["ch_s"].qkv_base + (i - 2) * 2 + tc + 1)
                        last = None
                        for dt in range(KT):
                            stat = wqkv[:, p * KT * 512 + dt * 128:
                                        p * KT * 512 + (dt + 1) * 128]
                            mv = xn2T[:, dt * T_LOC + tc * 512:dt * T_LOC + (tc + 1) * 512]
                            last = tensor.matmul(psQ[:, bank * 512:(bank + 1) * 512],
                                                 stat, mv, start=(dt == 0),
                                                 stop=(dt == KT - 1))
                        mm.inc(last, 1)

        @block.scalar
        def _(scalar):
            for ji, (wt, outd, sc) in enumerate(jobs):
                for et in range(16):
                    i = ji * 16 + et
                    for tc in range(2):
                        bank = (i % 2) * 2 + tc
                        scalar.wait_ge(mm.h, mmb + i * 2 + tc + 1)
                        ev = pev[:, bank * 512:(bank + 1) * 512]
                        if i >= 2:
                            scalar.wait_ge(qe[i % 2].h, qeb[i % 2] + 16 * ((i // 2 - 1) * 2 + tc + 1))
                        if sc is None:
                            scalar.activation(out=ev, in_=psQ[:, bank * 512:(bank + 1) * 512],
                                              func=AF.Copy,
                                              scale=consts[:, CC["one"]:CC["one"] + 1]
                                              ).then_inc(ch_s.h, 1)
                        else:
                            scalar.activation(out=ev, in_=psQ[:, bank * 512:(bank + 1) * 512],
                                              func=AF.Copy,
                                              scale=consts[:, sc:sc + 1]).then_inc(ch_s.h, 1)
                        ch_s.n += 1

        @block.gpsimd
        def _(gpsimd):
            for ji, (wt, outd, sc) in enumerate(jobs):
                for et in range(16):
                    i = ji * 16 + et
                    for tc in range(2):
                        bank = (i % 2) * 2 + tc
                        gpsimd.wait_ge(ch_s.h, ch_s.qkv_base + i * 2 + tc + 1)
                        gpsimd.dma_start(out=outd[et * 128:(et + 1) * 128,
                                                  tc * 512:(tc + 1) * 512],
                                         in_=pev[:, bank * 512:(bank + 1) * 512]
                                         ).then_inc(qe[i % 2].h, 16)
        qe[0].n += 16 * 32
        qe[1].n += 16 * 32

    with nc.psum_tensor(_u("ps_v"), [128, 4 * 512], F32) as psV:
        vbase_mm = mm.n
        vbase_chs = ch_s.n
        vwlb = [wl[0].n, wl[1].n]
        qe0_final, qe1_final = qe[0].n, qe[1].n
        qe[0].n_v_base = qe[0].n

        @block.sync
        def _(sync):
            sync.wait_ge(mm.h, vbase_mm)
            for ec in range(4):
                p = ec % 2
                if ec >= 2:
                    sync.wait_ge(mm.h, vbase_mm + (ec - 1) * NT)
                ap = wt_v[:, ec * 512:(ec + 1) * 512].rearrange(
                    "(dt q) e -> q dt e", q=128)
                sync.dma_start(out=wqkv[:, p * KT * 512:(p + 1) * KT * 512]
                               .rearrange("q (dt e) -> q dt e", e=512),
                               in_=ap).then_inc(wl[p].h, 16)
        wl[0].n += 16 * 2
        wl[1].n += 16 * 2

        @block.tensor
        def _(tensor):
            for ec in range(4):
                p = ec % 2
                tensor.wait_ge(wl[p].h, vwlb[p] + 16 * (ec // 2 + 1))
                for t in range(NT):
                    u = ec * NT + t
                    bank = u % 4
                    if u >= 4:
                        tensor.wait_ge(ch_s.h, vbase_chs + u - 3)
                    last = None
                    for dt in range(KT):
                        stat = xn2T[:, dt * T_LOC + t * 128:dt * T_LOC + (t + 1) * 128]
                        mv = wqkv[:, p * KT * 512 + dt * 512:p * KT * 512 + (dt + 1) * 512]
                        last = tensor.matmul(psV[:, bank * 512:(bank + 1) * 512],
                                             stat, mv, start=(dt == 0),
                                             stop=(dt == KT - 1))
                    mm.inc(last, 1)

        @block.scalar
        def _(scalar):
            for ec in range(4):
                for t in range(NT):
                    u = ec * NT + t
                    bank = u % 4
                    scalar.wait_ge(mm.h, vbase_mm + u + 1)
                    ev = pev[:, bank * 512:(bank + 1) * 512]
                    if u < 4:
                        scalar.wait_ge(qe[0].h, qe0_final)
                        scalar.wait_ge(qe[1].h, qe1_final)
                    else:
                        scalar.wait_ge(qe[0].h, qe[0].n_v_base + 16 * (u - 3))
                    scalar.activation(out=ev, in_=psV[:, bank * 512:(bank + 1) * 512],
                                      func=AF.Copy,
                                      scale=consts[:, CC["one"]:CC["one"] + 1]
                                      ).then_inc(ch_s.h, 1)
                    ch_s.n += 1

        @block.gpsimd
        def _(gpsimd):
            for ec in range(4):
                for t in range(NT):
                    u = ec * NT + t
                    bank = u % 4
                    gpsimd.wait_ge(ch_s.h, vbase_chs + u + 1)
                    gpsimd.dma_start(out=v_out[t * 128:(t + 1) * 128,
                                               ec * 512:(ec + 1) * 512],
                                     in_=pev[:, bank * 512:(bank + 1) * 512]
                                     ).then_inc(qe[0].h, 16)
        qe[0].n += 16 * 32



class Rec:
    NROT = 4

    def __init__(self, nc, block, sems, tag):
        self.nc = nc
        self.block = block
        self.ops = {e: [] for e in ("tensor", "scalar", "vector", "pool",
                                    "syncd", "gpsd")}
        self.tag = tag
        self.chain = {
            "tensor": sems["mm"], "scalar": sems["ch_s"],
            "vector": sems["ch_v"], "pool": sems["ch_p"],
        }
        self.dma = {
            "syncd": [sems["sd0"], sems["sd1"], sems["sd2"], sems["sd3"]],
            "gpsd": [sems["gd0"], sems["gd1"], sems["gd2"], sems["gd3"]],
        }

    def op(self, eng, fn, waits=()):
        lst = self.ops[eng]
        tok = (eng, len(lst))
        lst.append((fn, tuple(w for w in waits if w is not None)))
        return tok

    def _tok_sem_val(self, tok, bases):
        if tok[0] == "raw":
            return (tok[1], tok[2])
        eng, idx = tok
        if eng in self.chain:
            return (self.chain[eng].h, bases[eng] + idx + 1)
        sems = self.dma[eng]
        r = idx % self.NROT
        return (sems[r].h, bases[eng][r] + 16 * (idx // self.NROT + 1))

    def emit(self):
        bases = {}
        for e, s_ in self.chain.items():
            bases[e] = s_.n
        for e, lst in self.dma.items():
            bases[e] = [x.n for x in lst]

        engmap = {"tensor": self.block.tensor, "scalar": self.block.scalar,
                  "vector": self.block.vector, "pool": self.block.gpsimd,
                  "syncd": self.block.sync, "gpsd": self.block.gpsimd}

        for eng in ("syncd", "gpsd", "tensor", "scalar", "vector", "pool"):
            lst = self.ops[eng]
            if not lst:
                continue

            def body(E, eng=eng, lst=lst):
                hw = {}
                for i, (fn, waits) in enumerate(lst):
                    for tok in waits:
                        h, v = self._tok_sem_val(tok, bases)
                        if hw.get(id(h), -1) < v:
                            E.wait_ge(h, v)
                            hw[id(h)] = v
                    ins = fn(E)
                    if eng in self.chain:
                        ins.then_inc(self.chain[eng].h, 1)
                    else:
                        r = i % self.NROT
                        ins.then_inc(self.dma[eng][r].h, 16)
            engmap[eng](body)

        toks = {}
        for e, s_ in self.chain.items():
            n = len(self.ops[e])
            s_.n = bases[e] + n
        for e, lst2 in self.dma.items():
            n = len(self.ops[e])
            for r in range(self.NROT):
                lst2[r].n = bases[e][r] + 16 * ((n - r + self.NROT - 1) // self.NROT)

    def raw_of(self, tok):
        eng, idx = tok
        assert eng in self.chain
        return ("raw", self.chain[eng].h, self._final_base[eng] + idx + 1)



def emit_flash(nc, block, sems, qt_sb, ktf, vf, mask_sb, onat, n_heads,
               kv_div, gate, bufs, entry_waits):
    VW = 132
    n_kvh = n_heads // kv_div
    kbuf, vbuf, qsq, ksq = bufs["kbuf"], bufs["vbuf"], bufs["qsq"], bufs["ksq"]
    kn2, qn2, ptb, gsb = bufs["kn2"], bufs["qn2"], bufs["ptb"], bufs["gsb"]
    obuf, qm2, km2, nbrow = bufs["obuf"], bufs["qm2"], bufs["km2"], bufs["nbrow"]
    negB, rcol = bufs["negB"], bufs["rcol"]
    ones_row = bufs["ones_row"]
    ew = tuple(entry_waits)

    rec = Rec(nc, block, sems, "flash")
    last_exp = {}
    last_pv = {}
    gate_free = {}
    ost = {}
    epi = {}

    with (
        nc.psum_tensor(_u("ps_st"), [128, 2 * 512], F32) as psS,
        nc.psum_tensor(_u("ps_o"), [128, 2 * 512], F32) as psO,
        nc.psum_tensor(_u("ps_g"), [128, 2 * 512], F32) as psG,
        nc.psum_tensor(_u("ps_nb"), [128, 512], F32) as psNB,
    ):
        vinit = []
        for p in range(2):
            vv = vbuf[:, p * 32 * VW:(p + 1) * 32 * VW].rearrange(
                "q (kt w) -> q kt w", w=VW)
            vinit.append(rec.op("vector", lambda E, vv=vv: E.memset(
                vv[:, :, 128:129], 1.0), waits=ew))
            vinit.append(rec.op("vector", lambda E, vv=vv: E.memset(
                vv[:, :, 129:132], 0.0)))

        qtok = {}
        prev_pool = None
        for h in range(n_heads):
            sq = rec.op("scalar", lambda E, h=h: E.activation(
                out=qsq[:, :], in_=qt_sb[:, h * T_LOC:(h + 1) * T_LOC],
                func=AF.Square), waits=ew + (prev_pool,))
            pl = rec.op("pool", lambda E: E.tensor_reduce(
                out=qn2[:, :], in_=qsq[:, :], axis=AX.C, op=ALU.add),
                waits=(sq, qtok.get(h - 1)))
            mx = rec.op("vector", lambda E, h=h: E.tensor_reduce(
                out=qm2[:, h:h + 1], in_=qn2[:, :], axis=AX.X, op=ALU.max),
                waits=(pl,))
            qtok[h] = mx
            prev_pool = pl

        kload, vload = {}, {}

        def rec_loads(kvh):
            p = kvh % 2
            guard = None
            if kvh >= 2:
                guard = epi.get(((kvh - 2) + 1) * kv_div * NT - 1)
                assert guard is not None, kvh
            kload[kvh] = rec.op("syncd", lambda E, kvh=kvh, p=p: E.dma_start(
                out=kbuf[:, p * SEQ:(p + 1) * SEQ],
                in_=ktf[kvh * 128:(kvh + 1) * 128, :]), waits=(guard,) + ew)
            vv = vbuf[:, p * 32 * VW:(p + 1) * 32 * VW].rearrange(
                "q (kt w) -> q kt w", w=VW)
            vload[kvh] = rec.op("syncd", lambda E, kvh=kvh, vv=vv: E.dma_start(
                out=vv[:, :, 0:128],
                in_=vf[:, kvh * 128:(kvh + 1) * 128].rearrange(
                    "(kt q) d -> q kt d", q=128)),
                waits=(guard, vinit[2 * p], vinit[2 * p + 1]))

        rec_loads(0)

        nbready = {}
        for kvh in range(n_kvh):
            p = kvh % 2
            if kvh + 1 < n_kvh:
                rec_loads(kvh + 1)

            ksqt = rec.op("scalar", lambda E, p=p: E.activation(
                out=ksq[:, :], in_=kbuf[:, p * SEQ:(p + 1) * SEQ],
                func=AF.Square), waits=(kload[kvh], nbready.get(kvh - 1)))
            kpl = rec.op("pool", lambda E: E.tensor_reduce(
                out=kn2[:, :], in_=ksq[:, :], axis=AX.C, op=ALU.add),
                waits=(ksqt, nbready.get(kvh - 1)))
            cols = slice(kvh * kv_div, (kvh + 1) * kv_div)
            kmx = None
            for t in range(kv_div):
                c0 = kvh * kv_div + t
                kmx = rec.op("vector", lambda E, c0=c0: E.tensor_reduce(
                    out=km2[:, c0:c0 + 1], in_=kn2[:, :], axis=AX.X,
                    op=ALU.max), waits=(kpl,))
            tmul = rec.op("vector", lambda E, cols=cols: E.tensor_tensor(
                out=nbrow[:, cols], in0=qm2[:, cols], in1=km2[:, cols],
                op=ALU.mult), waits=(kmx,) + tuple(
                    qtok[h] for h in range(kvh * kv_div, (kvh + 1) * kv_div)))
            tsq = rec.op("scalar", lambda E, cols=cols: E.activation(
                out=nbrow[:, cols], in_=nbrow[:, cols], func=AF.Sqrt),
                waits=(tmul,))
            tneg = rec.op("vector", lambda E, cols=cols: E.tensor_scalar(
                out=nbrow[:, cols], in0=nbrow[:, cols], scalar1=-1.0,
                scalar2=None, op0=ALU.mult), waits=(tsq,))
            bc = rec.op("tensor", lambda E, cols=cols, kvh=kvh: E.matmul(
                psNB[:, kvh * kv_div:(kvh + 1) * kv_div], ones_row[:, 0:128],
                nbrow[:, cols], start=True, stop=True),
                waits=(tneg, nbready.get(kvh - 1)))
            nbready[kvh] = rec.op("vector", lambda E, cols=cols, kvh=kvh:
                E.tensor_copy(negB[:, cols],
                              psNB[:, kvh * kv_div:(kvh + 1) * kv_div]),
                waits=(bc,))

            for hq in range(kvh * kv_div, (kvh + 1) * kv_div):
                for j in range(NT):
                    sidx = hq * NT + j
                    po = sidx % 2
                    st_tok, exp_tok, pv_tok = {}, {}, {}

                    def rec_pv(g):
                        ps_par = g % 2
                        pvs = None
                        for kt in range(4):
                            ktile = 4 * g + kt
                            stat = ptb[:, ps_par * 512 + kt * 128:
                                       ps_par * 512 + (kt + 1) * 128]
                            mv = vbuf[:, p * 32 * VW + ktile * VW:
                                      p * 32 * VW + (ktile + 1) * VW]
                            pvs = rec.op("tensor", lambda E, po=po, stat=stat,
                                         mv=mv, s=(g == 0 and kt == 0),
                                         e2=(g == j and kt == 3): E.matmul(
                                psO[:, po * 512:po * 512 + VW], stat, mv,
                                start=s, stop=e2),
                                waits=(exp_tok[g], vload[kvh],
                                       epi.get(sidx - 2)))
                        pv_tok[g] = pvs
                        last_pv[(sidx, ps_par)] = pvs

                    for g in range(j + 1):
                        ps_par = g % 2
                        dep_exp = exp_tok.get(g - 2) or last_exp.get(
                            (sidx - 1, ps_par)) or last_exp.get((sidx - 2, ps_par))
                        sts = None
                        for kt in range(4):
                            ktile = 4 * g + kt
                            stat = kbuf[:, p * SEQ + ktile * 128:
                                        p * SEQ + (ktile + 1) * 128]
                            mv = qt_sb[:, hq * T_LOC + j * 128:
                                       hq * T_LOC + (j + 1) * 128]
                            sts = rec.op("tensor", lambda E, ps_par=ps_par,
                                         kt=kt, stat=stat, mv=mv: E.matmul(
                                psS[:, ps_par * 512 + kt * 128:
                                    ps_par * 512 + (kt + 1) * 128],
                                stat, mv, start=True, stop=True),
                                waits=(kload[kvh], dep_exp))
                        st_tok[g] = sts
                        mtok = None
                        if g == j:
                            mtok = rec.op("vector", lambda E, ps_par=ps_par,
                                          j=j: E.tensor_tensor(
                                out=psS[:, ps_par * 512:(ps_par + 1) * 512],
                                in0=psS[:, ps_par * 512:(ps_par + 1) * 512],
                                in1=mask_sb[:, j * 512:(j + 1) * 512],
                                op=ALU.add), waits=(sts,))
                        pt_dep = pv_tok.get(g - 2) or last_pv.get(
                            (sidx - 1, ps_par)) or last_pv.get((sidx - 2, ps_par))
                        ex = rec.op("scalar", lambda E, ps_par=ps_par, hq=hq:
                            E.activation(
                                out=ptb[:, ps_par * 512:(ps_par + 1) * 512],
                                in_=psS[:, ps_par * 512:(ps_par + 1) * 512],
                                func=AF.Exp, bias=negB[:, hq:hq + 1]),
                            waits=(sts, mtok, nbready[kvh], pt_dep))
                        exp_tok[g] = ex
                        last_exp[(sidx, ps_par)] = ex
                        if g >= 1:
                            rec_pv(g - 1)
                    rec_pv(j)

                    rc = rec.op("vector", lambda E, po=po: E.reciprocal(
                        rcol[:, po:po + 1],
                        psO[:, po * 512 + 128:po * 512 + 129]),
                        waits=(pv_tok[j], epi.get(sidx - 2)))
                    if gate is not None:
                        gwT, gbB, gate_ready = gate
                        gm = rec.op("tensor", lambda E, po=po, hq=hq, j=j:
                            E.matmul(psG[:, po * 512:po * 512 + 128],
                                     qt_sb[:, hq * T_LOC + j * 128:
                                           hq * T_LOC + (j + 1) * 128],
                                     gwT[:, :], start=True, stop=True),
                            waits=(gate_free.get(sidx - 2), gate_ready))
                        gt = rec.op("vector", lambda E, po=po: E.tensor_tensor(
                            out=psG[:, po * 512:po * 512 + 128],
                            in0=psG[:, po * 512:po * 512 + 128],
                            in1=gbB[:, :], op=ALU.add), waits=(gm,))
                        sg = rec.op("scalar", lambda E, po=po: E.activation(
                            out=gsb[:, po * 128:(po + 1) * 128],
                            in_=psG[:, po * 512:po * 512 + 128],
                            func=AF.Sigmoid), waits=(gt,))
                        gate_free[sidx] = sg
                        ep = rec.op("vector", lambda E, po=po:
                            E.scalar_tensor_tensor(
                                out=obuf[:, po * 128:(po + 1) * 128],
                                in0=psO[:, po * 512:po * 512 + 128],
                                scalar=rcol[:, po:po + 1],
                                in1=gsb[:, po * 128:(po + 1) * 128],
                                op0=ALU.mult, op1=ALU.mult),
                            waits=(rc, sg, ost.get(sidx - 2)))
                    else:
                        ep = rec.op("vector", lambda E, po=po: E.tensor_scalar(
                            out=obuf[:, po * 128:(po + 1) * 128],
                            in0=psO[:, po * 512:po * 512 + 128],
                            scalar1=rcol[:, po:po + 1], scalar2=None,
                            op0=ALU.mult),
                            waits=(rc, ost.get(sidx - 2)))
                    epi[sidx] = ep
                    ost[sidx] = rec.op("syncd", lambda E, po=po, hq=hq, j=j:
                        E.dma_start(
                            out=onat[j * 128:(j + 1) * 128,
                                     hq * 128:(hq + 1) * 128],
                            in_=obuf[:, po * 128:(po + 1) * 128]),
                        waits=(ep,))

        rec.op("vector", lambda E: E.tensor_copy(rcol[:, 0:1], rcol[:, 0:1]),
               waits=tuple(ost[s2] for s2 in range(n_heads * NT - 2,
                                                   n_heads * NT)))
        rec.emit()


def emit_oproj_residual(nc, block, sems, onat, wt_o, xs, oT_sb, wch, entry):
    rec = Rec(nc, block, sems, "oproj")
    with nc.psum_tensor(_u("ps_op"), [128, 4 * 512], F32) as psP:
        tts = []
        for dt in range(16):
            tts.append(rec.op("syncd", lambda E, dt=dt: E.dma_start_transpose(
                out=oT_sb[:, dt * T_LOC:(dt + 1) * T_LOC],
                in_=onat[:, dt * 128:(dt + 1) * 128]), waits=tuple(entry)))
        wl = {}
        evac = {}
        for ec in range(4):
            pw = ec % 2
            wl[ec] = rec.op("syncd", lambda E, ec=ec, pw=pw: E.dma_start(
                out=wch[:, pw * 16 * 512:(pw + 1) * 16 * 512].rearrange(
                    "q (dt e) -> q dt e", e=512),
                in_=wt_o[:, ec * 512:(ec + 1) * 512].rearrange(
                    "(dt q) e -> q dt e", q=128)),
                waits=(evac.get((ec - 2, NT - 1)),))
            for t in range(NT):
                bank = (ec % 2) * 2 + (t % 2)
                mmt = None
                for dt in range(16):
                    mmt = rec.op("tensor", lambda E, bank=bank, dt=dt, t=t,
                                 pw=pw, s=(dt == 0), e2=(dt == 15): E.matmul(
                        psP[:, bank * 512:(bank + 1) * 512],
                        oT_sb[:, dt * T_LOC + t * 128:dt * T_LOC + (t + 1) * 128],
                        wch[:, pw * 16 * 512 + dt * 512:
                            pw * 16 * 512 + (dt + 1) * 512],
                        start=s, stop=e2),
                        waits=tuple(tts) + (wl[ec], evac.get((ec - 1, t - 1)),
                                            evac.get((ec - 2, t)),
                                            evac.get((ec, t - 2))))
                evac[(ec, t)] = rec.op("vector", lambda E, bank=bank, t=t,
                                       ec=ec: E.tensor_tensor(
                    out=xs[:, t * D + ec * 512:t * D + (ec + 1) * 512],
                    in0=psP[:, bank * 512:(bank + 1) * 512],
                    in1=xs[:, t * D + ec * 512:t * D + (ec + 1) * 512],
                    op=ALU.add), waits=(mmt,))
        rec.emit()


def emit_l1_proj(nc, block, sems, xn2T, wq_n, wk_n, wv_n, qfull, kfull,
                 v1_out, v16, wch, entry):
    rec = Rec(nc, block, sems, "l1proj")
    jobs = [("q", wq_n, c) for c in range(4)] + [("k", wk_n, 0), ("v", wv_n, 0)]
    with nc.psum_tensor(_u("ps_l1p"), [128, 4 * 512], F32) as psL:
        evac = {}
        wload = {}
        for ji, (nm, w, c) in enumerate(jobs):
            pw = ji % 2
            wload[ji] = rec.op("syncd", lambda E, w=w, c=c, pw=pw:
                E.dma_start(
                    out=wch[:, pw * 16 * 512:(pw + 1) * 16 * 512].rearrange(
                        "q (dt e) -> q dt e", e=512),
                    in_=w[:, c * 512:(c + 1) * 512].rearrange(
                        "(dt q) e -> q dt e", q=128)),
                waits=tuple(entry) + (evac.get((ji - 2, NT - 1)),))
            for t in range(NT):
                bank = (ji % 2) * 2 + (t % 2)
                mmt = None
                for dt in range(16):
                    mmt = rec.op("tensor", lambda E, bank=bank, dt=dt, t=t,
                                 pw=pw, s=(dt == 0), e2=(dt == 15): E.matmul(
                        psL[:, bank * 512:(bank + 1) * 512],
                        xn2T[:, dt * T_LOC + t * 128:dt * T_LOC + (t + 1) * 128],
                        wch[:, pw * 16 * 512 + dt * 512:
                            pw * 16 * 512 + (dt + 1) * 512],
                        start=s, stop=e2),
                        waits=(wload[ji], evac.get((ji - 1, t - 1)),
                               evac.get((ji - 2, t)),
                               evac.get((ji, t - 2))) + tuple(entry))
                if nm == "q":
                    dst = qfull[:, t * D + c * 512:t * D + (c + 1) * 512]
                elif nm == "k":
                    dst = kfull[:, t * 512:(t + 1) * 512]
                else:
                    dst = v16[:, :]
                ev = rec.op("scalar", lambda E, bank=bank, dst=dst: E.activation(
                    out=dst, in_=psL[:, bank * 512:(bank + 1) * 512],
                    func=AF.Copy),
                    waits=(mmt,) + ((evac.get((ji, t - 1)),) if nm == "v" else ()))
                if nm == "v":
                    ev = rec.op("syncd", lambda E, t=t: E.dma_start(
                        out=v1_out[t * 128:(t + 1) * 128, :], in_=v16[:, :]),
                        waits=(ev,))
                evac[(ji, t)] = ev
        rec.emit()


def emit_l1_normrope(nc, block, sems, qfull, kfull, gqB, gkB, tabs, qnat,
                     knat, qt1_out, kt1_out, bufs, entry):
    rec = Rec(nc, block, sems, "l1nr")
    qb, rot, tq, tk = bufs["qb"], bufs["rot"], bufs["tq"], bufs["tk"]
    qss, kss = bufs["qss"], bufs["kss"]
    qb16, kb16, kb = bufs["qb16"], bufs["kb16"], bufs["kb"]
    TT = bufs["TT"]

    tabtok = {}
    tfree = {}
    qdone, kdone = {}, {}
    for t in range(NT):
        toks = []
        for i, (nm, buf, wdt) in enumerate((("qc", tq, D), ("qs", tq, D),
                                            ("kc", tk, 512), ("ks", tk, 512))):
            off = (i % 2) * (D if nm[0] == "q" else 512)
            toks.append(rec.op("syncd", lambda E, nm=nm, buf=buf, off=off,
                               wdt=wdt, t=t: E.dma_start(
                out=buf[:, off:off + wdt],
                in_=tabs[nm][t * 128:(t + 1) * 128, :]),
                waits=tuple(entry) + (tfree.get(t - 1),)))
        tabtok[t] = toks

        cp = rec.op("vector", lambda E, t=t: E.tensor_copy(
            qb[:, :], qfull[:, t * D:(t + 1) * D]),
            waits=tuple(entry) + (qdone.get(t - 1),))
        sq = rec.op("scalar", lambda E: E.activation(
            out=rot[:, :], in_=qb[:, :], func=AF.Square),
            waits=(cp, qdone.get(t - 1)))
        rd = rec.op("vector", lambda E: E.tensor_reduce(
            out=qss[:, :], in_=rot[:, :].rearrange("q (h d) -> q h d", d=128),
            axis=AX.X, op=ALU.add), waits=(sq,))
        n1 = rec.op("vector", lambda E: E.tensor_scalar(
            out=qss[:, :], in0=qss[:, :], scalar1=1.0 / 128.0,
            scalar2=EPS_RMS, op0=ALU.mult, op1=ALU.add), waits=(rd,))
        n2 = rec.op("scalar", lambda E: E.activation(
            out=qss[:, :], in_=qss[:, :], func=AF.Sqrt), waits=(n1,))
        n3 = rec.op("vector", lambda E: E.reciprocal(qss[:, :], qss[:, :]),
                    waits=(n2,))
        hops = []
        for h in range(16):
            hops.append(rec.op("vector", lambda E, h=h:
                E.scalar_tensor_tensor(
                    out=qb[:, h * 128:(h + 1) * 128],
                    in0=qb[:, h * 128:(h + 1) * 128],
                    scalar=qss[:, h:h + 1],
                    in1=gqB[:, h * 128:(h + 1) * 128],
                    op0=ALU.mult, op1=ALU.mult), waits=(n3, sq)))
        rops = []
        for h in range(16):
            rops.append(rec.op("vector", lambda E, h=h: E.tensor_scalar(
                out=rot[:, h * 128:h * 128 + 64],
                in0=qb[:, h * 128 + 64:h * 128 + 128],
                scalar1=-1.0, scalar2=None, op0=ALU.mult),
                waits=(hops[h], rd)))
            rops.append(rec.op("vector", lambda E, h=h: E.tensor_copy(
                rot[:, h * 128 + 64:h * 128 + 128],
                qb[:, h * 128:h * 128 + 64]), waits=(hops[h],)))
        m1 = rec.op("vector", lambda E: E.tensor_tensor(
            out=qb[:, :], in0=qb[:, :], in1=tq[:, 0:D],
            op=ALU.mult), waits=tuple(rops) + tuple(toks))
        m2_ = rec.op("vector", lambda E: E.tensor_tensor(
            out=rot[:, :], in0=rot[:, :], in1=tq[:, D:2 * D],
            op=ALU.mult), waits=(m1,) + tuple(toks))
        ad = rec.op("vector", lambda E: E.tensor_tensor(
            out=qb16[:, :], in0=qb[:, :], in1=rot[:, :], op=ALU.add),
            waits=(m1, m2_, qdone.get(t - 1)))
        qdone[t] = rec.op("syncd", lambda E, t=t: E.dma_start(
            out=qnat[t * 128:(t + 1) * 128, :], in_=qb16[:, :]),
            waits=(ad,))

        kcp = rec.op("vector", lambda E, t=t: E.tensor_copy(
            kb[:, 0:512], kfull[:, t * 512:(t + 1) * 512]),
            waits=tuple(entry) + (kdone.get(t - 1),))
        sqk = rec.op("scalar", lambda E: E.activation(
            out=kb[:, 512:1024], in_=kb[:, 0:512], func=AF.Square),
            waits=(kcp, kdone.get(t - 1)))
        rdk = rec.op("vector", lambda E: E.tensor_reduce(
            out=kss[:, :], in_=kb[:, 512:1024].rearrange(
                "q (h d) -> q h d", d=128),
            axis=AX.X, op=ALU.add), waits=(sqk,))
        nk1 = rec.op("vector", lambda E: E.tensor_scalar(
            out=kss[:, :], in0=kss[:, :], scalar1=1.0 / 128.0,
            scalar2=EPS_RMS, op0=ALU.mult, op1=ALU.add), waits=(rdk,))
        nk2 = rec.op("scalar", lambda E: E.activation(
            out=kss[:, :], in_=kss[:, :], func=AF.Sqrt), waits=(nk1,))
        nk3 = rec.op("vector", lambda E: E.reciprocal(kss[:, :], kss[:, :]),
                     waits=(nk2,))
        khops = []
        for h in range(4):
            khops.append(rec.op("vector", lambda E, h=h:
                E.scalar_tensor_tensor(
                    out=kb[:, h * 128:(h + 1) * 128],
                    in0=kb[:, h * 128:(h + 1) * 128],
                    scalar=kss[:, h:h + 1],
                    in1=gkB[:, h * 128:(h + 1) * 128],
                    op0=ALU.mult, op1=ALU.mult), waits=(nk3, rdk)))
        krops = []
        for h in range(4):
            krops.append(rec.op("vector", lambda E, h=h: E.tensor_scalar(
                out=kb[:, 512 + h * 128:512 + h * 128 + 64],
                in0=kb[:, h * 128 + 64:h * 128 + 128],
                scalar1=-1.0, scalar2=None, op0=ALU.mult),
                waits=(khops[h],)))
            krops.append(rec.op("vector", lambda E, h=h: E.tensor_copy(
                kb[:, 512 + h * 128 + 64:512 + h * 128 + 128],
                kb[:, h * 128:h * 128 + 64]), waits=(khops[h],)))
        km1 = rec.op("vector", lambda E: E.tensor_tensor(
            out=kb[:, 0:512], in0=kb[:, 0:512], in1=tk[:, 0:512],
            op=ALU.mult), waits=tuple(krops) + tuple(toks))
        km2 = rec.op("vector", lambda E: E.tensor_tensor(
            out=kb[:, 512:1024], in0=kb[:, 512:1024], in1=tk[:, 512:1024],
            op=ALU.mult), waits=(km1,) + tuple(toks))
        kad = rec.op("vector", lambda E: E.tensor_tensor(
            out=kb16[:, :], in0=kb[:, 0:512], in1=kb[:, 512:1024],
            op=ALU.add), waits=(km1, km2, kdone.get(t - 1)))
        kdone[t] = rec.op("syncd", lambda E, t=t: E.dma_start(
            out=knat[t * 128:(t + 1) * 128, :], in_=kb16[:, :]),
            waits=(kad,))
        tfree[t] = kad

    allq = tuple(qdone.values()) + tuple(kdone.values())
    prev_st = None
    for dt in range(16):
        tt_ = rec.op("syncd", lambda E, dt=dt: E.dma_start_transpose(
            out=TT[:, 0:T_LOC], in_=qnat[:, dt * 128:(dt + 1) * 128]),
            waits=allq + ((prev_st,) if prev_st else ()))
        prev_st = rec.op("syncd", lambda E, dt=dt: E.dma_start(
            out=qt1_out[dt * 128:(dt + 1) * 128, :], in_=TT[:, 0:T_LOC]),
            waits=(tt_,))
    for dt in range(4):
        tt_ = rec.op("syncd", lambda E, dt=dt: E.dma_start_transpose(
            out=TT[:, 0:T_LOC], in_=knat[:, dt * 128:(dt + 1) * 128]),
            waits=(prev_st,))
        prev_st = rec.op("syncd", lambda E, dt=dt: E.dma_start(
            out=kt1_out[dt * 128:(dt + 1) * 128, :], in_=TT[:, 0:T_LOC]),
            waits=(tt_,))
    rec.op("vector", lambda E: E.tensor_copy(kss[:, 0:1], kss[:, 0:1]),
           waits=(prev_st,))
    rec.emit()



CC = {"neg1": 16, "invsq": 17, "epsln": 18, "epsq": 19, "one": 20}
NCOL = 24

P1_QW = [("L0gu", 0), ("L0dn", 1)]


def build_p1():
    nc = bass.Bass("TRN2", target_bir_lowering=False)
    x_in = nc.dram_tensor("x", [T_LOC, D], F32, kind="ExternalInput").ap()
    partials = nc.dram_tensor("partials", [8, len(QW_NAMES)], F32, kind="ExternalInput").ap()
    w_gu = nc.dram_tensor("w_L0gu", [EL0, D], F32, kind="ExternalInput").ap()
    w_dn = nc.dram_tensor("w_L0dn", [D, F_L0], F32, kind="ExternalInput").ap()
    w_q = nc.dram_tensor("w_q", [D, D], F32, kind="ExternalInput").ap()
    w_k = nc.dram_tensor("w_k", [D, D], F32, kind="ExternalInput").ap()
    w_v = nc.dram_tensor("w_v", [D, D], F32, kind="ExternalInput").ap()
    w_o = nc.dram_tensor("w_o", [D, D], F32, kind="ExternalInput").ap()

    x1_out = nc.dram_tensor("x1", [T_LOC, D], F32, kind="ExternalOutput").ap()
    qt_out = nc.dram_tensor("qt", [D, T_LOC], BF16, kind="ExternalOutput").ap()
    kt_out = nc.dram_tensor("kt", [D, T_LOC], BF16, kind="ExternalOutput").ap()
    v_out = nc.dram_tensor("v", [T_LOC, D], BF16, kind="ExternalOutput").ap()
    wt_gu = nc.dram_tensor("wt_L0gu", [D, EL0], BF16, kind="ExternalOutput").ap()
    wt_dn = nc.dram_tensor("wt_L0dn", [F_L0, D], BF16, kind="ExternalOutput").ap()
    wt_o = nc.dram_tensor("wt_o", [D, D], BF16, kind="ExternalOutput").ap()
    wt_w1_out = nc.dram_tensor("wt_L1w1", [D, F_L1], BF16, kind="ExternalOutput").ap()
    wt_w2_out = nc.dram_tensor("wt_L1w2", [F_L1, D], BF16, kind="ExternalOutput").ap()
    wnq1_out = nc.dram_tensor("wnat_q1", [D, D], BF16, kind="ExternalOutput").ap()
    wnk1_out = nc.dram_tensor("wnat_k1", [D, 512], BF16, kind="ExternalOutput").ap()
    wnv1_out = nc.dram_tensor("wnat_v1", [D, 512], BF16, kind="ExternalOutput").ap()
    wno1_out = nc.dram_tensor("wnat_o1", [D, D], BF16, kind="ExternalOutput").ap()
    dbg_out = nc.dram_tensor("dbg", [128, 4 * NT], F32, kind="ExternalOutput").ap()
    w_l1w1 = nc.dram_tensor("w_L1w1", [F_L1, D], F32, kind="ExternalInput").ap()
    w_l1w2 = nc.dram_tensor("w_L1w2", [D, F_L1], F32, kind="ExternalInput").ap()
    w1q_in = nc.dram_tensor("w1q", [D, D], F32, kind="ExternalInput").ap()
    w1k_in = nc.dram_tensor("w1k", [D, 512], F32, kind="ExternalInput").ap()
    w1v_in = nc.dram_tensor("w1v", [D, 512], F32, kind="ExternalInput").ap()
    w1o_in = nc.dram_tensor("w1o", [D, D], F32, kind="ExternalInput").ap()
    wnat_l1w1 = nc.dram_tensor("wnat_l1w1", [F_L1, D], BF16).ap()
    wnat_l1w2 = nc.dram_tensor("wnat_l1w2", [D, F_L1], BF16).ap()

    wnat_gu = nc.dram_tensor("wnat_gu", [EL0, D], BF16).ap()
    wnat_dn = nc.dram_tensor("wnat_dn", [D, F_L0], BF16).ap()
    wnat_q = nc.dram_tensor("wnat_q", [D, D], BF16).ap()
    wnat_k = nc.dram_tensor("wnat_k", [D, D], BF16).ap()
    wnat_v = nc.dram_tensor("wnat_v", [D, D], BF16).ap()
    wnat_o = nc.dram_tensor("wnat_o", [D, D], BF16).ap()
    wt_q = nc.dram_tensor("wtq", [D, D], BF16).ap()
    wt_k = nc.dram_tensor("wtk", [D, D], BF16).ap()
    wt_v = nc.dram_tensor("wtv", [D, D], BF16).ap()
    k1nat = nc.dram_tensor("k1nat", [T_LOC, D], BF16).ap()
    g_dram = nc.dram_tensor("g_dram", [T_LOC, F_L0], F32).ap()
    k2nat = nc.dram_tensor("k2nat", [T_LOC, F_L0], BF16).ap()
    xn2nat = nc.dram_tensor("xn2nat", [T_LOC, D], BF16).ap()

    sem_names = ["ld0", "ch_v", "ch_s", "mm", "wp_ld0", "wp_ld1", "wp_sg",
                 "wp_cv0", "wp_cv1", "wp_st0", "wp_st1", "wp_tt0", "wp_tt1",
                 "wp_wo0", "wp_wo1", "wl0", "wl1", "tt0", "tt1", "ue_s",
                 "ue_v", "gst", "gr0", "gr1", "kn", "k2t0", "k2t1",
                 "wd0", "wd1", "dn_ev", "ob", "qe0", "qe1"]

    import contextlib
    with contextlib.ExitStack() as st_:
        sems = {n: Sem(st_.enter_context(nc.semaphore(n))) for n in sem_names}
        consts = st_.enter_context(nc.sbuf_tensor("consts", [128, NCOL], F32))
        ones_col = st_.enter_context(nc.sbuf_tensor("ones_c", [128, 1], F32))
        ones_row = st_.enter_context(nc.sbuf_tensor("ones_r", [1, 128], F32))
        xs = st_.enter_context(nc.sbuf_tensor("xs", [128, NT * D], F32))
        scU = st_.enter_context(nc.sbuf_tensor("scU", [128, NT], F32))
        m2 = st_.enter_context(nc.sbuf_tensor("m2", [128, NT], F32))
        s2c = st_.enter_context(nc.sbuf_tensor("s2c", [128, NT], F32))
        sc2 = st_.enter_context(nc.sbuf_tensor("sc2", [128, NT], F32))
        block = st_.enter_context(nc.Block())

        ch_v, ch_s, mm = sems["ch_v"], sems["ch_s"], sems["mm"]
        ld0, kn = sems["ld0"], sems["kn"]

        @block.vector
        def _(vector):
            for name, val in (("neg1", -1.0), ("invsq", INV_SQRT_DH),
                              ("epsln", EPS_LN), ("epsq", EPS_Q), ("one", 1.0)):
                ch_v.inc(vector.memset(consts[:, CC[name]:CC[name] + 1], val), 1)

        counts = [QW_SHAPES[n][0] * QW_SHAPES[n][1] for n in QW_NAMES]
        emit_scales(nc, block, sems, partials, consts, ones_col, ones_row, counts)

        @block.gpsimd
        def _(gpsimd):
            for t in range(NT):
                ld0.inc(gpsimd.dma_start(out=xs[:, t * D:(t + 1) * D],
                                         in_=x_in[t * 128:(t + 1) * 128, :]), 16)
        x_loaded = ld0.n

        CW = 4096
        with (
            nc.sbuf_tensor("wpf", [128, 2 * CW], F32) as wpf,
            nc.sbuf_tensor("sg1", [128, CW], F32) as sg1,
            nc.sbuf_tensor("sg2", [128, CW], F32) as sg2,
            nc.sbuf_tensor("wpb", [128, 2 * CW], BF16) as wpb,
            nc.sbuf_tensor("wtt", [128, EL0], BF16) as wtt,
        ):
            bufs = {"wpf": wpf, "sg1": sg1, "sg2": sg2, "wpb": wpb, "wtt": wtt}
            consts_ready = ch_v.n

            @block.scalar
            def _(scalar):
                scalar.wait_ge(ch_v.h, consts_ready)

            @block.vector
            def _(vector):
                vector.wait_ge(ch_v.h, consts_ready)

            jobs = [
                (w_gu, wnat_gu, wt_gu, EL0, D, True, 0, True),
                (w_dn, wnat_dn, wt_dn, D, F_L0, True, 1, True),
                (w_q, wnat_q, wt_q, D, D, False, 0, True),
                (w_k, wnat_k, wt_k, D, D, False, 0, True),
                (w_v, wnat_v, wt_v, D, D, False, 0, True),
                (w_o, wnat_o, wt_o, D, D, False, 0, True),
                (w_l1w1, wnat_l1w1, wt_w1_out, F_L1, D, True, 2, True),
                (w_l1w2, wnat_l1w2, wt_w2_out, D, F_L1, True, 3, True),
                (w1q_in, wnq1_out, None, D, D, False, 0, False),
                (w1k_in, wnk1_out, None, D, 512, False, 0, False),
                (w1v_in, wnv1_out, None, D, 512, False, 0, False),
                (w1o_in, wno1_out, None, D, D, False, 0, False),
            ]
            for (win, wnat, wtout, R, C, tern, wi, dot) in jobs:
                flatR = R * C // CW
                emit_weight_prep(
                    nc, block, sems,
                    win.rearrange("r c -> (r c)").rearrange("(q c) -> q c", c=CW),
                    wnat.rearrange("r c -> (r c)").rearrange("(q c) -> q c", c=CW),
                    wtout, R, C, tern, consts, wi, bufs,
                    flat_rows=flatR, passA_cols=CW, wnat_geo=wnat,
                    do_transpose=dot)

        k1T = st_.enter_context(nc.sbuf_tensor("k1T", [128, 16 * T_LOC], BF16))
        with (
            nc.sbuf_tensor("sqs", [128, D], F32) as sqs,
            nc.sbuf_tensor("ybuf", [128, D], F32) as ybuf,
            nc.sbuf_tensor("kibuf", [128, D], I32) as kibuf,
            nc.sbuf_tensor("dbuf", [128, D], F32) as dbuf,
            nc.sbuf_tensor("t1buf", [128, D], F32) as t1buf,
            nc.sbuf_tensor("nkbuf", [128, D], BF16) as nkbuf,
            nc.sbuf_tensor("tcs", [128, 12], F32) as tcs,
        ):
            prep_gates = [(sems[n2], sems[n2].n) for n2 in
                          ("wp_ld0", "wp_ld1", "wp_sg", "wp_cv0", "wp_cv1",
                           "wp_st0", "wp_st1", "wp_tt0", "wp_tt1", "wp_wo0",
                           "wp_wo1")]
            emit_lnq_tiles(nc, block, sems, xs, k1nat, consts, tcs, sqs, ybuf,
                           kibuf, dbuf, t1buf, nkbuf,
                           [(ld0, x_loaded)] + prep_gates,
                           quant=True, sc_out=scU, sc_wcol=2, xn_nat=None)

            tt0 = sems["tt0"]
            k1_stored = kn.n

            @block.sync
            def _(sync):
                sync.wait_ge(kn.h, k1_stored)
                for s2_ in ("wp_st0", "wp_st1", "wp_wo0", "wp_wo1"):
                    sync.wait_ge(sems[s2_].h, sems[s2_].n)
                for dt in range(16):
                    tt0.inc(sync.dma_start_transpose(
                        out=k1T[:, dt * T_LOC:(dt + 1) * T_LOC],
                        in_=k1nat[:, dt * 128:(dt + 1) * 128]), 16)

        with (
            nc.sbuf_tensor("wua", [128, 2 * 16 * 512], BF16) as wua,
            nc.sbuf_tensor("wub", [128, 2 * 16 * 512], BF16) as wub,
            nc.sbuf_tensor("gab", [128, 2 * 512], F32) as gab,
            nc.sbuf_tensor("gbb", [128, 2 * 512], F32) as gbb,
            nc.sbuf_tensor("rmaxb", [128, 2], F32) as rmaxb,
        ):
            emit_ffn_up_glu(nc, block, sems, k1T, wt_gu, g_dram, scU, m2,
                            Fh=F_L0, glu=True, silu=True,
                            bufs={"wua": wua, "wub": wub, "ga": gab, "gb": gbb,
                                  "rmax": rmaxb})

        with (
            nc.sbuf_tensor("grl", [128, 2 * 1024], F32) as grl,
            nc.sbuf_tensor("y2", [128, 1024], F32) as y2,
            nc.sbuf_tensor("ki2", [128, 1024], I32) as ki2,
            nc.sbuf_tensor("d2", [128, 1024], F32) as d2,
            nc.sbuf_tensor("t12", [128, 1024], F32) as t12,
            nc.sbuf_tensor("nk2", [128, 1024], BF16) as nk2,
            nc.sbuf_tensor("k2T", [128, 2 * F_L0], BF16) as k2T,
            nc.sbuf_tensor("wdn", [128, 8 * D], BF16) as wdn,
        ):
            emit_quant_g(nc, block, sems, g_dram, k2nat, m2, s2c, sc2, consts,
                         grl, y2, ki2, d2, t12, nk2, dn_wi=1, Fh=F_L0,
                         extra_half=True)
            emit_ffn_down(nc, block, sems, k2nat, wt_dn, xs, sc2, k2T, wdn,
                          Fh=F_L0, Eo=D)
        x1_ready = sems["dn_ev"].n

        with (
            nc.sbuf_tensor("sqs2", [128, D], F32) as sqs2,
            nc.sbuf_tensor("xn2b", [128, D], BF16) as xn2b,
            nc.sbuf_tensor("tcs2", [128, 12], F32) as tcs2,
            nc.sbuf_tensor("xn2T", [128, 16 * T_LOC], BF16) as xn2T,
            nc.sbuf_tensor("wqkv", [128, 2 * 16 * 512], BF16) as wqkv,
            nc.sbuf_tensor("pev", [128, 4 * 512], BF16) as pev,
        ):
            emit_lnq_tiles(nc, block, sems, xs, None, consts, tcs2, sqs2, None,
                           None, None, None, xn2b, (sems["dn_ev"], x1_ready),
                           quant=False, sc_out=None, sc_wcol=None,
                           xn_nat=xn2nat)

            tt1 = sems["tt1"]
            xn2_stored = kn.n

            @block.sync
            def _(sync):
                sync.wait_ge(kn.h, xn2_stored)
                for dt in range(16):
                    tt1.inc(sync.dma_start_transpose(
                        out=xn2T[:, dt * T_LOC:(dt + 1) * T_LOC],
                        in_=xn2nat[:, dt * 128:(dt + 1) * 128]), 16)

            ob = sems["ob"]

            @block.gpsimd
            def _(gpsimd):
                gpsimd.wait_ge(sems["dn_ev"].h, x1_ready)
                gpsimd.wait_ge(tt1.h, tt1.n)
                for t in range(NT):
                    ob.inc(gpsimd.dma_start(out=x1_out[t * 128:(t + 1) * 128, :],
                                            in_=xs[:, t * D:(t + 1) * D]), 16)

            emit_qkv_l0(nc, block, sems, xn2T, wt_q, wt_k, wt_v, qt_out,
                        kt_out, v_out, consts, wqkv, pev)

        dbg_sb = st_.enter_context(nc.sbuf_tensor("dbgsb", [128, 4 * NT], F32))

        @block.vector
        def _(vector):
            vector.wait_ge(sems["dn_ev"].h, sems["dn_ev"].n)
            for i, src in enumerate((scU, m2, sc2, s2c)):
                ch_v.inc(vector.tensor_copy(dbg_sb[:, i * NT:(i + 1) * NT], src[:, :]), 1)

        @block.sync
        def _(sync):
            sync.wait_ge(ch_v.h, ch_v.n)
            sems["ob"].inc(sync.dma_start(out=dbg_out[:, :], in_=dbg_sb[:, :]), 16)
            for s_ in (sems["ob"], sems["qe0"], sems["qe1"], sems["wp_wo0"],
                       sems["wp_wo1"]):
                sync.wait_ge(s_.h, s_.n)
    return nc



def build_p2():
    nc = bass.Bass("TRN2", target_bir_lowering=False)
    x1_in = nc.dram_tensor("x1", [T_LOC, D], F32, kind="ExternalInput").ap()
    partials = nc.dram_tensor("partials", [8, len(QW_NAMES)], F32, kind="ExternalInput").ap()
    qt_in = nc.dram_tensor("qt", [D, T_LOC], BF16, kind="ExternalInput").ap()
    ktf_in = nc.dram_tensor("ktf", [D, SEQ], BF16, kind="ExternalInput").ap()
    vf_in = nc.dram_tensor("vf", [SEQ, D], BF16, kind="ExternalInput").ap()
    mask_in = nc.dram_tensor("mask", [128, NT * 512], F32, kind="ExternalInput").ap()
    gw_in = nc.dram_tensor("gate_w", [128, 128], F32, kind="ExternalInput").ap()
    gb_in = nc.dram_tensor("gate_b", [1, 128], F32, kind="ExternalInput").ap()
    wt_gu = nc.dram_tensor("wt_L0gu", [D, EL0], BF16, kind="ExternalInput").ap()
    wt_dn = nc.dram_tensor("wt_L0dn", [F_L0, D], BF16, kind="ExternalInput").ap()
    wt_o = nc.dram_tensor("wt_o", [D, D], BF16, kind="ExternalInput").ap()
    wt_w1_in = nc.dram_tensor("wt_L1w1", [D, F_L1], BF16, kind="ExternalInput").ap()
    wt_w2_in = nc.dram_tensor("wt_L1w2", [F_L1, D], BF16, kind="ExternalInput").ap()
    wnq1_in = nc.dram_tensor("wnat_q1", [D, D], BF16, kind="ExternalInput").ap()
    wnk1_in = nc.dram_tensor("wnat_k1", [D, 512], BF16, kind="ExternalInput").ap()
    wnv1_in = nc.dram_tensor("wnat_v1", [D, 512], BF16, kind="ExternalInput").ap()
    gq_in = nc.dram_tensor("gq_row", [1, D], F32, kind="ExternalInput").ap()
    gk_in = nc.dram_tensor("gk_row", [1, 512], F32, kind="ExternalInput").ap()
    tabs = {nm: nc.dram_tensor("tab_" + nm, [T_LOC, D if nm[0] == "q" else 512],
                               F32, kind="ExternalInput").ap()
            for nm in ("qc", "qs", "kc", "ks")}

    x4_out = nc.dram_tensor("x4", [T_LOC, D], F32, kind="ExternalOutput").ap()
    qt1_out = nc.dram_tensor("qt1", [D, T_LOC], BF16, kind="ExternalOutput").ap()
    kt1_out = nc.dram_tensor("kt1", [512, T_LOC], BF16, kind="ExternalOutput").ap()
    v1_out = nc.dram_tensor("v1", [T_LOC, 512], BF16, kind="ExternalOutput").ap()
    dbg_out = nc.dram_tensor("dbg", [T_LOC, D], BF16, kind="ExternalOutput").ap()

    onat = nc.dram_tensor("onat", [T_LOC, D], BF16).ap()
    k1nat = nc.dram_tensor("k1nat", [T_LOC, D], BF16).ap()
    g_dram = nc.dram_tensor("g_dram", [T_LOC, F_L0], F32).ap()
    k2nat = nc.dram_tensor("k2nat", [T_LOC, F_L0], BF16).ap()
    xn2nat = nc.dram_tensor("xn2nat", [T_LOC, D], BF16).ap()
    qnat = nc.dram_tensor("qnat", [T_LOC, D], BF16).ap()
    knat = nc.dram_tensor("knat", [T_LOC, 512], BF16).ap()

    sem_names = ["ld0", "ch_v", "ch_s", "ch_p", "mm", "wp_ld0", "wp_ld1",
                 "wp_sg", "wp_cv0", "wp_cv1", "wp_st0", "wp_st1", "wp_tt0",
                 "wp_tt1", "wp_wo0", "wp_wo1", "wl0", "wl1", "tt0", "tt1",
                 "ue_s", "ue_v", "gst", "gr0", "gr1", "kn", "k2t0", "k2t1",
                 "wd0", "wd1", "dn_ev", "ob", "sd0", "sd1", "sd2", "sd3",
                 "gd0", "gd1", "gd2", "gd3", "kv0", "kv1", "oe"]

    import contextlib
    with contextlib.ExitStack() as st_:
        sems = {n: Sem(st_.enter_context(nc.semaphore(n))) for n in sem_names}
        consts = st_.enter_context(nc.sbuf_tensor("consts", [128, NCOL], F32))
        ones_col = st_.enter_context(nc.sbuf_tensor("ones_c", [128, 1], F32))
        ones_row = st_.enter_context(nc.sbuf_tensor("ones_r", [1, 128], F32))
        xs = st_.enter_context(nc.sbuf_tensor("xs", [128, NT * D], F32))
        scU = st_.enter_context(nc.sbuf_tensor("scU", [128, NT], F32))
        m2 = st_.enter_context(nc.sbuf_tensor("m2", [128, NT], F32))
        s2c = st_.enter_context(nc.sbuf_tensor("s2c", [128, NT], F32))
        sc2 = st_.enter_context(nc.sbuf_tensor("sc2", [128, NT], F32))
        block = st_.enter_context(nc.Block())

        ch_v, ch_s, mm = sems["ch_v"], sems["ch_s"], sems["mm"]
        ld0, kn = sems["ld0"], sems["kn"]

        @block.vector
        def _(vector):
            for name, val in (("neg1", -1.0), ("invsq", INV_SQRT_DH),
                              ("epsln", EPS_LN), ("epsq", EPS_Q), ("one", 1.0)):
                ch_v.inc(vector.memset(consts[:, CC[name]:CC[name] + 1], val), 1)

        counts = [QW_SHAPES[n][0] * QW_SHAPES[n][1] for n in QW_NAMES]
        emit_scales(nc, block, sems, partials, consts, ones_col, ones_row, counts)

        @block.gpsimd
        def _(gpsimd):
            for t in range(NT):
                ld0.inc(gpsimd.dma_start(out=xs[:, t * D:(t + 1) * D],
                                         in_=x1_in[t * 128:(t + 1) * 128, :]), 16)
        x_loaded = ld0.n

        es0 = contextlib.ExitStack()
        qt_sb = es0.enter_context(nc.sbuf_tensor("qt_sb", [128, 16 * T_LOC], BF16))
        mask_sb = es0.enter_context(nc.sbuf_tensor("mask_sb", [128, NT * 512], F32))
        gwf = es0.enter_context(nc.sbuf_tensor("gwf", [128, 128], F32))
        gw16 = es0.enter_context(nc.sbuf_tensor("gw16", [128, 128], BF16))
        gwT = es0.enter_context(nc.sbuf_tensor("gwT", [128, 128], BF16))
        gbB = es0.enter_context(nc.sbuf_tensor("gbB", [128, 128], F32))
        gbr = es0.enter_context(nc.sbuf_tensor("gbr", [1, 128], F32))
        kbuf = es0.enter_context(nc.sbuf_tensor("kbuf", [128, 2 * SEQ], BF16))
        vbuf = es0.enter_context(nc.sbuf_tensor("vbuf", [128, 2 * 32 * 132], BF16))
        qsq = es0.enter_context(nc.sbuf_tensor("qsq", [128, T_LOC], F32))
        ksq = es0.enter_context(nc.sbuf_tensor("ksq", [128, SEQ], F32))
        kn2 = es0.enter_context(nc.sbuf_tensor("kn2", [1, SEQ], F32))
        qn2 = es0.enter_context(nc.sbuf_tensor("qn2", [1, T_LOC], F32))
        ptb = es0.enter_context(nc.sbuf_tensor("ptb", [128, 2 * 512], BF16))
        gsb = es0.enter_context(nc.sbuf_tensor("gsb", [128, 2 * 128], F32))
        obuf = es0.enter_context(nc.sbuf_tensor("obuf", [128, 2 * 128], BF16))
        qm2 = es0.enter_context(nc.sbuf_tensor("qm2", [1, 16], F32))
        km2 = es0.enter_context(nc.sbuf_tensor("km2", [1, 16], F32))
        nbrow = es0.enter_context(nc.sbuf_tensor("nbrow", [1, 16], F32))
        negB = es0.enter_context(nc.sbuf_tensor("negB", [128, 16], F32))
        rcolf = es0.enter_context(nc.sbuf_tensor("rcolf", [128, 2], F32))
        brec = Rec(nc, block, sems, "p2load")
        qts = []
        for h in range(16):
            qts.append(brec.op("syncd", lambda E, h=h: E.dma_start(
                out=qt_sb[:, h * T_LOC:(h + 1) * T_LOC],
                in_=qt_in[h * 128:(h + 1) * 128, :])))
        mld = brec.op("syncd", lambda E: E.dma_start(
            out=mask_sb[:, :], in_=mask_in[:, :]))
        gld = brec.op("syncd", lambda E: E.dma_start(
            out=gwf[:, :], in_=gw_in[:, :]))
        gbl = brec.op("syncd", lambda E: E.dma_start(
            out=gbr[:, :], in_=gb_in[:, :]))
        gcv = brec.op("vector", lambda E: E.tensor_copy(gw16[:, :], gwf[:, :]),
                      waits=(gld,))
        gtt = brec.op("syncd", lambda E: E.dma_start_transpose(
            out=gwT[:, :], in_=gw16[:, :]), waits=(gcv,))
        with nc.psum_tensor("ps_gb", [128, 512], F32) as psGB:
            gbm = brec.op("tensor", lambda E: E.matmul(
                psGB[:, 0:128], ones_row[:, 0:128], gbr[:, :],
                start=True, stop=True), waits=(gbl,))
            gbc = brec.op("vector", lambda E: E.tensor_copy(
                gbB[:, :], psGB[:, 0:128]), waits=(gbm,))
            brec.emit()

        entry = [("raw", sems["sd%d" % r].h, sems["sd%d" % r].n)
                 for r in range(4)] + [("raw", ch_v.h, ch_v.n)]
        fbufs = {"kbuf": kbuf, "vbuf": vbuf, "qsq": qsq, "ksq": ksq,
                 "kn2": kn2, "qn2": qn2, "ptb": ptb, "gsb": gsb,
                 "obuf": obuf, "qm2": qm2, "km2": km2, "nbrow": nbrow,
                 "negB": negB, "rcol": rcolf, "ones_row": ones_row}
        gate_ready = ("raw", ch_v.h, ch_v.n)
        emit_flash(nc, block, sems, qt_sb, ktf_in, vf_in, mask_sb, onat,
                   16, 1, (gwT, gbB, gate_ready), fbufs, entry)
        es0.close()
        flash_done = [("raw", sems["sd%d" % r].h, sems["sd%d" % r].n)
                      for r in range(4)]

        @block.gpsimd
        def _(gpsimd):
            for r in range(4):
                gpsimd.wait_ge(sems["sd%d" % r].h, sems["sd%d" % r].n)
            for t in range(NT):
                sems["oe"].inc(gpsimd.dma_start(
                    out=dbg_out[t * 128:(t + 1) * 128, :],
                    in_=onat[t * 128:(t + 1) * 128, :]), 16)

        es1 = contextlib.ExitStack()
        oT = es1.enter_context(nc.sbuf_tensor("oT", [128, 16 * T_LOC], BF16))
        wcho = es1.enter_context(nc.sbuf_tensor("wcho", [128, 2 * 16 * 512], BF16))
        @block.vector
        def _(vector):
            vector.wait_ge(ld0.h, x_loaded)
            ch_v.inc(vector.tensor_copy(scU[:, 0:1], scU[:, 0:1]), 1)
        emit_oproj_residual(nc, block, sems, onat, wt_o, xs, oT, wcho,
                            flash_done + [("raw", ld0.h, x_loaded)])
        es1.close()
        x2_ready = (sems["ch_v"], ch_v.n)

        esK = contextlib.ExitStack()
        k1T = esK.enter_context(nc.sbuf_tensor("k1T", [128, 16 * T_LOC], BF16))
        es2 = contextlib.ExitStack()
        sqs = es2.enter_context(nc.sbuf_tensor("sqs", [128, D], F32))
        ybuf = es2.enter_context(nc.sbuf_tensor("ybuf", [128, D], F32))
        kibuf = es2.enter_context(nc.sbuf_tensor("kibuf", [128, D], I32))
        dbuf = es2.enter_context(nc.sbuf_tensor("dbuf", [128, D], F32))
        t1buf = es2.enter_context(nc.sbuf_tensor("t1buf", [128, D], F32))
        nkbuf = es2.enter_context(nc.sbuf_tensor("nkbuf", [128, D], BF16))
        tcs = es2.enter_context(nc.sbuf_tensor("tcs", [128, 12], F32))
        emit_lnq_tiles(nc, block, sems, xs, k1nat, consts, tcs, sqs, ybuf,
                       kibuf, dbuf, t1buf, nkbuf,
                       (x2_ready[0], x2_ready[1]),
                       quant=True, sc_out=scU, sc_wcol=2, xn_nat=None)
        tt0 = sems["tt0"]
        k1_stored = kn.n

        @block.sync
        def _(sync):
            sync.wait_ge(kn.h, k1_stored)
            for dt in range(16):
                tt0.inc(sync.dma_start_transpose(
                    out=k1T[:, dt * T_LOC:(dt + 1) * T_LOC],
                    in_=k1nat[:, dt * 128:(dt + 1) * 128]), 16)

        es2.close()
        es3 = contextlib.ExitStack()
        wua = es3.enter_context(nc.sbuf_tensor("wua", [128, 2 * 16 * 512], BF16))
        wub = es3.enter_context(nc.sbuf_tensor("wub", [128, 2 * 16 * 512], BF16))
        gab = es3.enter_context(nc.sbuf_tensor("gab", [128, 2 * 512], F32))
        gbb2 = es3.enter_context(nc.sbuf_tensor("gbb2", [128, 2 * 512], F32))
        rmaxb = es3.enter_context(nc.sbuf_tensor("rmaxb", [128, 2], F32))
        emit_ffn_up_glu(nc, block, sems, k1T, wt_gu, g_dram, scU, m2,
                        Fh=F_L0, glu=True, silu=True,
                        bufs={"wua": wua, "wub": wub, "ga": gab,
                              "gb": gbb2, "rmax": rmaxb})
        es3.close()
        es4 = contextlib.ExitStack()
        grl = es4.enter_context(nc.sbuf_tensor("grl", [128, 2 * 1024], F32))
        y2 = es4.enter_context(nc.sbuf_tensor("y2", [128, 1024], F32))
        ki2 = es4.enter_context(nc.sbuf_tensor("ki2", [128, 1024], I32))
        d2 = es4.enter_context(nc.sbuf_tensor("d2", [128, 1024], F32))
        t12 = es4.enter_context(nc.sbuf_tensor("t12", [128, 1024], F32))
        nk2 = es4.enter_context(nc.sbuf_tensor("nk2", [128, 1024], BF16))
        k2T = es4.enter_context(nc.sbuf_tensor("k2T", [128, 2 * F_L0], BF16))
        wdn = es4.enter_context(nc.sbuf_tensor("wdn", [128, 8 * D], BF16))
        emit_quant_g(nc, block, sems, g_dram, k2nat, m2, s2c, sc2, consts,
                     grl, y2, ki2, d2, t12, nk2, dn_wi=1, Fh=F_L0,
                     extra_half=True)
        emit_ffn_down(nc, block, sems, k2nat, wt_dn, xs, sc2, k2T, wdn,
                      Fh=F_L0, Eo=D)
        es4.close()
        x3_ready_n = sems["dn_ev"].n

        es6 = contextlib.ExitStack()
        sqs = es6.enter_context(nc.sbuf_tensor("sqsB", [128, D], F32))
        ybuf = es6.enter_context(nc.sbuf_tensor("ybufB", [128, D], F32))
        kibuf = es6.enter_context(nc.sbuf_tensor("kibufB", [128, D], I32))
        dbuf = es6.enter_context(nc.sbuf_tensor("dbufB", [128, D], F32))
        t1buf = es6.enter_context(nc.sbuf_tensor("t1bufB", [128, D], F32))
        nkbuf = es6.enter_context(nc.sbuf_tensor("nkbufB", [128, D], BF16))
        tcs = es6.enter_context(nc.sbuf_tensor("tcsB", [128, 12], F32))
        emit_lnq_tiles(nc, block, sems, xs, k1nat, consts, tcs, sqs, ybuf,
                       kibuf, dbuf, t1buf, nkbuf,
                       (sems["dn_ev"], x3_ready_n),
                       quant=True, sc_out=scU, sc_wcol=4 * 2 + 2,
                       xn_nat=None)
        tt0 = sems["tt0"]
        k1b_stored = kn.n

        @block.sync
        def _(sync):
            sync.wait_ge(kn.h, k1b_stored)
            sync.wait_ge(sems["ue_v"].h, sems["ue_v"].n)
            for dt in range(16):
                tt0.inc(sync.dma_start_transpose(
                    out=k1T[:, dt * T_LOC:(dt + 1) * T_LOC],
                    in_=k1nat[:, dt * 128:(dt + 1) * 128]), 16)

        es6.close()
        es7 = contextlib.ExitStack()
        wua = es7.enter_context(nc.sbuf_tensor("wuaB", [128, 2 * 16 * 512], BF16))
        gab = es7.enter_context(nc.sbuf_tensor("gabB", [128, 2 * 512], F32))
        gbb2 = es7.enter_context(nc.sbuf_tensor("gbbB", [128, 2 * 512], F32))
        rmaxb = es7.enter_context(nc.sbuf_tensor("rmaxB", [128, 2], F32))
        emit_ffn_up_glu(nc, block, sems, k1T, wt_w1_in, g_dram, scU, m2,
                        Fh=F_L1, glu=False, silu=False,
                        bufs={"wua": wua, "wub": wua, "ga": gab,
                              "gb": gbb2, "rmax": rmaxb})
        es7.close()
        es8 = contextlib.ExitStack()
        grl = es8.enter_context(nc.sbuf_tensor("grlB", [128, 2 * 1024], F32))
        y2 = es8.enter_context(nc.sbuf_tensor("y2B", [128, 1024], F32))
        ki2 = es8.enter_context(nc.sbuf_tensor("ki2B", [128, 1024], I32))
        d2 = es8.enter_context(nc.sbuf_tensor("d2B", [128, 1024], F32))
        t12 = es8.enter_context(nc.sbuf_tensor("t12B", [128, 1024], F32))
        nk2 = es8.enter_context(nc.sbuf_tensor("nk2B", [128, 1024], BF16))
        k2T = es8.enter_context(nc.sbuf_tensor("k2TB", [128, 2 * F_L1], BF16))
        wdn = es8.enter_context(nc.sbuf_tensor("wdnB", [128, 8 * D], BF16))
        emit_quant_g(nc, block, sems, g_dram, k2nat, m2, s2c, sc2, consts,
                     grl, y2, ki2, d2, t12, nk2, dn_wi=3, Fh=F_L1,
                     extra_half=True)
        emit_ffn_down(nc, block, sems, k2nat, wt_w2_in, xs, sc2, k2T, wdn,
                      Fh=F_L1, Eo=D)
        es8.close()
        x4_ready_n = sems["dn_ev"].n
        esK.close()

        ob = sems["ob"]

        es9 = contextlib.ExitStack()
        sqs = es9.enter_context(nc.sbuf_tensor("sqsC", [128, D], F32))
        xnb = es9.enter_context(nc.sbuf_tensor("xnb", [128, D], BF16))
        tcs = es9.enter_context(nc.sbuf_tensor("tcsC", [128, 12], F32))
        emit_lnq_tiles(nc, block, sems, xs, None, consts, tcs, sqs, None,
                       None, None, None, xnb, (sems["dn_ev"], x4_ready_n),
                       quant=False, sc_out=None, sc_wcol=None,
                       xn_nat=xn2nat)
        es9.close()
        xn_stored = kn.n

        es10 = contextlib.ExitStack()
        qfull = es10.enter_context(nc.sbuf_tensor("qfull", [128, NT * D], BF16))
        kfull = es10.enter_context(nc.sbuf_tensor("kfull", [128, NT * 512], BF16))
        v16 = es10.enter_context(nc.sbuf_tensor("v16", [128, 512], BF16))
        gqB = es10.enter_context(nc.sbuf_tensor("gqB", [128, D], F32))
        gkB = es10.enter_context(nc.sbuf_tensor("gkB", [128, 512], F32))
        gqr = es10.enter_context(nc.sbuf_tensor("gqr", [1, D], F32))
        gkr = es10.enter_context(nc.sbuf_tensor("gkr", [1, 512], F32))
        esPJ = contextlib.ExitStack()
        xn2T = esPJ.enter_context(nc.sbuf_tensor("xn2T", [128, 16 * T_LOC], BF16))
        wchL = esPJ.enter_context(nc.sbuf_tensor("wchL", [128, 2 * 16 * 512], BF16))
        tt1 = sems["tt1"]

        @block.sync
        def _(sync):
            sync.wait_ge(kn.h, xn_stored)
            for dt in range(16):
                tt1.inc(sync.dma_start_transpose(
                    out=xn2T[:, dt * T_LOC:(dt + 1) * T_LOC],
                    in_=xn2nat[:, dt * 128:(dt + 1) * 128]), 16)

        @block.gpsimd
        def _(gpsimd):
            gpsimd.wait_ge(sems["dn_ev"].h, x4_ready_n)
            gpsimd.wait_ge(tt1.h, tt1.n)
            for t in range(NT):
                ob.inc(gpsimd.dma_start(out=x4_out[t * 128:(t + 1) * 128, :],
                                        in_=xs[:, t * D:(t + 1) * D]), 16)

        grec = Rec(nc, block, sems, "gains")
        g1 = grec.op("syncd", lambda E: E.dma_start(out=gqr[:, :], in_=gq_in[:, :]))
        g2 = grec.op("syncd", lambda E: E.dma_start(out=gkr[:, :], in_=gk_in[:, :]))
        with nc.psum_tensor("ps_gq", [128, 4 * 512], F32) as psGQ:
            mms = []
            for c in range(4):
                mms.append(grec.op("tensor", lambda E, c=c: E.matmul(
                    psGQ[:, c * 512:(c + 1) * 512], ones_row[:, 0:128],
                    gqr[:, c * 512:(c + 1) * 512], start=True, stop=True),
                    waits=(g1,)))
            cps = []
            for c in range(4):
                cps.append(grec.op("vector", lambda E, c=c: E.tensor_copy(
                    gqB[:, c * 512:(c + 1) * 512],
                    psGQ[:, c * 512:(c + 1) * 512]), waits=(mms[c],)))
            mk = grec.op("tensor", lambda E: E.matmul(
                psGQ[:, 0:512], ones_row[:, 0:128], gkr[:, :],
                start=True, stop=True), waits=(g2, cps[0]))
            grec.op("vector", lambda E: E.tensor_copy(
                gkB[:, :], psGQ[:, 0:512]), waits=(mk,))
            grec.emit()

        entry = [("raw", tt1.h, tt1.n),
                 ("raw", ch_v.h, ch_v.n)]
        emit_l1_proj(nc, block, sems, xn2T, wnq1_in, wnk1_in, wnv1_in,
                     qfull, kfull, v1_out, v16, wchL, entry)
        esPJ.close()

        with (
            nc.sbuf_tensor("qbD", [128, D], F32) as qb,
            nc.sbuf_tensor("rotD", [128, D], F32) as rot,
            nc.sbuf_tensor("tqD", [128, 2 * D], F32) as tq,
            nc.sbuf_tensor("tkD", [128, 2 * 512], F32) as tk,
            nc.sbuf_tensor("qssD", [128, 16], F32) as qss,
            nc.sbuf_tensor("kssD", [128, 4], F32) as kss,
            nc.sbuf_tensor("qb16D", [128, D], BF16) as qb16,
            nc.sbuf_tensor("kb16D", [128, 512], BF16) as kb16,
            nc.sbuf_tensor("kbD", [128, 1024], F32) as kb,
            nc.sbuf_tensor("TTD", [128, T_LOC], BF16) as TT,
        ):
            nrentry = [("raw", ch_v.h, ch_v.n),
                       ("raw", ch_s.h, ch_s.n)]
            nrbufs = {"qb": qb, "rot": rot, "tq": tq, "tk": tk,
                      "qss": qss, "kss": kss, "qb16": qb16, "kb16": kb16,
                      "kb": kb, "TT": TT}
            emit_l1_normrope(nc, block, sems, qfull, kfull, gqB, gkB,
                             tabs, qnat, knat, qt1_out, kt1_out, nrbufs,
                             nrentry)

        es10.close()
        @block.sync
        def _(sync):
            for s_ in (sems["ob"], sems["oe"], sems["ch_v"]):
                sync.wait_ge(s_.h, s_.n)
            for r in range(4):
                sync.wait_ge(sems["sd%d" % r].h, sems["sd%d" % r].n)
    return nc



def build_p3():
    nc = bass.Bass("TRN2", target_bir_lowering=False)
    x4_in = nc.dram_tensor("x4", [T_LOC, D], F32, kind="ExternalInput").ap()
    partials = nc.dram_tensor("partials", [8, len(QW_NAMES)], F32, kind="ExternalInput").ap()
    qt_in = nc.dram_tensor("qt1", [D, T_LOC], BF16, kind="ExternalInput").ap()
    ktf_in = nc.dram_tensor("ktf1", [512, SEQ], BF16, kind="ExternalInput").ap()
    vf_in = nc.dram_tensor("vf1", [SEQ, 512], BF16, kind="ExternalInput").ap()
    mask_in = nc.dram_tensor("mask", [128, NT * 512], F32, kind="ExternalInput").ap()
    wno1_in = nc.dram_tensor("wnat_o1", [D, D], BF16, kind="ExternalInput").ap()
    wt_w1_in = nc.dram_tensor("wt_L1w1", [D, F_L1], BF16, kind="ExternalInput").ap()
    wt_w2_in = nc.dram_tensor("wt_L1w2", [F_L1, D], BF16, kind="ExternalInput").ap()
    x6_out = nc.dram_tensor("x6", [T_LOC, D], F32, kind="ExternalOutput").ap()

    onat = nc.dram_tensor("onat", [T_LOC, D], BF16).ap()
    k1nat = nc.dram_tensor("k1nat", [T_LOC, D], BF16).ap()
    g_dram = nc.dram_tensor("g_dram", [T_LOC, F_L1], F32).ap()
    k2nat = nc.dram_tensor("k2nat", [T_LOC, F_L1], BF16).ap()

    sem_names = ["ld0", "ch_v", "ch_s", "ch_p", "mm", "wl0", "wl1", "tt0",
                 "tt1", "ue_s", "ue_v", "gst", "gr0", "gr1", "kn", "k2t0",
                 "k2t1", "wd0", "wd1", "dn_ev", "ob", "sd0", "sd1", "sd2",
                 "sd3", "gd0", "gd1", "gd2", "gd3", "kv0", "kv1", "oe"]

    import contextlib
    with contextlib.ExitStack() as st_:
        sems = {n: Sem(st_.enter_context(nc.semaphore(n))) for n in sem_names}
        consts = st_.enter_context(nc.sbuf_tensor("consts", [128, NCOL], F32))
        ones_col = st_.enter_context(nc.sbuf_tensor("ones_c", [128, 1], F32))
        ones_row = st_.enter_context(nc.sbuf_tensor("ones_r", [1, 128], F32))
        xs = st_.enter_context(nc.sbuf_tensor("xs", [128, NT * D], F32))
        scU = st_.enter_context(nc.sbuf_tensor("scU", [128, NT], F32))
        m2 = st_.enter_context(nc.sbuf_tensor("m2", [128, NT], F32))
        s2c = st_.enter_context(nc.sbuf_tensor("s2c", [128, NT], F32))
        sc2 = st_.enter_context(nc.sbuf_tensor("sc2", [128, NT], F32))
        block = st_.enter_context(nc.Block())

        ch_v, ch_s, mm = sems["ch_v"], sems["ch_s"], sems["mm"]
        ld0, kn = sems["ld0"], sems["kn"]

        @block.vector
        def _(vector):
            for name, val in (("neg1", -1.0), ("invsq", INV_SQRT_DH),
                              ("epsln", EPS_LN), ("epsq", EPS_Q), ("one", 1.0)):
                ch_v.inc(vector.memset(consts[:, CC[name]:CC[name] + 1], val), 1)

        counts = [QW_SHAPES[n][0] * QW_SHAPES[n][1] for n in QW_NAMES]
        emit_scales(nc, block, sems, partials, consts, ones_col, ones_row, counts)

        @block.gpsimd
        def _(gpsimd):
            for t in range(NT):
                ld0.inc(gpsimd.dma_start(out=xs[:, t * D:(t + 1) * D],
                                         in_=x4_in[t * 128:(t + 1) * 128, :]), 16)
        x_loaded = ld0.n

        esF = contextlib.ExitStack()
        qt_sb = esF.enter_context(nc.sbuf_tensor("qt_sb", [128, 16 * T_LOC], BF16))
        mask_sb = esF.enter_context(nc.sbuf_tensor("mask_sb", [128, NT * 512], F32))
        kbuf = esF.enter_context(nc.sbuf_tensor("kbuf", [128, 2 * SEQ], BF16))
        vbuf = esF.enter_context(nc.sbuf_tensor("vbuf", [128, 2 * 32 * 132], BF16))
        qsq = esF.enter_context(nc.sbuf_tensor("qsq", [128, T_LOC], F32))
        ksq = esF.enter_context(nc.sbuf_tensor("ksq", [128, SEQ], F32))
        kn2 = esF.enter_context(nc.sbuf_tensor("kn2", [1, SEQ], F32))
        qn2 = esF.enter_context(nc.sbuf_tensor("qn2", [1, T_LOC], F32))
        ptb = esF.enter_context(nc.sbuf_tensor("ptb", [128, 2 * 512], BF16))
        gsb = esF.enter_context(nc.sbuf_tensor("gsb", [128, 2 * 128], F32))
        obuf = esF.enter_context(nc.sbuf_tensor("obuf", [128, 2 * 128], BF16))
        qm2 = esF.enter_context(nc.sbuf_tensor("qm2", [1, 16], F32))
        km2 = esF.enter_context(nc.sbuf_tensor("km2", [1, 16], F32))
        nbrow = esF.enter_context(nc.sbuf_tensor("nbrow", [1, 16], F32))
        negB = esF.enter_context(nc.sbuf_tensor("negB", [128, 16], F32))
        rcolf = esF.enter_context(nc.sbuf_tensor("rcolf", [128, 2], F32))

        brec = Rec(nc, block, sems, "p3load")
        for h in range(16):
            brec.op("syncd", lambda E, h=h: E.dma_start(
                out=qt_sb[:, h * T_LOC:(h + 1) * T_LOC],
                in_=qt_in[h * 128:(h + 1) * 128, :]))
        brec.op("syncd", lambda E: E.dma_start(out=mask_sb[:, :], in_=mask_in[:, :]))
        brec.emit()

        entry = [("raw", sems["sd%d" % r].h, sems["sd%d" % r].n)
                 for r in range(4)] + [("raw", ch_v.h, ch_v.n)]
        fbufs = {"kbuf": kbuf, "vbuf": vbuf, "qsq": qsq, "ksq": ksq,
                 "kn2": kn2, "qn2": qn2, "ptb": ptb, "gsb": gsb,
                 "obuf": obuf, "qm2": qm2, "km2": km2, "nbrow": nbrow,
                 "negB": negB, "rcol": rcolf, "ones_row": ones_row}
        emit_flash(nc, block, sems, qt_sb, ktf_in, vf_in, mask_sb, onat,
                   16, 4, None, fbufs, entry)
        esF.close()
        flash_done = [("raw", sems["sd%d" % r].h, sems["sd%d" % r].n)
                      for r in range(4)]

        esO = contextlib.ExitStack()
        oT = esO.enter_context(nc.sbuf_tensor("oT", [128, 16 * T_LOC], BF16))
        wcho = esO.enter_context(nc.sbuf_tensor("wcho", [128, 2 * 16 * 512], BF16))

        @block.vector
        def _(vector):
            vector.wait_ge(ld0.h, x_loaded)
            ch_v.inc(vector.tensor_copy(scU[:, 0:1], scU[:, 0:1]), 1)
        emit_oproj_residual(nc, block, sems, onat, wno1_in, xs, oT, wcho,
                            flash_done + [("raw", ld0.h, x_loaded)])
        esO.close()
        x5_ready = (sems["ch_v"], ch_v.n)

        esK = contextlib.ExitStack()
        k1T = esK.enter_context(nc.sbuf_tensor("k1T", [128, 16 * T_LOC], BF16))
        esA = contextlib.ExitStack()
        sqs = esA.enter_context(nc.sbuf_tensor("sqs", [128, D], F32))
        ybuf = esA.enter_context(nc.sbuf_tensor("ybuf", [128, D], F32))
        kibuf = esA.enter_context(nc.sbuf_tensor("kibuf", [128, D], I32))
        dbuf = esA.enter_context(nc.sbuf_tensor("dbuf", [128, D], F32))
        t1buf = esA.enter_context(nc.sbuf_tensor("t1buf", [128, D], F32))
        nkbuf = esA.enter_context(nc.sbuf_tensor("nkbuf", [128, D], BF16))
        tcs = esA.enter_context(nc.sbuf_tensor("tcs", [128, 12], F32))
        emit_lnq_tiles(nc, block, sems, xs, k1nat, consts, tcs, sqs, ybuf,
                       kibuf, dbuf, t1buf, nkbuf, x5_ready,
                       quant=True, sc_out=scU, sc_wcol=4 * 2 + 2, xn_nat=None)
        tt0 = sems["tt0"]
        k1_stored = kn.n

        @block.sync
        def _(sync):
            sync.wait_ge(kn.h, k1_stored)
            for dt in range(16):
                tt0.inc(sync.dma_start_transpose(
                    out=k1T[:, dt * T_LOC:(dt + 1) * T_LOC],
                    in_=k1nat[:, dt * 128:(dt + 1) * 128]), 16)
        esA.close()

        esB = contextlib.ExitStack()
        wua = esB.enter_context(nc.sbuf_tensor("wua", [128, 2 * 16 * 512], BF16))
        gab = esB.enter_context(nc.sbuf_tensor("gab", [128, 2 * 512], F32))
        gbb2 = esB.enter_context(nc.sbuf_tensor("gbb2", [128, 2 * 512], F32))
        rmaxb = esB.enter_context(nc.sbuf_tensor("rmaxb", [128, 2], F32))
        emit_ffn_up_glu(nc, block, sems, k1T, wt_w1_in, g_dram, scU, m2,
                        Fh=F_L1, glu=False, silu=False,
                        bufs={"wua": wua, "wub": wua, "ga": gab, "gb": gbb2,
                              "rmax": rmaxb})
        esB.close()

        esC = contextlib.ExitStack()
        grl = esC.enter_context(nc.sbuf_tensor("grl", [128, 2 * 1024], F32))
        y2 = esC.enter_context(nc.sbuf_tensor("y2", [128, 1024], F32))
        ki2 = esC.enter_context(nc.sbuf_tensor("ki2", [128, 1024], I32))
        d2 = esC.enter_context(nc.sbuf_tensor("d2", [128, 1024], F32))
        t12 = esC.enter_context(nc.sbuf_tensor("t12", [128, 1024], F32))
        nk2 = esC.enter_context(nc.sbuf_tensor("nk2", [128, 1024], BF16))
        k2T = esC.enter_context(nc.sbuf_tensor("k2T", [128, 2 * F_L1], BF16))
        wdn = esC.enter_context(nc.sbuf_tensor("wdn", [128, 8 * D], BF16))
        emit_quant_g(nc, block, sems, g_dram, k2nat, m2, s2c, sc2, consts,
                     grl, y2, ki2, d2, t12, nk2, dn_wi=3, Fh=F_L1,
                     extra_half=True)
        emit_ffn_down(nc, block, sems, k2nat, wt_w2_in, xs, sc2, k2T, wdn,
                      Fh=F_L1, Eo=D)
        esC.close()
        esK.close()
        x6_ready_n = sems["dn_ev"].n

        ob = sems["ob"]

        @block.gpsimd
        def _(gpsimd):
            gpsimd.wait_ge(sems["dn_ev"].h, x6_ready_n)
            for t in range(NT):
                ob.inc(gpsimd.dma_start(out=x6_out[t * 128:(t + 1) * 128, :],
                                        in_=xs[:, t * D:(t + 1) * D]), 16)

        @block.sync
        def _(sync):
            sync.wait_ge(ob.h, ob.n)
            for r in range(4):
                sync.wait_ge(sems["sd%d" % r].h, sems["sd%d" % r].n)
    return nc



def _rope_tables(core, width, scale=1.0):
    k = core % 4
    pos = np.array([(k + 4 * (r // 128)) * 128 + r % 128 for r in range(T_LOC)],
                   dtype=np.float64)
    ang = 10000.0 ** (-np.arange(64) / 64.0)
    rad = pos[:, None] * ang[None, :]
    c = np.cos(rad)
    sn = np.sin(rad)
    nh = width // 128
    C = np.tile(np.concatenate([c, c], 1), (1, nh)) * scale
    S = np.tile(np.concatenate([sn, sn], 1), (1, nh)) * scale
    return (np.ascontiguousarray(C, dtype=np.float32),
            np.ascontiguousarray(S, dtype=np.float32))


def _build_mask(core):
    k = core % 4
    m = np.zeros((128, NT * 512), dtype=np.float32)
    for j in range(NT):
        tq = k + 4 * j
        for i in range(4):
            kt = 4 * j + i
            key_abs = kt * 128 + np.arange(128)[:, None]
            q_abs = tq * 128 + np.arange(128)[None, :]
            m[:, j * 512 + i * 128:j * 512 + (i + 1) * 128] = np.where(
                key_abs <= q_abs, 0.0, -1e30).astype(np.float32)
    return m


def kernel(L0_attn_x, L0_attn_q_w, L0_attn_k_w, L0_attn_v_w, L0_attn_o_w,
           L0_attn_gate_w, L0_attn_gate_b, L0_ffn_w_gate_up, L0_ffn_w_down,
           L1_attn_wq, L1_attn_wk, L1_attn_wv, L1_attn_wo, L1_attn_gq,
           L1_attn_gk, L1_ffn_w1, L1_ffn_w2):
    f32 = lambda a: np.ascontiguousarray(np.asarray(a), dtype=np.float32)
    x_full = f32(L0_attn_x).reshape(B, SEQ, D)
    ws = {"L0gu": f32(L0_ffn_w_gate_up), "L0dn": f32(L0_ffn_w_down),
          "L1w1": f32(L1_ffn_w1), "L1w2": f32(L1_ffn_w2)}
    wqkv0 = {"w_q": f32(L0_attn_q_w), "w_k": f32(L0_attn_k_w),
             "w_v": f32(L0_attn_v_w), "w_o": f32(L0_attn_o_w)}
    wl1 = {"w1q": f32(L1_attn_wq), "w1k": f32(L1_attn_wk),
           "w1v": f32(L1_attn_wv), "w1o": f32(L1_attn_wo)}

    nc0 = build_p0()
    in_maps0 = []
    for c in range(N_CORES):
        m = {}
        for n in QW_NAMES:
            r = QW_SHAPES[n][0] // 8
            m["w_" + n] = np.ascontiguousarray(ws[n][c * r:(c + 1) * r])
        in_maps0.append(m)
    res0 = run_bass_kernel_spmd(nc0, in_maps0, core_ids=list(range(N_CORES)))
    partials = np.concatenate([res0.results[c]["partials"]
                               for c in range(N_CORES)], 0)

    nc1 = build_p1()
    in_maps1 = []
    for c in range(N_CORES):
        sq = c // 4
        tiles = owned_tiles(c)
        xc = np.concatenate([x_full[sq, t * 128:(t + 1) * 128] for t in tiles], 0)
        im = {"x": np.ascontiguousarray(xc), "partials": partials,
              "w_L0gu": ws["L0gu"], "w_L0dn": ws["L0dn"],
              "w_L1w1": ws["L1w1"], "w_L1w2": ws["L1w2"]}
        im.update(wqkv0)
        im.update(wl1)
        in_maps1.append(im)
    res1 = run_bass_kernel_spmd(nc1, in_maps1, core_ids=list(range(N_CORES)))

    ktf, vf = {}, {}
    for sq in range(B):
        ktfull = np.zeros((D, SEQ), dtype=ml_dtypes.bfloat16)
        vfull = np.zeros((SEQ, D), dtype=ml_dtypes.bfloat16)
        for k in range(4):
            c = sq * 4 + k
            for j in range(NT):
                T = k + 4 * j
                ktfull[:, T * 128:(T + 1) * 128] = \
                    res1.results[c]["kt"][:, j * 128:(j + 1) * 128]
                vfull[T * 128:(T + 1) * 128, :] = \
                    res1.results[c]["v"][j * 128:(j + 1) * 128, :]
        ktf[sq] = ktfull
        vf[sq] = vfull

    nc2 = build_p2()
    in_maps2 = []
    for c in range(N_CORES):
        sq = c // 4
        qc, qs_ = _rope_tables(c, D)
        kc, ks_ = _rope_tables(c, 512, scale=INV_SQRT_DH)
        r1 = res1.results[c]
        in_maps2.append({
            "x1": r1["x1"], "partials": partials, "qt": r1["qt"],
            "ktf": ktf[sq], "vf": vf[sq], "mask": _build_mask(c),
            "gate_w": f32(L0_attn_gate_w),
            "gate_b": f32(L0_attn_gate_b).reshape(1, 128),
            "wt_L0gu": r1["wt_L0gu"], "wt_L0dn": r1["wt_L0dn"],
            "wt_o": r1["wt_o"], "wt_L1w1": r1["wt_L1w1"],
            "wt_L1w2": r1["wt_L1w2"], "wnat_q1": r1["wnat_q1"],
            "wnat_k1": r1["wnat_k1"], "wnat_v1": r1["wnat_v1"],
            "gq_row": f32(L1_attn_gq).reshape(1, D),
            "gk_row": f32(L1_attn_gk).reshape(1, 512),
            "tab_qc": qc, "tab_qs": qs_, "tab_kc": kc, "tab_ks": ks_,
        })
    res2 = run_bass_kernel_spmd(nc2, in_maps2, core_ids=list(range(N_CORES)))

    ktf1, vf1 = {}, {}
    for sq in range(B):
        ktfull = np.zeros((512, SEQ), dtype=ml_dtypes.bfloat16)
        vfull = np.zeros((SEQ, 512), dtype=ml_dtypes.bfloat16)
        for k in range(4):
            c = sq * 4 + k
            for j in range(NT):
                T = k + 4 * j
                ktfull[:, T * 128:(T + 1) * 128] = \
                    res2.results[c]["kt1"][:, j * 128:(j + 1) * 128]
                vfull[T * 128:(T + 1) * 128, :] = \
                    res2.results[c]["v1"][j * 128:(j + 1) * 128, :]
        ktf1[sq] = ktfull
        vf1[sq] = vfull

    nc3 = build_p3()
    in_maps3 = []
    for c in range(N_CORES):
        sq = c // 4
        r1 = res1.results[c]
        r2 = res2.results[c]
        in_maps3.append({
            "x4": r2["x4"], "partials": partials, "qt1": r2["qt1"],
            "ktf1": ktf1[sq], "vf1": vf1[sq], "mask": _build_mask(c),
            "wnat_o1": r1["wnat_o1"], "wt_L1w1": r1["wt_L1w1"],
            "wt_L1w2": r1["wt_L1w2"],
        })
    res3 = run_bass_kernel_spmd(nc3, in_maps3, core_ids=list(range(N_CORES)))

    out = np.zeros((B, SEQ, D), dtype=np.float32)
    for c in range(N_CORES):
        sq = c // 4
        x6 = res3.results[c]["x6"]
        for j, t in enumerate(owned_tiles(c)):
            out[sq, t * 128:(t + 1) * 128] = x6[j * 128:(j + 1) * 128]
    return out.reshape(np.asarray(L0_attn_x).shape)


# version 04
# speedup vs baseline: 1.7866x; 1.7866x over previous
import numpy as np
import ml_dtypes
import concourse.bass as bass
import concourse.mybir as mybir
from concourse.bass_utils import run_bass_kernel_spmd

F32 = mybir.dt.float32
BF16 = mybir.dt.bfloat16
I32 = mybir.dt.int32
AF = mybir.ActivationFunctionType
ALU = mybir.AluOpType
AX = mybir.AxisListType

N_CORES = 8
D = 2048
T_LOC = 1024
NT = T_LOC // 128
SEQ = 4096
B = 2
DH = 128
NH = 16
EPS_LN = 1e-5
EPS_Q = 1e-5
EPS_RMS = 1e-6
INV_SQRT_DH = float(1.0 / np.sqrt(128.0))

F_L0 = 8192
F_L1 = 8192
EL0 = 16384

QW_SHAPES = {
    "L0gu": (EL0, D),
    "L0dn": (D, F_L0),
    "L1w1": (F_L1, D),
    "L1w2": (D, F_L1),
}
QW_NAMES = ["L0gu", "L0dn", "L1w1", "L1w2"]


def owned_tiles(core):
    k = core % 4
    return [k + 4 * j for j in range(NT)]



class Sem:

    def __init__(self, handle):
        self.h = handle
        self.n = 0

    def inc(self, instr, k=1):
        instr.then_inc(self.h, k)
        self.n += k
        return self.n


class Ctx:
    def __init__(self, nc, block):
        self.nc = nc
        self.block = block
        self.sems = {}
        self._stack = None

    def sem(self, name):
        return self.sems[name]


def ceil_div(a, b):
    return (a + b - 1) // b


_UID = [0]


def _u(name):
    _UID[0] += 1
    return f"{name}_{_UID[0]}" 



def build_p0():
    nc = bass.Bass("TRN2", target_bir_lowering=False)
    ins = {}
    for name, (r, c) in QW_SHAPES.items():
        ins[name] = nc.dram_tensor("w_" + name, [r // 8, c], F32, kind="ExternalInput").ap()
    out = nc.dram_tensor("partials", [1, len(QW_NAMES)], F32, kind="ExternalOutput").ap()

    CH = 4096
    with (
        nc.sbuf_tensor("wbuf", [128, 2 * CH], F32) as wbuf,
        nc.sbuf_tensor("acc", [128, len(QW_NAMES)], F32) as acc,
        nc.sbuf_tensor("accs", [128, 1], F32) as accs,
        nc.sbuf_tensor("ones", [128, 1], F32) as ones,
        nc.sbuf_tensor("red", [128, 2], F32) as red,
        nc.sbuf_tensor("outt", [1, len(QW_NAMES)], F32) as outt,
        nc.psum_tensor("ps1", [1, len(QW_NAMES)], F32) as ps1,
        nc.semaphore("ld0") as ld0_h,
        nc.semaphore("ld1") as ld1_h,
        nc.semaphore("cm") as cm_h,
        nc.semaphore("ch") as ch_h,
        nc.semaphore("st") as st_h,
        nc.Block() as block,
    ):
        ld = [Sem(ld0_h), Sem(ld1_h)]
        cm = Sem(cm_h)
        ch = Sem(ch_h)
        st = Sem(st_h)

        loads = []
        for wi, name in enumerate(QW_NAMES):
            r, c = QW_SHAPES[name]
            r //= 8
            flat = ins[name].rearrange("r c -> (r c)")
            total = r * c
            assert total % 128 == 0
            per_row = total // 128
            view = flat.rearrange("(p f) -> p f", p=128)
            for off in range(0, per_row, CH):
                w = min(CH, per_row - off)
                loads.append((wi, view[:, off:off + w], w))

        @block.sync
        def _(sync):
            for i, (wi, ap, w) in enumerate(loads):
                buf = wbuf[:, (i % 2) * CH:(i % 2) * CH + w]
                if i >= 2:
                    sync.wait_ge(cm.h, i - 1)
                ld[i % 2].inc(sync.dma_start(out=buf, in_=ap), 16)

        @block.vector
        def _(vector):
            ch.inc(vector.memset(acc[:], 0.0), 1)
            vector.memset(ones[:], 1.0)
            for i, (wi, ap, w) in enumerate(loads):
                buf = wbuf[:, (i % 2) * CH:(i % 2) * CH + w]
                vector.wait_ge(ld[i % 2].h, 16 * (i // 2 + 1))
                rr = red[:, (i % 2):(i % 2) + 1]
                ch.inc(vector.tensor_reduce(out=rr, in_=buf, axis=AX.X, op=ALU.add,
                                            apply_absolute_value=True), 1)
                vector.wait_ge(ch.h, ch.n)
                cm.inc(vector.tensor_tensor(out=acc[:, wi:wi + 1], in0=acc[:, wi:wi + 1],
                                            in1=rr, op=ALU.add), 1)
            vector.wait_ge(cm.h, cm.n)
            cm.inc(vector.tensor_copy(accs[:], acc[:, 0:1]), 1)

        @block.tensor
        def _(tensor):
            tensor.wait_ge(cm.h, len(loads) + 1)
            tensor.matmul(ps1[:, :], ones[:, :], acc[:, :], start=True, stop=True).then_inc(cm.h, 1)
        cm.n += 1

        @block.vector
        def _(vector):
            vector.wait_ge(cm.h, len(loads) + 2)
            cm.inc(vector.tensor_copy(outt[:], ps1[:, :]), 1)

        @block.sync
        def _(sync):
            sync.wait_ge(cm.h, len(loads) + 3)
            st.inc(sync.dma_start(out=out[:, :], in_=outt[:]), 16)
            sync.wait_ge(st.h, 16)
    return nc




def emit_scales(nc, block, sems, partials_ap, consts, ones_col, ones_row, counts):
    ld, ch = sems["ld0"], sems["ch_v"]
    mm = sems["mm"]
    NW = len(QW_NAMES)

    with (
        nc.sbuf_tensor(_u("sc_part"), [8, NW], F32) as part,
        nc.sbuf_tensor(_u("sc_row"), [1, NW], F32) as row,
        nc.sbuf_tensor(_u("sc_cnt"), [1, NW], F32) as cnt,
        nc.psum_tensor(_u("sc_ps"), [1, NW], F32) as ps,
        nc.psum_tensor(_u("sc_psb"), [128, NW], F32) as psb,
    ):
        @block.sync
        def _(sync):
            ld.inc(sync.dma_start(out=part[:, :], in_=partials_ap), 16)

        @block.vector
        def _(vector):
            for wi, name in enumerate(QW_NAMES):
                ch.inc(vector.memset(cnt[:, wi:wi + 1], 1.0 / counts[wi]), 1)
            ch.inc(vector.memset(ones_col[:], 1.0), 1)
            ch.inc(vector.memset(ones_row[:], 1.0), 1)

        @block.tensor
        def _(tensor):
            tensor.wait_ge(ld.h, ld.n)
            tensor.wait_ge(ch.h, ch.n)
            mm.inc(tensor.matmul(ps[:, :], ones_col[0:8, :], part[:, :],
                                 start=True, stop=True), 1)

        @block.vector
        def _(vector):
            vector.wait_ge(mm.h, mm.n)
            ch.inc(vector.tensor_tensor(out=row[:, :], in0=ps[:, :], in1=cnt[:, :],
                                        op=ALU.mult), 1)
            vector.wait_ge(ch.h, ch.n)
            ch.inc(vector.tensor_scalar(out=row[:, :], in0=row[:, :], scalar1=EPS_Q,
                                        scalar2=None, op0=ALU.max), 1)

        @block.tensor
        def _(tensor):
            tensor.wait_ge(ch.h, ch.n)
            mm.inc(tensor.matmul(psb[:, :], ones_row[:, 0:128], row[:, :],
                                 start=True, stop=True), 1)

        @block.vector
        def _(vector):
            vector.wait_ge(mm.h, mm.n)
            for wi in range(NW):
                src = psb[:, wi:wi + 1]
                ch.inc(vector.tensor_scalar(out=consts[:, 4 * wi:4 * wi + 1], in0=src,
                                            scalar1=0.5, scalar2=None, op0=ALU.mult), 1)
                ch.inc(vector.tensor_scalar(out=consts[:, 4 * wi + 1:4 * wi + 2], in0=src,
                                            scalar1=-0.5, scalar2=None, op0=ALU.mult), 1)
                ch.inc(vector.tensor_scalar(out=consts[:, 4 * wi + 2:4 * wi + 3], in0=src,
                                            scalar1=1.0 / 254.0, scalar2=None, op0=ALU.mult), 1)
                ch.inc(vector.tensor_copy(consts[:, 4 * wi + 3:4 * wi + 4], src), 1)


def emit_weight_prep(nc, block, sems, w_in, wnat_dram, wt_out, R, C, ternary,
                     consts, wi, bufs, flat_rows, passA_cols, wnat_geo,
                     do_transpose=True):
    ld = [sems["wp_ld0"], sems["wp_ld1"]]
    sg = sems["wp_sg"]
    cv = [sems["wp_cv0"], sems["wp_cv1"]]
    stg = [sems["wp_st0"], sems["wp_st1"]]
    tt = [sems["wp_tt0"], sems["wp_tt1"]]
    wo = [sems["wp_wo0"], sems["wp_wo1"]]
    wpf, sg1, sg2, wpb, wtt = (bufs["wpf"], bufs["sg1"], bufs["sg2"],
                               bufs["wpb"], bufs["wtt"])
    CW = passA_cols
    nrb = flat_rows // 128
    nct = C // 128
    wtt_cols = wtt.shape[1]
    dbl_tt = (2 * R <= wtt_cols)

    ldb = [ld[0].n, ld[1].n]
    sgb = sg.n
    cvb = [cv[0].n, cv[1].n]
    stb = [stg[0].n, stg[1].n]
    ttb = [tt[0].n, tt[1].n]
    wob = [wo[0].n, wo[1].n]

    @block.sync
    def _(sync):
        for rb in range(nrb):
            p = rb % 2
            if rb >= 2:
                sync.wait_ge(cv[p].h, cvb[p] + (rb // 2))
            sync.dma_start(out=wpf[:, p * CW:p * CW + CW],
                           in_=w_in[rb * 128:(rb + 1) * 128, :]).then_inc(ld[p].h, 16)
    ld[0].n += 16 * ((nrb + 1) // 2)
    ld[1].n += 16 * (nrb // 2)

    if ternary:
        @block.scalar
        def _(scalar):
            for rb in range(nrb):
                p = rb % 2
                scalar.wait_ge(ld[p].h, ldb[p] + 16 * (rb // 2 + 1))
                if rb >= 1:
                    scalar.wait_ge(cv[0].h if (rb - 1) % 2 == 0 else cv[1].h,
                                   (cvb[0] + (rb - 1) // 2 + 1) if (rb - 1) % 2 == 0
                                   else (cvb[1] + (rb - 1) // 2 + 1))
                src = wpf[:, p * CW:p * CW + CW]
                scalar.activation(out=sg1[:, :], in_=src, func=AF.Sign,
                                  bias=consts[:, 4 * wi:4 * wi + 1],
                                  scale=consts[:, CC["neg1"]:CC["neg1"] + 1])
                scalar.activation(out=sg2[:, :], in_=src, func=AF.Sign,
                                  bias=consts[:, 4 * wi + 1:4 * wi + 2],
                                  scale=consts[:, CC["neg1"]:CC["neg1"] + 1]
                                  ).then_inc(sg.h, 1)
        sg.n += nrb

        @block.vector
        def _(vector):
            for rb in range(nrb):
                p = rb % 2
                vector.wait_ge(sg.h, sgb + rb + 1)
                if rb >= 2:
                    vector.wait_ge(stg[p].h, stb[p] + 16 * (rb // 2))
                vector.tensor_tensor(out=wpb[:, p * CW:p * CW + CW],
                                     in0=sg1[:, :], in1=sg2[:, :],
                                     op=ALU.add).then_inc(cv[p].h, 1)
    else:
        @block.vector
        def _(vector):
            for rb in range(nrb):
                p = rb % 2
                vector.wait_ge(ld[p].h, ldb[p] + 16 * (rb // 2 + 1))
                if rb >= 2:
                    vector.wait_ge(stg[p].h, stb[p] + 16 * (rb // 2))
                vector.tensor_copy(wpb[:, p * CW:p * CW + CW],
                                   wpf[:, p * CW:p * CW + CW]).then_inc(cv[p].h, 1)
    cv[0].n += (nrb + 1) // 2
    cv[1].n += nrb // 2

    @block.gpsimd
    def _(gpsimd):
        for rb in range(nrb):
            p = rb % 2
            gpsimd.wait_ge(cv[p].h, cvb[p] + (rb // 2 + 1))
            gpsimd.dma_start(out=wnat_dram[rb * 128:(rb + 1) * 128, :],
                             in_=wpb[:, p * CW:p * CW + CW]).then_inc(stg[p].h, 16)
    stg[0].n += 16 * ((nrb + 1) // 2)
    stg[1].n += 16 * (nrb // 2)

    if not do_transpose:
        return

    @block.sync
    def _(sync):
        sync.wait_ge(stg[0].h, stg[0].n)
        sync.wait_ge(stg[1].h, stg[1].n)
        for ct in range(nct):
            sync.wait_ge(wo[0].h, wob[0] + 16 * ct)
            sync.dma_start_transpose(out=wtt[:, 0:R],
                                     in_=wnat_geo[:, ct * 128:(ct + 1) * 128]
                                     ).then_inc(tt[0].h, 16)
    tt[0].n += 16 * nct

    @block.gpsimd
    def _(gpsimd):
        for ct in range(nct):
            gpsimd.wait_ge(tt[0].h, ttb[0] + 16 * (ct + 1))
            gpsimd.dma_start(out=wt_out[ct * 128:(ct + 1) * 128, :],
                             in_=wtt[:, 0:R]).then_inc(wo[0].h, 16)
    wo[0].n += 16 * nct


def emit_lnq_tiles(nc, block, sems, xs, k1nat, consts, tc, sqs, ybuf, kibuf,
                   dbuf, t1buf, nkbuf, ready, quant, sc_out, sc_wcol, xn_nat):
    ch_v, ch_s, kn = sems["ch_v"], sems["ch_s"], sems["kn"]
    rlist = ready if isinstance(ready, list) else [ready]
    invD = 1.0 / D

    for t in range(NT):
        xt = xs[:, t * D:(t + 1) * D]

        @block.vector
        def _(vector, t=t, xt=xt):
            for rsem, rcnt in rlist:
                vector.wait_ge(rsem.h, rcnt)
            ch_v.inc(vector.tensor_reduce(out=tc[:, 0:1], in_=xt, axis=AX.X,
                                          op=ALU.add), 1)
            ch_v.inc(vector.tensor_reduce(out=tc[:, 3:4], in_=xt, axis=AX.X,
                                          op=ALU.max), 1)
            ch_v.inc(vector.tensor_reduce(out=tc[:, 4:5], in_=xt, axis=AX.X,
                                          op=ALU.min), 1)
            vector.wait_ge(ch_v.h, ch_v.n)
            ch_v.inc(vector.tensor_scalar(out=tc[:, 0:1], in0=tc[:, 0:1],
                                          scalar1=invD, scalar2=None,
                                          op0=ALU.mult), 1)
            ch_v.inc(vector.memset(tc[:, 1:2], 0.0), 1)

        @block.scalar
        def _(scalar, t=t, xt=xt):
            for rsem, rcnt in rlist:
                scalar.wait_ge(rsem.h, rcnt)
            scalar.wait_ge(ch_v.h, ch_v.n)
            ch_s.inc(scalar.activation(out=sqs[:, :], in_=xt, func=AF.Square,
                                       accum_out=tc[:, 1:2]), 1)

        @block.vector
        def _(vector, t=t):
            vector.wait_ge(ch_s.h, ch_s.n)
            vector.wait_ge(ch_v.h, ch_v.n)
            ch_v.inc(vector.tensor_tensor(out=tc[:, 5:6], in0=tc[:, 0:1],
                                          in1=tc[:, 0:1], op=ALU.mult), 1)
            ch_v.inc(vector.tensor_scalar(out=tc[:, 1:2], in0=tc[:, 1:2],
                                          scalar1=invD, scalar2=None,
                                          op0=ALU.mult), 1)
            vector.wait_ge(ch_v.h, ch_v.n)
            ch_v.inc(vector.tensor_tensor(out=tc[:, 1:2], in0=tc[:, 1:2],
                                          in1=tc[:, 5:6], op=ALU.subtract), 1)

        @block.scalar
        def _(scalar, t=t):
            scalar.wait_ge(ch_v.h, ch_v.n)
            ch_s.inc(scalar.activation(out=tc[:, 5:6], in_=tc[:, 1:2],
                                       func=AF.Sqrt,
                                       bias=consts[:, CC["epsln"]:CC["epsln"] + 1]), 1)

        @block.vector
        def _(vector, t=t, xt=xt):
            vector.wait_ge(ch_s.h, ch_s.n)
            ch_v.inc(vector.reciprocal(tc[:, 2:3], tc[:, 5:6]), 1)
            vector.wait_ge(ch_v.h, ch_v.n)
            if quant:
                ch_v.inc(vector.tensor_tensor(out=tc[:, 3:4], in0=tc[:, 3:4],
                                              in1=tc[:, 0:1], op=ALU.subtract), 1)
                ch_v.inc(vector.tensor_tensor(out=tc[:, 4:5], in0=tc[:, 0:1],
                                              in1=tc[:, 4:5], op=ALU.subtract), 1)
                vector.wait_ge(ch_v.h, ch_v.n)
                ch_v.inc(vector.tensor_tensor(out=tc[:, 3:4], in0=tc[:, 3:4],
                                              in1=tc[:, 4:5], op=ALU.max), 1)
                vector.wait_ge(ch_v.h, ch_v.n)
                ch_v.inc(vector.tensor_tensor(out=tc[:, 3:4], in0=tc[:, 3:4],
                                              in1=tc[:, 2:3], op=ALU.mult), 1)
                vector.wait_ge(ch_v.h, ch_v.n)
                ch_v.inc(vector.tensor_scalar(out=tc[:, 3:4], in0=tc[:, 3:4],
                                              scalar1=EPS_Q, scalar2=None,
                                              op0=ALU.max), 1)
                vector.wait_ge(ch_v.h, ch_v.n)
                ch_v.inc(vector.tensor_tensor(out=sc_out[:, t:t + 1],
                                              in0=tc[:, 3:4],
                                              in1=consts[:, sc_wcol:sc_wcol + 1],
                                              op=ALU.mult), 1)
                ch_v.inc(vector.reciprocal(tc[:, 6:7], tc[:, 3:4]), 1)
                vector.wait_ge(ch_v.h, ch_v.n)
                ch_v.inc(vector.tensor_scalar(out=tc[:, 6:7], in0=tc[:, 6:7],
                                              scalar1=127.0, scalar2=None,
                                              op0=ALU.mult), 1)
                vector.wait_ge(ch_v.h, ch_v.n)
                ch_v.inc(vector.tensor_tensor(out=tc[:, 7:8], in0=tc[:, 2:3],
                                              in1=tc[:, 6:7], op=ALU.mult), 1)
                vector.wait_ge(ch_v.h, ch_v.n)
                ch_v.inc(vector.scalar_tensor_tensor(out=tc[:, 8:9],
                                                     in0=tc[:, 0:1], scalar=-1.0,
                                                     in1=tc[:, 7:8],
                                                     op0=ALU.mult, op1=ALU.mult), 1)
            else:
                ch_v.inc(vector.tensor_copy(tc[:, 7:8], tc[:, 2:3]), 1)
                ch_v.inc(vector.scalar_tensor_tensor(out=tc[:, 8:9],
                                                     in0=tc[:, 0:1], scalar=-1.0,
                                                     in1=tc[:, 2:3],
                                                     op0=ALU.mult, op1=ALU.mult), 1)

        @block.scalar
        def _(scalar, t=t, xt=xt):
            scalar.wait_ge(ch_v.h, ch_v.n)
            out = ybuf[:, :] if quant else nkbuf[:, :]
            ch_s.inc(scalar.activation(out=out, in_=xt, func=AF.Identity,
                                       scale=tc[:, 7:8], bias=tc[:, 8:9]), 1)

        if quant:
            @block.vector
            def _(vector, t=t):
                vector.wait_ge(ch_s.h, ch_s.n)
                if t >= 1:
                    vector.wait_ge(kn.h, kn.n)
                ch_v.inc(vector.tensor_copy(kibuf[:, :], ybuf[:, :]), 1)
                vector.wait_ge(ch_v.h, ch_v.n)
                ch_v.inc(vector.tensor_tensor(out=dbuf[:, :], in0=ybuf[:, :],
                                              in1=kibuf[:, :], op=ALU.subtract), 1)
                vector.wait_ge(ch_v.h, ch_v.n)
                ch_v.inc(vector.scalar_tensor_tensor(out=t1buf[:, :], in0=dbuf[:, :],
                                                     scalar=0.5, in1=kibuf[:, :],
                                                     op0=ALU.is_ge, op1=ALU.add), 1)
                vector.wait_ge(ch_v.h, ch_v.n)
                ch_v.inc(vector.scalar_tensor_tensor(out=nkbuf[:, :], in0=dbuf[:, :],
                                                     scalar=-0.5, in1=t1buf[:, :],
                                                     op0=ALU.is_le, op1=ALU.subtract), 1)

            @block.gpsimd
            def _(gpsimd, t=t):
                gpsimd.wait_ge(ch_v.h, ch_v.n)
                kn.inc(gpsimd.dma_start(out=k1nat[t * 128:(t + 1) * 128, :],
                                        in_=nkbuf[:, :]), 16)
        else:
            @block.gpsimd
            def _(gpsimd, t=t):
                gpsimd.wait_ge(ch_s.h, ch_s.n)
                kn.inc(gpsimd.dma_start(out=xn_nat[t * 128:(t + 1) * 128, :],
                                        in_=nkbuf[:, :]), 16)


def emit_ffn_up_glu(nc, block, sems, k1T, wt_up, g_dram, scU, m2, Fh, glu,
                    silu, bufs):
    wl = [sems["wl0"], sems["wl1"]]
    mm, ue_s, ue_v, gst = sems["mm"], sems["ue_s"], sems["ue_v"], sems["gst"]
    ch_v = sems["ch_v"]
    nch = Fh // 512
    KT = 16
    wlb = [wl[0].n, wl[1].n]
    mmb = mm.n
    usb, uvb = ue_s.n, ue_v.n
    gsb = gst.n
    wua, wub, ga, gb, rmax = (bufs["wua"], bufs["wub"], bufs["ga"], bufs["gb"],
                              bufs["rmax"])

    with (
        nc.psum_tensor(_u("ps_upA"), [128, 2 * 512], F32) as psA,
        nc.psum_tensor(_u("ps_upB"), [128, 2 * 512], F32) as psB,
    ):
        @block.vector
        def _(vector):
            ch_v.inc(vector.memset(m2[:, :], 0.0), 1)
        m2_init = ch_v.n

        per_c = 32 if glu else 16

        @block.sync
        def _(sync):
            sync.wait_ge(sems["tt0"].h, sems["tt0"].n)
            for c in range(nch):
                p = c % 2
                if c >= 2:
                    sync.wait_ge(ue_v.h, uvb + (c - 1) * NT)
                ap_a = wt_up[:, c * 512:(c + 1) * 512].rearrange(
                    "(dt q) f -> q dt f", q=128)
                sync.dma_start(out=wua[:, p * KT * 512:(p + 1) * KT * 512]
                               .rearrange("q (dt f) -> q dt f", f=512),
                               in_=ap_a).then_inc(wl[p].h, 16)
                if glu:
                    ap_b = wt_up[:, Fh + c * 512:Fh + (c + 1) * 512].rearrange(
                        "(dt q) f -> q dt f", q=128)
                    sync.dma_start(out=wub[:, p * KT * 512:(p + 1) * KT * 512]
                                   .rearrange("q (dt f) -> q dt f", f=512),
                                   in_=ap_b).then_inc(wl[p].h, 16)
        wl[0].n += per_c * ((nch + 1) // 2)
        wl[1].n += per_c * (nch // 2)

        @block.tensor
        def _(tensor):
            tensor.wait_ge(sems["ch_v"].h, m2_init)
            for c in range(nch):
                p = c % 2
                tensor.wait_ge(wl[p].h, wlb[p] + per_c * (c // 2 + 1))
                for t in range(NT):
                    q = t % 2
                    u = c * NT + t
                    if u >= 2:
                        tensor.wait_ge(ue_s.h, usb + u - 1)
                        tensor.wait_ge(ue_v.h, uvb + u - 1)
                    last = None
                    for dt in range(KT):
                        stat = k1T[:, dt * T_LOC + t * 128:dt * T_LOC + (t + 1) * 128]
                        mva = wua[:, p * KT * 512 + dt * 512:p * KT * 512 + (dt + 1) * 512]
                        last = tensor.matmul(psA[:, q * 512:(q + 1) * 512], stat,
                                             mva, start=(dt == 0), stop=(dt == KT - 1))
                    if glu:
                        for dt in range(KT):
                            stat = k1T[:, dt * T_LOC + t * 128:dt * T_LOC + (t + 1) * 128]
                            mvb = wub[:, p * KT * 512 + dt * 512:p * KT * 512 + (dt + 1) * 512]
                            last = tensor.matmul(psB[:, q * 512:(q + 1) * 512], stat,
                                                 mvb, start=(dt == 0), stop=(dt == KT - 1))
                    mm.inc(last, 1)

        @block.scalar
        def _(scalar):
            for c in range(nch):
                for t in range(NT):
                    q = t % 2
                    u = c * NT + t
                    scalar.wait_ge(mm.h, mmb + u + 1)
                    if u >= 2:
                        scalar.wait_ge(ue_v.h, uvb + u - 1)
                    fn = AF.Silu if silu else AF.Gelu
                    scalar.activation(out=ga[:, q * 512:(q + 1) * 512],
                                      in_=psA[:, q * 512:(q + 1) * 512],
                                      func=fn,
                                      scale=scU[:, t:t + 1]).then_inc(ue_s.h, 1)
        ue_s.n += nch * NT

        @block.vector
        def _(vector):
            vector.wait_ge(ch_v.h, m2_init)
            for c in range(nch):
                for t in range(NT):
                    q = t % 2
                    u = c * NT + t
                    vector.wait_ge(ue_s.h, usb + u + 1)
                    if u >= 2:
                        vector.wait_ge(gst.h, gsb + 16 * (u - 1))
                    gbq = gb[:, q * 512:(q + 1) * 512]
                    if glu:
                        vector.tensor_scalar(out=gbq,
                                             in0=psB[:, q * 512:(q + 1) * 512],
                                             scalar1=scU[:, t:t + 1], scalar2=None,
                                             op0=ALU.mult).then_inc(ch_v.h, 1)
                        ch_v.n += 1
                        vector.wait_ge(ch_v.h, ch_v.n)
                        vector.tensor_tensor(out=gbq,
                                             in0=ga[:, q * 512:(q + 1) * 512],
                                             in1=gbq,
                                             op=ALU.mult).then_inc(ch_v.h, 1)
                        ch_v.n += 1
                    else:
                        vector.tensor_copy(gbq, ga[:, q * 512:(q + 1) * 512]
                                           ).then_inc(ch_v.h, 1)
                        ch_v.n += 1
                    vector.wait_ge(ch_v.h, ch_v.n)
                    vector.tensor_reduce(out=rmax[:, q:q + 1], in_=gbq,
                                         axis=AX.X, op=ALU.max,
                                         apply_absolute_value=True).then_inc(ch_v.h, 1)
                    ch_v.n += 1
                    vector.wait_ge(ch_v.h, ch_v.n)
                    vector.tensor_tensor(out=m2[:, t:t + 1], in0=m2[:, t:t + 1],
                                         in1=rmax[:, q:q + 1],
                                         op=ALU.max).then_inc(ue_v.h, 1)
        ue_v.n += nch * NT

        @block.gpsimd
        def _(gpsimd):
            for c in range(nch):
                for t in range(NT):
                    q = t % 2
                    u = c * NT + t
                    gpsimd.wait_ge(ue_v.h, uvb + u + 1)
                    gpsimd.dma_start(out=g_dram[t * 128:(t + 1) * 128,
                                                c * 512:(c + 1) * 512],
                                     in_=gb[:, q * 512:(q + 1) * 512]
                                     ).then_inc(gst.h, 16)
        gst.n += 16 * nch * NT


def emit_quant_g(nc, block, sems, g_dram, k2nat, m2, s2c, sc2, consts, grl,
                 y2, ki2, d2, t12, nk2, dn_wi, Fh, extra_half):
    ch_v, ch_s, kn = sems["ch_v"], sems["ch_s"], sems["kn"]
    gr = [sems["gr0"], sems["gr1"]]
    gst = sems["gst"]

    @block.vector
    def _(vector):
        vector.wait_ge(sems["ue_v"].h, sems["ue_v"].n)
        ch_v.inc(vector.tensor_scalar(out=m2[:, :], in0=m2[:, :], scalar1=EPS_Q,
                                      scalar2=None, op0=ALU.max), 1)
        vector.wait_ge(ch_v.h, ch_v.n)
        ch_v.inc(vector.reciprocal(s2c[:, :], m2[:, :]), 1)
        vector.wait_ge(ch_v.h, ch_v.n)
        ch_v.inc(vector.tensor_scalar(out=s2c[:, :], in0=s2c[:, :], scalar1=127.0,
                                      scalar2=None, op0=ALU.mult), 1)
        hf = 0.5 if extra_half else 1.0
        ch_v.inc(vector.tensor_scalar(out=sc2[:, :], in0=m2[:, :], scalar1=hf,
                                      scalar2=None, op0=ALU.mult), 1)
        vector.wait_ge(ch_v.h, ch_v.n)
        for t in range(NT):
            ch_v.inc(vector.tensor_tensor(out=sc2[:, t:t + 1], in0=sc2[:, t:t + 1],
                                          in1=consts[:, 4 * dn_wi + 2:4 * dn_wi + 3],
                                          op=ALU.mult), 1)

    grb = [gr[0].n, gr[1].n]
    knb = kn.n
    nsec = Fh // 1024
    units = [(t, sec) for t in range(NT) for sec in range(nsec)]

    @block.sync
    def _(sync):
        sync.wait_ge(gst.h, gst.n)
        for i, (t, sec) in enumerate(units):
            p = i % 2
            if i >= 2:
                sync.wait_ge(kn.h, knb + 16 * (i - 1))
            sync.dma_start(out=grl[:, p * 1024:(p + 1) * 1024],
                           in_=g_dram[t * 128:(t + 1) * 128,
                                      sec * 1024:(sec + 1) * 1024]
                           ).then_inc(gr[p].h, 16)
    gr[0].n += 16 * ((len(units) + 1) // 2)
    gr[1].n += 16 * (len(units) // 2)

    for i, (t, sec) in enumerate(units):
        p = i % 2

        @block.scalar
        def _(scalar, i=i, t=t, p=p):
            scalar.wait_ge(gr[p].h, grb[p] + 16 * (i // 2 + 1))
            scalar.wait_ge(ch_v.h, ch_v.n)
            scalar.activation(out=y2[:, :], in_=grl[:, p * 1024:(p + 1) * 1024],
                              func=AF.Copy, scale=s2c[:, t:t + 1]).then_inc(ch_s.h, 1)
        ch_s.n += 1

        @block.vector
        def _(vector, i=i):
            vector.wait_ge(ch_s.h, ch_s.n)
            if i >= 1:
                vector.wait_ge(kn.h, knb + 16 * i)
            ch_v.inc(vector.tensor_copy(ki2[:, :], y2[:, :]), 1)
            vector.wait_ge(ch_v.h, ch_v.n)
            ch_v.inc(vector.tensor_tensor(out=d2[:, :], in0=y2[:, :], in1=ki2[:, :],
                                          op=ALU.subtract), 1)
            vector.wait_ge(ch_v.h, ch_v.n)
            ch_v.inc(vector.scalar_tensor_tensor(out=t12[:, :], in0=d2[:, :],
                                                 scalar=0.5, in1=ki2[:, :],
                                                 op0=ALU.is_ge, op1=ALU.add), 1)
            vector.wait_ge(ch_v.h, ch_v.n)
            ch_v.inc(vector.scalar_tensor_tensor(out=nk2[:, :], in0=d2[:, :],
                                                 scalar=-0.5, in1=t12[:, :],
                                                 op0=ALU.is_le, op1=ALU.subtract), 1)

        @block.gpsimd
        def _(gpsimd, i=i, t=t, sec=sec):
            gpsimd.wait_ge(ch_v.h, ch_v.n)
            kn.inc(gpsimd.dma_start(out=k2nat[t * 128:(t + 1) * 128,
                                              sec * 1024:(sec + 1) * 1024],
                                    in_=nk2[:, :]), 16)


def emit_ffn_down(nc, block, sems, k2nat, wt_dn, xs, sc2, k2T, wdn, Fh,
                  Eo=D):
    k2t = [sems["k2t0"], sems["k2t1"]]
    wd = [sems["wd0"], sems["wd1"]]
    mm, dn_ev, kn = sems["mm"], sems["dn_ev"], sems["kn"]
    ch_v = sems["ch_v"]
    nsec = Fh // 1024
    nec = Eo // 512
    k2b = [k2t[0].n, k2t[1].n]
    wdb = [wd[0].n, wd[1].n]
    mmb = mm.n
    evb = dn_ev.n
    NU = nsec * 4

    with nc.psum_tensor(_u("ps_dn"), [128, 8 * 512], F32) as psD:
        @block.sync
        def _(sync):
            sync.wait_ge(kn.h, kn.n)
            for sec in range(nsec):
                p = sec % 2
                if sec >= 2:
                    sync.wait_ge(dn_ev.h, evb + 8 * 4 * (sec - 1))
                for ft in range(8):
                    sync.dma_start_transpose(
                        out=k2T[:, p * Fh + ft * T_LOC:p * Fh + (ft + 1) * T_LOC],
                        in_=k2nat[:, (sec * 8 + ft) * 128:(sec * 8 + ft + 1) * 128]
                    ).then_inc(k2t[p].h, 16)
        k2t[0].n += 16 * 8 * ((nsec + 1) // 2)
        k2t[1].n += 16 * 8 * (nsec // 2)

        @block.gpsimd
        def _(gpsimd):
            gpsimd.wait_ge(kn.h, kn.n)
            for sec in range(nsec):
                if sec >= 1:
                    gpsimd.wait_ge(mm.h, mmb + 32 * sec)
                rows = wt_dn[sec * 1024:(sec + 1) * 1024, :]
                gpsimd.dma_start(out=wdn[:, :].rearrange("q (ft e) -> q ft e", e=Eo),
                                 in_=rows.rearrange("(ft q) e -> q ft e", q=128)
                                 ).then_inc(wd[0].h, 16)
        wd[0].n += 16 * nsec

        @block.tensor
        def _(tensor):
            gu = 0
            for sec in range(nsec):
                p = sec % 2
                tensor.wait_ge(k2t[p].h, k2b[p] + 16 * 8 * (sec // 2 + 1))
                tensor.wait_ge(wd[0].h, wdb[0] + 16 * (sec + 1))
                for tg in range(4):
                    un = sec * 4 + tg
                    if un >= 2:
                        tensor.wait_ge(dn_ev.h, evb + 8 * (un - 1))
                    for t2 in range(2):
                        t = tg * 2 + t2
                        for ec in range(nec):
                            bank = (tg % 2) * 4 + t2 * nec // 1 * 0 + t2 * 2 + ec                                 if nec == 2 else (tg % 2) * 4 + ec
                            if nec == 4:
                                bank = (tg % 2) * 4 + ec
                            last = None
                            for ft in range(8):
                                stat = k2T[:, p * Fh + ft * T_LOC + t * 128:
                                           p * Fh + ft * T_LOC + (t + 1) * 128]
                                mv = wdn[:, ft * Eo + ec * 512:
                                         ft * Eo + (ec + 1) * 512]
                                last = tensor.matmul(
                                    psD[:, bank * 512:(bank + 1) * 512], stat, mv,
                                    start=(ft == 0), stop=(ft == 7))
                            mm.inc(last, 1)
                        if nec == 4 and t2 == 0:
                            tensor.wait_ge(dn_ev.h, evb + un * 8 + 4)

        @block.vector
        def _(vector):
            g = 0
            for sec in range(nsec):
                for tg in range(4):
                    for t2 in range(2):
                        t = tg * 2 + t2
                        for ec in range(nec):
                            bank = (tg % 2) * 4 + (t2 * 2 + ec if nec == 2 else ec)
                            g += 1
                            vector.wait_ge(mm.h, mmb + g)
                            xv = xs[:, t * Eo + ec * 512:t * Eo + (ec + 1) * 512]
                            vector.scalar_tensor_tensor(
                                out=xv, in0=psD[:, bank * 512:(bank + 1) * 512],
                                scalar=sc2[:, t:t + 1], in1=xv,
                                op0=ALU.mult, op1=ALU.add).then_inc(dn_ev.h, 1)
                            dn_ev.n += 1


def emit_qkv_l0(nc, block, sems, xn2T, wt_q, wt_k, wt_v, qt_out, kt_out,
                v_out, consts, wqkv, pev):
    wl = [sems["wl0"], sems["wl1"]]
    mm, qe = sems["mm"], [sems["qe0"], sems["qe1"]]
    ch_s = sems["ch_s"]
    tt1 = sems["tt1"]
    wlb = [wl[0].n, wl[1].n]
    mmb = mm.n
    qeb = [qe[0].n, qe[1].n]
    KT = 16

    with nc.psum_tensor(_u("ps_qkv"), [128, 4 * 512], F32) as psQ:
        jobs = []
        jobs.append((wt_q, qt_out, None))
        jobs.append((wt_k, kt_out, CC["invsq"]))

        @block.sync
        def _(sync):
            sync.wait_ge(tt1.h, tt1.n)
            for ji, (wt, outd, sc) in enumerate(jobs):
                for et in range(16):
                    i = ji * 16 + et
                    p = i % 2
                    if i >= 2:
                        sync.wait_ge(mm.h, mmb + (i - 1) * 2)
                    ap = wt[:, et * 128:(et + 1) * 128].rearrange(
                        "(dt q) e -> q dt e", q=128)
                    sync.dma_start(out=wqkv[:, p * KT * 512:p * KT * 512 + KT * 128]
                                   .rearrange("q (dt e) -> q dt e", e=128),
                                   in_=ap).then_inc(wl[p].h, 16)
        wl[0].n += 16 * 16
        wl[1].n += 16 * 16
        ch_s.qkv_base = ch_s.n

        @block.tensor
        def _(tensor):
            for ji, (wt, outd, sc) in enumerate(jobs):
                for et in range(16):
                    i = ji * 16 + et
                    p = i % 2
                    tensor.wait_ge(wl[p].h, wlb[p] + 16 * (i // 2 + 1))
                    for tc in range(2):
                        bank = (i % 2) * 2 + tc
                        if i >= 2:
                            tensor.wait_ge(ch_s.h, sems["ch_s"].qkv_base + (i - 2) * 2 + tc + 1)
                        last = None
                        for dt in range(KT):
                            stat = wqkv[:, p * KT * 512 + dt * 128:
                                        p * KT * 512 + (dt + 1) * 128]
                            mv = xn2T[:, dt * T_LOC + tc * 512:dt * T_LOC + (tc + 1) * 512]
                            last = tensor.matmul(psQ[:, bank * 512:(bank + 1) * 512],
                                                 stat, mv, start=(dt == 0),
                                                 stop=(dt == KT - 1))
                        mm.inc(last, 1)

        @block.scalar
        def _(scalar):
            for ji, (wt, outd, sc) in enumerate(jobs):
                for et in range(16):
                    i = ji * 16 + et
                    for tc in range(2):
                        bank = (i % 2) * 2 + tc
                        scalar.wait_ge(mm.h, mmb + i * 2 + tc + 1)
                        ev = pev[:, bank * 512:(bank + 1) * 512]
                        if i >= 2:
                            scalar.wait_ge(qe[i % 2].h, qeb[i % 2] + 16 * ((i // 2 - 1) * 2 + tc + 1))
                        if sc is None:
                            scalar.activation(out=ev, in_=psQ[:, bank * 512:(bank + 1) * 512],
                                              func=AF.Copy,
                                              scale=consts[:, CC["one"]:CC["one"] + 1]
                                              ).then_inc(ch_s.h, 1)
                        else:
                            scalar.activation(out=ev, in_=psQ[:, bank * 512:(bank + 1) * 512],
                                              func=AF.Copy,
                                              scale=consts[:, sc:sc + 1]).then_inc(ch_s.h, 1)
                        ch_s.n += 1

        @block.gpsimd
        def _(gpsimd):
            for ji, (wt, outd, sc) in enumerate(jobs):
                for et in range(16):
                    i = ji * 16 + et
                    for tc in range(2):
                        bank = (i % 2) * 2 + tc
                        gpsimd.wait_ge(ch_s.h, ch_s.qkv_base + i * 2 + tc + 1)
                        gpsimd.dma_start(out=outd[et * 128:(et + 1) * 128,
                                                  tc * 512:(tc + 1) * 512],
                                         in_=pev[:, bank * 512:(bank + 1) * 512]
                                         ).then_inc(qe[i % 2].h, 16)
        qe[0].n += 16 * 32
        qe[1].n += 16 * 32

    with nc.psum_tensor(_u("ps_v"), [128, 4 * 512], F32) as psV:
        vbase_mm = mm.n
        vbase_chs = ch_s.n
        vwlb = [wl[0].n, wl[1].n]
        qe0_final, qe1_final = qe[0].n, qe[1].n
        qe[0].n_v_base = qe[0].n

        @block.sync
        def _(sync):
            sync.wait_ge(mm.h, vbase_mm)
            for ec in range(4):
                p = ec % 2
                if ec >= 2:
                    sync.wait_ge(mm.h, vbase_mm + (ec - 1) * NT)
                ap = wt_v[:, ec * 512:(ec + 1) * 512].rearrange(
                    "(dt q) e -> q dt e", q=128)
                sync.dma_start(out=wqkv[:, p * KT * 512:(p + 1) * KT * 512]
                               .rearrange("q (dt e) -> q dt e", e=512),
                               in_=ap).then_inc(wl[p].h, 16)
        wl[0].n += 16 * 2
        wl[1].n += 16 * 2

        @block.tensor
        def _(tensor):
            for ec in range(4):
                p = ec % 2
                tensor.wait_ge(wl[p].h, vwlb[p] + 16 * (ec // 2 + 1))
                for t in range(NT):
                    u = ec * NT + t
                    bank = u % 4
                    if u >= 4:
                        tensor.wait_ge(ch_s.h, vbase_chs + u - 3)
                    last = None
                    for dt in range(KT):
                        stat = xn2T[:, dt * T_LOC + t * 128:dt * T_LOC + (t + 1) * 128]
                        mv = wqkv[:, p * KT * 512 + dt * 512:p * KT * 512 + (dt + 1) * 512]
                        last = tensor.matmul(psV[:, bank * 512:(bank + 1) * 512],
                                             stat, mv, start=(dt == 0),
                                             stop=(dt == KT - 1))
                    mm.inc(last, 1)

        @block.scalar
        def _(scalar):
            for ec in range(4):
                for t in range(NT):
                    u = ec * NT + t
                    bank = u % 4
                    scalar.wait_ge(mm.h, vbase_mm + u + 1)
                    ev = pev[:, bank * 512:(bank + 1) * 512]
                    if u < 4:
                        scalar.wait_ge(qe[0].h, qe0_final)
                        scalar.wait_ge(qe[1].h, qe1_final)
                    else:
                        scalar.wait_ge(qe[0].h, qe[0].n_v_base + 16 * (u - 3))
                    scalar.activation(out=ev, in_=psV[:, bank * 512:(bank + 1) * 512],
                                      func=AF.Copy,
                                      scale=consts[:, CC["one"]:CC["one"] + 1]
                                      ).then_inc(ch_s.h, 1)
                    ch_s.n += 1

        @block.gpsimd
        def _(gpsimd):
            for ec in range(4):
                for t in range(NT):
                    u = ec * NT + t
                    bank = u % 4
                    gpsimd.wait_ge(ch_s.h, vbase_chs + u + 1)
                    gpsimd.dma_start(out=v_out[t * 128:(t + 1) * 128,
                                               ec * 512:(ec + 1) * 512],
                                     in_=pev[:, bank * 512:(bank + 1) * 512]
                                     ).then_inc(qe[0].h, 16)
        qe[0].n += 16 * 32



class Rec:
    NROT = 4

    def __init__(self, nc, block, sems, tag):
        self.nc = nc
        self.block = block
        self.ops = {e: [] for e in ("tensor", "scalar", "vector", "pool",
                                    "syncd", "gpsd")}
        self.tag = tag
        self.chain = {
            "tensor": sems["mm"], "scalar": sems["ch_s"],
            "vector": sems["ch_v"], "pool": sems["ch_p"],
        }
        self.dma = {
            "syncd": [sems["sd0"], sems["sd1"], sems["sd2"], sems["sd3"]],
            "gpsd": [sems["gd0"], sems["gd1"], sems["gd2"], sems["gd3"]],
        }

    def op(self, eng, fn, waits=()):
        lst = self.ops[eng]
        tok = (eng, len(lst))
        lst.append((fn, tuple(w for w in waits if w is not None)))
        return tok

    def _tok_sem_val(self, tok, bases):
        if tok[0] == "raw":
            return (tok[1], tok[2])
        eng, idx = tok
        if eng in self.chain:
            return (self.chain[eng].h, bases[eng] + idx + 1)
        sems = self.dma[eng]
        r = idx % self.NROT
        return (sems[r].h, bases[eng][r] + 16 * (idx // self.NROT + 1))

    def emit(self):
        bases = {}
        for e, s_ in self.chain.items():
            bases[e] = s_.n
        for e, lst in self.dma.items():
            bases[e] = [x.n for x in lst]

        engmap = {"tensor": self.block.tensor, "scalar": self.block.scalar,
                  "vector": self.block.vector, "pool": self.block.gpsimd,
                  "syncd": self.block.sync, "gpsd": self.block.gpsimd}

        for eng in ("syncd", "gpsd", "tensor", "scalar", "vector", "pool"):
            lst = self.ops[eng]
            if not lst:
                continue

            def body(E, eng=eng, lst=lst):
                hw = {}
                for i, (fn, waits) in enumerate(lst):
                    for tok in waits:
                        h, v = self._tok_sem_val(tok, bases)
                        if hw.get(id(h), -1) < v:
                            E.wait_ge(h, v)
                            hw[id(h)] = v
                    ins = fn(E)
                    if eng in self.chain:
                        ins.then_inc(self.chain[eng].h, 1)
                    else:
                        r = i % self.NROT
                        ins.then_inc(self.dma[eng][r].h, 16)
            engmap[eng](body)

        toks = {}
        for e, s_ in self.chain.items():
            n = len(self.ops[e])
            s_.n = bases[e] + n
        for e, lst2 in self.dma.items():
            n = len(self.ops[e])
            for r in range(self.NROT):
                lst2[r].n = bases[e][r] + 16 * ((n - r + self.NROT - 1) // self.NROT)

    def raw_of(self, tok):
        eng, idx = tok
        assert eng in self.chain
        return ("raw", self.chain[eng].h, self._final_base[eng] + idx + 1)



def emit_flash(nc, block, sems, qt_sb, ktf, vf, mask_sb, onat, n_heads,
               kv_div, gate, bufs, entry_waits):
    VW = 132
    n_kvh = n_heads // kv_div
    kbuf, vbuf, qsq, ksq = bufs["kbuf"], bufs["vbuf"], bufs["qsq"], bufs["ksq"]
    kn2, qn2, ptb, gsb = bufs["kn2"], bufs["qn2"], bufs["ptb"], bufs["gsb"]
    obuf, qm2, km2, nbrow = bufs["obuf"], bufs["qm2"], bufs["km2"], bufs["nbrow"]
    negB, rcol = bufs["negB"], bufs["rcol"]
    ones_row = bufs["ones_row"]
    ew = tuple(entry_waits)

    rec = Rec(nc, block, sems, "flash")
    last_exp = {}
    last_pv = {}
    gate_free = {}
    ost = {}
    epi = {}

    with (
        nc.psum_tensor(_u("ps_st"), [128, 2 * 512], F32) as psS,
        nc.psum_tensor(_u("ps_o"), [128, 2 * 512], F32) as psO,
        nc.psum_tensor(_u("ps_g"), [128, 512], F32) as psG,
        nc.psum_tensor(_u("ps_nb"), [128, 512], F32) as psNB,
        nc.psum_tensor(_u("ps_r"), [1, 1024], F32) as psR,
    ):
        vinit = []
        for p in range(2):
            vv = vbuf[:, p * 32 * VW:(p + 1) * 32 * VW].rearrange(
                "q (kt w) -> q kt w", w=VW)
            vinit.append(rec.op("vector", lambda E, vv=vv: E.memset(
                vv[:, :, 128:129], 1.0), waits=ew))
            vinit.append(rec.op("vector", lambda E, vv=vv: E.memset(
                vv[:, :, 129:132], 0.0)))

        ones_col = bufs["ones_col"]
        qtok = {}
        prev_mm = None
        for h in range(n_heads):
            sq = rec.op("scalar", lambda E, h=h: E.activation(
                out=qsq[:, :], in_=qt_sb[:, h * T_LOC:(h + 1) * T_LOC],
                func=AF.Square), waits=ew + (prev_mm,))
            mmq = None
            for chx in range(T_LOC // 512):
                mmq = rec.op("tensor", lambda E, chx=chx: E.matmul(
                    psR[:, chx * 512:(chx + 1) * 512], ones_col[:, :],
                    qsq[:, chx * 512:(chx + 1) * 512], start=True, stop=True),
                    waits=(sq, qtok.get(h - 1)))
            prev_mm = mmq
            mx = rec.op("vector", lambda E, h=h: E.tensor_reduce(
                out=qm2[:, h:h + 1], in_=psR[:, 0:T_LOC], axis=AX.X,
                op=ALU.max), waits=(mmq,))
            qtok[h] = mx

        kload, vload = {}, {}

        def rec_loads(kvh):
            p = kvh % 2
            guard = None
            if kvh >= 2:
                guard = epi.get(((kvh - 2) + 1) * kv_div * NT - 1)
                assert guard is not None, kvh
            kload[kvh] = rec.op("syncd", lambda E, kvh=kvh, p=p: E.dma_start(
                out=kbuf[:, p * SEQ:(p + 1) * SEQ],
                in_=ktf[kvh * 128:(kvh + 1) * 128, :]), waits=(guard,) + ew)
            vv = vbuf[:, p * 32 * VW:(p + 1) * 32 * VW].rearrange(
                "q (kt w) -> q kt w", w=VW)
            vload[kvh] = rec.op("syncd", lambda E, kvh=kvh, vv=vv: E.dma_start(
                out=vv[:, :, 0:128],
                in_=vf[:, kvh * 128:(kvh + 1) * 128].rearrange(
                    "(kt q) d -> q kt d", q=128)),
                waits=(guard, vinit[2 * p], vinit[2 * p + 1]))

        rec_loads(0)

        nbready = {}
        for kvh in range(n_kvh):
            p = kvh % 2
            if kvh + 1 < n_kvh:
                rec_loads(kvh + 1)

            ksqt = rec.op("scalar", lambda E, p=p: E.activation(
                out=ksq[:, :], in_=kbuf[:, p * SEQ:(p + 1) * SEQ],
                func=AF.Square), waits=(kload[kvh], nbready.get(kvh - 1)))
            cols = slice(kvh * kv_div, (kvh + 1) * kv_div)
            c0_first = kvh * kv_div
            kz = rec.op("vector", lambda E, c0=c0_first: E.memset(
                km2[:, c0:c0 + 1], 0.0), waits=(nbready.get(kvh - 1),))
            kmx = kz
            for rnd in range(SEQ // 1024):
                mmr = None
                for chx in range(2):
                    off = (rnd * 2 + chx) * 512
                    mmr = rec.op("tensor", lambda E, chx=chx, off=off:
                        E.matmul(psR[:, chx * 512:(chx + 1) * 512],
                                 ones_col[:, :], ksq[:, off:off + 512],
                                 start=True, stop=True),
                        waits=(ksqt, kmx, qtok[n_heads - 1]))
                rd1 = rec.op("vector", lambda E: E.tensor_reduce(
                    out=nbrow[:, 0:1] if False else km2[:, 15:16],
                    in_=psR[:, 0:1024], axis=AX.X, op=ALU.max),
                    waits=(mmr,)) if False else None
                kmx = rec.op("vector", lambda E, c0=c0_first: E.tensor_reduce(
                    out=qn2[:, 0:1], in_=psR[:, 0:1024], axis=AX.X,
                    op=ALU.max), waits=(mmr, kmx))
                kmx = rec.op("vector", lambda E, c0=c0_first: E.tensor_tensor(
                    out=km2[:, c0:c0 + 1], in0=km2[:, c0:c0 + 1],
                    in1=qn2[:, 0:1], op=ALU.max), waits=(kmx,))
            for t in range(1, kv_div):
                c1 = kvh * kv_div + t
                kmx = rec.op("vector", lambda E, c1=c1, c0=c0_first:
                    E.tensor_copy(km2[:, c1:c1 + 1], km2[:, c0:c0 + 1]),
                    waits=(kmx,))
            tmul = rec.op("vector", lambda E, cols=cols: E.tensor_tensor(
                out=nbrow[:, cols], in0=qm2[:, cols], in1=km2[:, cols],
                op=ALU.mult), waits=(kmx,) + tuple(
                    qtok[h] for h in range(kvh * kv_div, (kvh + 1) * kv_div)))
            tsq = rec.op("scalar", lambda E, cols=cols: E.activation(
                out=nbrow[:, cols], in_=nbrow[:, cols], func=AF.Sqrt),
                waits=(tmul,))
            tneg = rec.op("vector", lambda E, cols=cols: E.tensor_scalar(
                out=nbrow[:, cols], in0=nbrow[:, cols], scalar1=-1.0,
                scalar2=None, op0=ALU.mult), waits=(tsq,))
            bc = rec.op("tensor", lambda E, cols=cols, kvh=kvh: E.matmul(
                psNB[:, kvh * kv_div:(kvh + 1) * kv_div], ones_row[:, 0:128],
                nbrow[:, cols], start=True, stop=True),
                waits=(tneg, nbready.get(kvh - 1)))
            nbready[kvh] = rec.op("vector", lambda E, cols=cols, kvh=kvh:
                E.tensor_copy(negB[:, cols],
                              psNB[:, kvh * kv_div:(kvh + 1) * kv_div]),
                waits=(bc,))

            for hq in range(kvh * kv_div, (kvh + 1) * kv_div):
                for j in range(NT):
                    sidx = hq * NT + j
                    po = sidx % 2
                    st_tok, exp_tok, pv_tok = {}, {}, {}

                    def rec_pv(g):
                        ps_par = g % 2
                        pvs = None
                        for kt in range(4):
                            ktile = 4 * g + kt
                            stat = ptb[:, ps_par * 512 + kt * 128:
                                       ps_par * 512 + (kt + 1) * 128]
                            mv = vbuf[:, p * 32 * VW + ktile * VW:
                                      p * 32 * VW + (ktile + 1) * VW]
                            pvs = rec.op("tensor", lambda E, po=po, stat=stat,
                                         mv=mv, s=(g == 0 and kt == 0),
                                         e2=(g == j and kt == 3): E.matmul(
                                psO[:, po * 512:po * 512 + VW], stat, mv,
                                start=s, stop=e2),
                                waits=(exp_tok[g], vload[kvh],
                                       epi.get(sidx - 2)))
                        pv_tok[g] = pvs
                        last_pv[(sidx, ps_par)] = pvs

                    for g in range(j + 1):
                        ps_par = g % 2
                        dep_exp = exp_tok.get(g - 2) or last_exp.get(
                            (sidx - 1, ps_par)) or last_exp.get((sidx - 2, ps_par))
                        sts = None
                        for kt in range(4):
                            ktile = 4 * g + kt
                            stat = kbuf[:, p * SEQ + ktile * 128:
                                        p * SEQ + (ktile + 1) * 128]
                            mv = qt_sb[:, hq * T_LOC + j * 128:
                                       hq * T_LOC + (j + 1) * 128]
                            sts = rec.op("tensor", lambda E, ps_par=ps_par,
                                         kt=kt, stat=stat, mv=mv: E.matmul(
                                psS[:, ps_par * 512 + kt * 128:
                                    ps_par * 512 + (kt + 1) * 128],
                                stat, mv, start=True, stop=True),
                                waits=(kload[kvh], dep_exp))
                        st_tok[g] = sts
                        mtok = None
                        if g == j:
                            mtok = rec.op("vector", lambda E, ps_par=ps_par,
                                          j=j: E.tensor_tensor(
                                out=psS[:, ps_par * 512:(ps_par + 1) * 512],
                                in0=psS[:, ps_par * 512:(ps_par + 1) * 512],
                                in1=mask_sb[:, j * 512:(j + 1) * 512],
                                op=ALU.add), waits=(sts,))
                        pt_dep = pv_tok.get(g - 2) or last_pv.get(
                            (sidx - 1, ps_par)) or last_pv.get((sidx - 2, ps_par))
                        ex = rec.op("scalar", lambda E, ps_par=ps_par, hq=hq:
                            E.activation(
                                out=ptb[:, ps_par * 512:(ps_par + 1) * 512],
                                in_=psS[:, ps_par * 512:(ps_par + 1) * 512],
                                func=AF.Exp, bias=negB[:, hq:hq + 1]),
                            waits=(sts, mtok, nbready[kvh], pt_dep))
                        exp_tok[g] = ex
                        last_exp[(sidx, ps_par)] = ex
                        if g >= 1:
                            rec_pv(g - 1)
                    rec_pv(j)

                    rc = rec.op("vector", lambda E, po=po: E.reciprocal(
                        rcol[:, po:po + 1],
                        psO[:, po * 512 + 128:po * 512 + 129]),
                        waits=(pv_tok[j], epi.get(sidx - 2)))
                    if gate is not None:
                        gwT, gbB, gate_ready = gate
                        gm = rec.op("tensor", lambda E, po=po, hq=hq, j=j:
                            E.matmul(psG[:, po * 256:po * 256 + 128],
                                     qt_sb[:, hq * T_LOC + j * 128:
                                           hq * T_LOC + (j + 1) * 128],
                                     gwT[:, :], start=True, stop=True),
                            waits=(gate_free.get(sidx - 2), gate_ready))
                        gt = rec.op("vector", lambda E, po=po: E.tensor_tensor(
                            out=psG[:, po * 256:po * 256 + 128],
                            in0=psG[:, po * 256:po * 256 + 128],
                            in1=gbB[:, :], op=ALU.add), waits=(gm,))
                        sg = rec.op("scalar", lambda E, po=po: E.activation(
                            out=gsb[:, po * 128:(po + 1) * 128],
                            in_=psG[:, po * 256:po * 256 + 128],
                            func=AF.Sigmoid), waits=(gt,))
                        gate_free[sidx] = sg
                        ep = rec.op("vector", lambda E, po=po:
                            E.scalar_tensor_tensor(
                                out=obuf[:, po * 128:(po + 1) * 128],
                                in0=psO[:, po * 512:po * 512 + 128],
                                scalar=rcol[:, po:po + 1],
                                in1=gsb[:, po * 128:(po + 1) * 128],
                                op0=ALU.mult, op1=ALU.mult),
                            waits=(rc, sg, ost.get(sidx - 2)))
                    else:
                        ep = rec.op("vector", lambda E, po=po: E.tensor_scalar(
                            out=obuf[:, po * 128:(po + 1) * 128],
                            in0=psO[:, po * 512:po * 512 + 128],
                            scalar1=rcol[:, po:po + 1], scalar2=None,
                            op0=ALU.mult),
                            waits=(rc, ost.get(sidx - 2)))
                    epi[sidx] = ep
                    ost[sidx] = rec.op("syncd", lambda E, po=po, hq=hq, j=j:
                        E.dma_start(
                            out=onat[j * 128:(j + 1) * 128,
                                     hq * 128:(hq + 1) * 128],
                            in_=obuf[:, po * 128:(po + 1) * 128]),
                        waits=(ep,))

        rec.op("vector", lambda E: E.tensor_copy(rcol[:, 0:1], rcol[:, 0:1]),
               waits=tuple(ost[s2] for s2 in range(n_heads * NT - 2,
                                                   n_heads * NT)))
        rec.emit()


def emit_oproj_residual(nc, block, sems, onat, wt_o, xs, oT_sb, wch, entry):
    rec = Rec(nc, block, sems, "oproj")
    with nc.psum_tensor(_u("ps_op"), [128, 4 * 512], F32) as psP:
        tts = []
        for dt in range(16):
            tts.append(rec.op("syncd", lambda E, dt=dt: E.dma_start_transpose(
                out=oT_sb[:, dt * T_LOC:(dt + 1) * T_LOC],
                in_=onat[:, dt * 128:(dt + 1) * 128]), waits=tuple(entry)))
        wl = {}
        evac = {}
        for ec in range(4):
            pw = ec % 2
            wl[ec] = rec.op("syncd", lambda E, ec=ec, pw=pw: E.dma_start(
                out=wch[:, pw * 16 * 512:(pw + 1) * 16 * 512].rearrange(
                    "q (dt e) -> q dt e", e=512),
                in_=wt_o[:, ec * 512:(ec + 1) * 512].rearrange(
                    "(dt q) e -> q dt e", q=128)),
                waits=(evac.get((ec - 2, NT - 1)),))
            for t in range(NT):
                bank = (ec % 2) * 2 + (t % 2)
                mmt = None
                for dt in range(16):
                    mmt = rec.op("tensor", lambda E, bank=bank, dt=dt, t=t,
                                 pw=pw, s=(dt == 0), e2=(dt == 15): E.matmul(
                        psP[:, bank * 512:(bank + 1) * 512],
                        oT_sb[:, dt * T_LOC + t * 128:dt * T_LOC + (t + 1) * 128],
                        wch[:, pw * 16 * 512 + dt * 512:
                            pw * 16 * 512 + (dt + 1) * 512],
                        start=s, stop=e2),
                        waits=tuple(tts) + (wl[ec], evac.get((ec - 1, t - 1)),
                                            evac.get((ec - 2, t)),
                                            evac.get((ec, t - 2))))
                evac[(ec, t)] = rec.op("vector", lambda E, bank=bank, t=t,
                                       ec=ec: E.tensor_tensor(
                    out=xs[:, t * D + ec * 512:t * D + (ec + 1) * 512],
                    in0=psP[:, bank * 512:(bank + 1) * 512],
                    in1=xs[:, t * D + ec * 512:t * D + (ec + 1) * 512],
                    op=ALU.add), waits=(mmt,))
        rec.emit()


def emit_l1_proj(nc, block, sems, xn2T, wq_n, wk_n, wv_n, qfull, kfull,
                 v1_out, v16, wch, entry):
    rec = Rec(nc, block, sems, "l1proj")
    jobs = [("q", wq_n, c) for c in range(4)] + [("k", wk_n, 0), ("v", wv_n, 0)]
    with nc.psum_tensor(_u("ps_l1p"), [128, 4 * 512], F32) as psL:
        evac = {}
        wload = {}
        for ji, (nm, w, c) in enumerate(jobs):
            pw = ji % 2
            wload[ji] = rec.op("syncd", lambda E, w=w, c=c, pw=pw:
                E.dma_start(
                    out=wch[:, pw * 16 * 512:(pw + 1) * 16 * 512].rearrange(
                        "q (dt e) -> q dt e", e=512),
                    in_=w[:, c * 512:(c + 1) * 512].rearrange(
                        "(dt q) e -> q dt e", q=128)),
                waits=tuple(entry) + (evac.get((ji - 2, NT - 1)),))
            for t in range(NT):
                bank = (ji % 2) * 2 + (t % 2)
                mmt = None
                for dt in range(16):
                    mmt = rec.op("tensor", lambda E, bank=bank, dt=dt, t=t,
                                 pw=pw, s=(dt == 0), e2=(dt == 15): E.matmul(
                        psL[:, bank * 512:(bank + 1) * 512],
                        xn2T[:, dt * T_LOC + t * 128:dt * T_LOC + (t + 1) * 128],
                        wch[:, pw * 16 * 512 + dt * 512:
                            pw * 16 * 512 + (dt + 1) * 512],
                        start=s, stop=e2),
                        waits=(wload[ji], evac.get((ji - 1, t - 1)),
                               evac.get((ji - 2, t)),
                               evac.get((ji, t - 2))) + tuple(entry))
                if nm == "q":
                    dst = qfull[:, t * D + c * 512:t * D + (c + 1) * 512]
                elif nm == "k":
                    dst = kfull[:, t * 512:(t + 1) * 512]
                else:
                    dst = v16[:, :]
                ev = rec.op("scalar", lambda E, bank=bank, dst=dst: E.activation(
                    out=dst, in_=psL[:, bank * 512:(bank + 1) * 512],
                    func=AF.Copy),
                    waits=(mmt,) + ((evac.get((ji, t - 1)),) if nm == "v" else ()))
                if nm == "v":
                    ev = rec.op("syncd", lambda E, t=t: E.dma_start(
                        out=v1_out[t * 128:(t + 1) * 128, :], in_=v16[:, :]),
                        waits=(ev,))
                evac[(ji, t)] = ev
        rec.emit()


def emit_l1_normrope(nc, block, sems, qfull, kfull, gqB, gkB, tabs, qnat,
                     knat, qt1_out, kt1_out, bufs, entry):
    rec = Rec(nc, block, sems, "l1nr")
    qb, rot, tq, tk = bufs["qb"], bufs["rot"], bufs["tq"], bufs["tk"]
    qss, kss = bufs["qss"], bufs["kss"]
    qb16, kb16, kb = bufs["qb16"], bufs["kb16"], bufs["kb"]
    TT = bufs["TT"]

    tabtok = {}
    tfree = {}
    qdone, kdone = {}, {}
    for t in range(NT):
        toks = []
        for i, (nm, buf, wdt) in enumerate((("qc", tq, D), ("qs", tq, D),
                                            ("kc", tk, 512), ("ks", tk, 512))):
            off = (i % 2) * (D if nm[0] == "q" else 512)
            toks.append(rec.op("syncd", lambda E, nm=nm, buf=buf, off=off,
                               wdt=wdt, t=t: E.dma_start(
                out=buf[:, off:off + wdt],
                in_=tabs[nm][t * 128:(t + 1) * 128, :]),
                waits=tuple(entry) + (tfree.get(t - 1),)))
        tabtok[t] = toks

        cp = rec.op("vector", lambda E, t=t: E.tensor_copy(
            qb[:, :], qfull[:, t * D:(t + 1) * D]),
            waits=tuple(entry) + (qdone.get(t - 1),))
        sq = rec.op("scalar", lambda E: E.activation(
            out=rot[:, :], in_=qb[:, :], func=AF.Square),
            waits=(cp, qdone.get(t - 1)))
        rd = rec.op("vector", lambda E: E.tensor_reduce(
            out=qss[:, :], in_=rot[:, :].rearrange("q (h d) -> q h d", d=128),
            axis=AX.X, op=ALU.add), waits=(sq,))
        n1 = rec.op("vector", lambda E: E.tensor_scalar(
            out=qss[:, :], in0=qss[:, :], scalar1=1.0 / 128.0,
            scalar2=EPS_RMS, op0=ALU.mult, op1=ALU.add), waits=(rd,))
        n2 = rec.op("scalar", lambda E: E.activation(
            out=qss[:, :], in_=qss[:, :], func=AF.Sqrt), waits=(n1,))
        n3 = rec.op("vector", lambda E: E.reciprocal(qss[:, :], qss[:, :]),
                    waits=(n2,))
        hops = []
        for h in range(16):
            hops.append(rec.op("vector", lambda E, h=h:
                E.scalar_tensor_tensor(
                    out=qb[:, h * 128:(h + 1) * 128],
                    in0=qb[:, h * 128:(h + 1) * 128],
                    scalar=qss[:, h:h + 1],
                    in1=gqB[:, h * 128:(h + 1) * 128],
                    op0=ALU.mult, op1=ALU.mult), waits=(n3, sq)))
        rops = []
        for h in range(16):
            rops.append(rec.op("vector", lambda E, h=h: E.tensor_scalar(
                out=rot[:, h * 128:h * 128 + 64],
                in0=qb[:, h * 128 + 64:h * 128 + 128],
                scalar1=-1.0, scalar2=None, op0=ALU.mult),
                waits=(hops[h], rd)))
            rops.append(rec.op("vector", lambda E, h=h: E.tensor_copy(
                rot[:, h * 128 + 64:h * 128 + 128],
                qb[:, h * 128:h * 128 + 64]), waits=(hops[h],)))
        m1 = rec.op("vector", lambda E: E.tensor_tensor(
            out=qb[:, :], in0=qb[:, :], in1=tq[:, 0:D],
            op=ALU.mult), waits=tuple(rops) + tuple(toks))
        m2_ = rec.op("vector", lambda E: E.tensor_tensor(
            out=rot[:, :], in0=rot[:, :], in1=tq[:, D:2 * D],
            op=ALU.mult), waits=(m1,) + tuple(toks))
        ad = rec.op("vector", lambda E: E.tensor_tensor(
            out=qb16[:, :], in0=qb[:, :], in1=rot[:, :], op=ALU.add),
            waits=(m1, m2_, qdone.get(t - 1)))
        qdone[t] = rec.op("syncd", lambda E, t=t: E.dma_start(
            out=qnat[t * 128:(t + 1) * 128, :], in_=qb16[:, :]),
            waits=(ad,))

        kcp = rec.op("vector", lambda E, t=t: E.tensor_copy(
            kb[:, 0:512], kfull[:, t * 512:(t + 1) * 512]),
            waits=tuple(entry) + (kdone.get(t - 1),))
        sqk = rec.op("scalar", lambda E: E.activation(
            out=kb[:, 512:1024], in_=kb[:, 0:512], func=AF.Square),
            waits=(kcp, kdone.get(t - 1)))
        rdk = rec.op("vector", lambda E: E.tensor_reduce(
            out=kss[:, :], in_=kb[:, 512:1024].rearrange(
                "q (h d) -> q h d", d=128),
            axis=AX.X, op=ALU.add), waits=(sqk,))
        nk1 = rec.op("vector", lambda E: E.tensor_scalar(
            out=kss[:, :], in0=kss[:, :], scalar1=1.0 / 128.0,
            scalar2=EPS_RMS, op0=ALU.mult, op1=ALU.add), waits=(rdk,))
        nk2 = rec.op("scalar", lambda E: E.activation(
            out=kss[:, :], in_=kss[:, :], func=AF.Sqrt), waits=(nk1,))
        nk3 = rec.op("vector", lambda E: E.reciprocal(kss[:, :], kss[:, :]),
                     waits=(nk2,))
        khops = []
        for h in range(4):
            khops.append(rec.op("vector", lambda E, h=h:
                E.scalar_tensor_tensor(
                    out=kb[:, h * 128:(h + 1) * 128],
                    in0=kb[:, h * 128:(h + 1) * 128],
                    scalar=kss[:, h:h + 1],
                    in1=gkB[:, h * 128:(h + 1) * 128],
                    op0=ALU.mult, op1=ALU.mult), waits=(nk3, rdk)))
        krops = []
        for h in range(4):
            krops.append(rec.op("vector", lambda E, h=h: E.tensor_scalar(
                out=kb[:, 512 + h * 128:512 + h * 128 + 64],
                in0=kb[:, h * 128 + 64:h * 128 + 128],
                scalar1=-1.0, scalar2=None, op0=ALU.mult),
                waits=(khops[h],)))
            krops.append(rec.op("vector", lambda E, h=h: E.tensor_copy(
                kb[:, 512 + h * 128 + 64:512 + h * 128 + 128],
                kb[:, h * 128:h * 128 + 64]), waits=(khops[h],)))
        km1 = rec.op("vector", lambda E: E.tensor_tensor(
            out=kb[:, 0:512], in0=kb[:, 0:512], in1=tk[:, 0:512],
            op=ALU.mult), waits=tuple(krops) + tuple(toks))
        km2 = rec.op("vector", lambda E: E.tensor_tensor(
            out=kb[:, 512:1024], in0=kb[:, 512:1024], in1=tk[:, 512:1024],
            op=ALU.mult), waits=(km1,) + tuple(toks))
        kad = rec.op("vector", lambda E: E.tensor_tensor(
            out=kb16[:, :], in0=kb[:, 0:512], in1=kb[:, 512:1024],
            op=ALU.add), waits=(km1, km2, kdone.get(t - 1)))
        kdone[t] = rec.op("syncd", lambda E, t=t: E.dma_start(
            out=knat[t * 128:(t + 1) * 128, :], in_=kb16[:, :]),
            waits=(kad,))
        tfree[t] = kad

    allq = tuple(qdone.values()) + tuple(kdone.values())
    prev_st = None
    for dt in range(16):
        tt_ = rec.op("syncd", lambda E, dt=dt: E.dma_start_transpose(
            out=TT[:, 0:T_LOC], in_=qnat[:, dt * 128:(dt + 1) * 128]),
            waits=allq + ((prev_st,) if prev_st else ()))
        prev_st = rec.op("syncd", lambda E, dt=dt: E.dma_start(
            out=qt1_out[dt * 128:(dt + 1) * 128, :], in_=TT[:, 0:T_LOC]),
            waits=(tt_,))
    for dt in range(4):
        tt_ = rec.op("syncd", lambda E, dt=dt: E.dma_start_transpose(
            out=TT[:, 0:T_LOC], in_=knat[:, dt * 128:(dt + 1) * 128]),
            waits=(prev_st,))
        prev_st = rec.op("syncd", lambda E, dt=dt: E.dma_start(
            out=kt1_out[dt * 128:(dt + 1) * 128, :], in_=TT[:, 0:T_LOC]),
            waits=(tt_,))
    rec.op("vector", lambda E: E.tensor_copy(kss[:, 0:1], kss[:, 0:1]),
           waits=(prev_st,))
    rec.emit()



CC = {"neg1": 16, "invsq": 17, "epsln": 18, "epsq": 19, "one": 20}
NCOL = 24

P1_QW = [("L0gu", 0), ("L0dn", 1)]


def build_p1():
    nc = bass.Bass("TRN2", target_bir_lowering=False)
    x_in = nc.dram_tensor("x", [T_LOC, D], F32, kind="ExternalInput").ap()
    partials = nc.dram_tensor("partials", [8, len(QW_NAMES)], F32, kind="ExternalInput").ap()
    w_gu = nc.dram_tensor("w_L0gu", [EL0, D], F32, kind="ExternalInput").ap()
    w_dn = nc.dram_tensor("w_L0dn", [D, F_L0], F32, kind="ExternalInput").ap()
    w_q = nc.dram_tensor("w_q", [D, D], F32, kind="ExternalInput").ap()
    w_k = nc.dram_tensor("w_k", [D, D], F32, kind="ExternalInput").ap()
    w_v = nc.dram_tensor("w_v", [D, D], F32, kind="ExternalInput").ap()
    w_o = nc.dram_tensor("w_o", [D, D], F32, kind="ExternalInput").ap()

    x1_out = nc.dram_tensor("x1", [T_LOC, D], F32, kind="ExternalOutput").ap()
    qt_out = nc.dram_tensor("qt", [D, T_LOC], BF16, kind="ExternalOutput").ap()
    kt_out = nc.dram_tensor("kt", [D, T_LOC], BF16, kind="ExternalOutput").ap()
    v_out = nc.dram_tensor("v", [T_LOC, D], BF16, kind="ExternalOutput").ap()
    wt_gu = nc.dram_tensor("wt_L0gu", [D, EL0], BF16, kind="ExternalOutput").ap()
    wt_dn = nc.dram_tensor("wt_L0dn", [F_L0, D], BF16, kind="ExternalOutput").ap()
    wt_o = nc.dram_tensor("wt_o", [D, D], BF16, kind="ExternalOutput").ap()
    wt_w1_out = nc.dram_tensor("wt_L1w1", [D, F_L1], BF16, kind="ExternalOutput").ap()
    wt_w2_out = nc.dram_tensor("wt_L1w2", [F_L1, D], BF16, kind="ExternalOutput").ap()
    wnq1_out = nc.dram_tensor("wnat_q1", [D, D], BF16, kind="ExternalOutput").ap()
    wnk1_out = nc.dram_tensor("wnat_k1", [D, 512], BF16, kind="ExternalOutput").ap()
    wnv1_out = nc.dram_tensor("wnat_v1", [D, 512], BF16, kind="ExternalOutput").ap()
    wno1_out = nc.dram_tensor("wnat_o1", [D, D], BF16, kind="ExternalOutput").ap()
    dbg_out = nc.dram_tensor("dbg", [128, 4 * NT], F32, kind="ExternalOutput").ap()
    w_l1w1 = nc.dram_tensor("w_L1w1", [F_L1, D], F32, kind="ExternalInput").ap()
    w_l1w2 = nc.dram_tensor("w_L1w2", [D, F_L1], F32, kind="ExternalInput").ap()
    w1q_in = nc.dram_tensor("w1q", [D, D], F32, kind="ExternalInput").ap()
    w1k_in = nc.dram_tensor("w1k", [D, 512], F32, kind="ExternalInput").ap()
    w1v_in = nc.dram_tensor("w1v", [D, 512], F32, kind="ExternalInput").ap()
    w1o_in = nc.dram_tensor("w1o", [D, D], F32, kind="ExternalInput").ap()
    wnat_l1w1 = nc.dram_tensor("wnat_l1w1", [F_L1, D], BF16).ap()
    wnat_l1w2 = nc.dram_tensor("wnat_l1w2", [D, F_L1], BF16).ap()

    wnat_gu = nc.dram_tensor("wnat_gu", [EL0, D], BF16).ap()
    wnat_dn = nc.dram_tensor("wnat_dn", [D, F_L0], BF16).ap()
    wnat_q = nc.dram_tensor("wnat_q", [D, D], BF16).ap()
    wnat_k = nc.dram_tensor("wnat_k", [D, D], BF16).ap()
    wnat_v = nc.dram_tensor("wnat_v", [D, D], BF16).ap()
    wnat_o = nc.dram_tensor("wnat_o", [D, D], BF16).ap()
    wt_q = nc.dram_tensor("wtq", [D, D], BF16).ap()
    wt_k = nc.dram_tensor("wtk", [D, D], BF16).ap()
    wt_v = nc.dram_tensor("wtv", [D, D], BF16).ap()
    k1nat = nc.dram_tensor("k1nat", [T_LOC, D], BF16).ap()
    g_dram = nc.dram_tensor("g_dram", [T_LOC, F_L0], F32).ap()
    k2nat = nc.dram_tensor("k2nat", [T_LOC, F_L0], BF16).ap()
    xn2nat = nc.dram_tensor("xn2nat", [T_LOC, D], BF16).ap()

    sem_names = ["ld0", "ch_v", "ch_s", "mm", "wp_ld0", "wp_ld1", "wp_sg",
                 "wp_cv0", "wp_cv1", "wp_st0", "wp_st1", "wp_tt0", "wp_tt1",
                 "wp_wo0", "wp_wo1", "wl0", "wl1", "tt0", "tt1", "ue_s",
                 "ue_v", "gst", "gr0", "gr1", "kn", "k2t0", "k2t1",
                 "wd0", "wd1", "dn_ev", "ob", "qe0", "qe1"]

    import contextlib
    with contextlib.ExitStack() as st_:
        sems = {n: Sem(st_.enter_context(nc.semaphore(n))) for n in sem_names}
        consts = st_.enter_context(nc.sbuf_tensor("consts", [128, NCOL], F32))
        ones_col = st_.enter_context(nc.sbuf_tensor("ones_c", [128, 1], F32))
        ones_row = st_.enter_context(nc.sbuf_tensor("ones_r", [1, 128], F32))
        xs = st_.enter_context(nc.sbuf_tensor("xs", [128, NT * D], F32))
        scU = st_.enter_context(nc.sbuf_tensor("scU", [128, NT], F32))
        m2 = st_.enter_context(nc.sbuf_tensor("m2", [128, NT], F32))
        s2c = st_.enter_context(nc.sbuf_tensor("s2c", [128, NT], F32))
        sc2 = st_.enter_context(nc.sbuf_tensor("sc2", [128, NT], F32))
        block = st_.enter_context(nc.Block())

        ch_v, ch_s, mm = sems["ch_v"], sems["ch_s"], sems["mm"]
        ld0, kn = sems["ld0"], sems["kn"]

        @block.vector
        def _(vector):
            for name, val in (("neg1", -1.0), ("invsq", INV_SQRT_DH),
                              ("epsln", EPS_LN), ("epsq", EPS_Q), ("one", 1.0)):
                ch_v.inc(vector.memset(consts[:, CC[name]:CC[name] + 1], val), 1)

        counts = [QW_SHAPES[n][0] * QW_SHAPES[n][1] for n in QW_NAMES]
        emit_scales(nc, block, sems, partials, consts, ones_col, ones_row, counts)

        @block.gpsimd
        def _(gpsimd):
            for t in range(NT):
                ld0.inc(gpsimd.dma_start(out=xs[:, t * D:(t + 1) * D],
                                         in_=x_in[t * 128:(t + 1) * 128, :]), 16)
        x_loaded = ld0.n

        CW = 4096
        with (
            nc.sbuf_tensor("wpf", [128, 2 * CW], F32) as wpf,
            nc.sbuf_tensor("sg1", [128, CW], F32) as sg1,
            nc.sbuf_tensor("sg2", [128, CW], F32) as sg2,
            nc.sbuf_tensor("wpb", [128, 2 * CW], BF16) as wpb,
            nc.sbuf_tensor("wtt", [128, EL0], BF16) as wtt,
        ):
            bufs = {"wpf": wpf, "sg1": sg1, "sg2": sg2, "wpb": wpb, "wtt": wtt}
            consts_ready = ch_v.n

            @block.scalar
            def _(scalar):
                scalar.wait_ge(ch_v.h, consts_ready)

            @block.vector
            def _(vector):
                vector.wait_ge(ch_v.h, consts_ready)

            jobs = [
                (w_gu, wnat_gu, wt_gu, EL0, D, True, 0, True),
                (w_dn, wnat_dn, wt_dn, D, F_L0, True, 1, True),
                (w_q, wnat_q, wt_q, D, D, False, 0, True),
                (w_k, wnat_k, wt_k, D, D, False, 0, True),
                (w_v, wnat_v, wt_v, D, D, False, 0, True),
                (w_o, wnat_o, wt_o, D, D, False, 0, True),
                (w_l1w1, wnat_l1w1, wt_w1_out, F_L1, D, True, 2, True),
                (w_l1w2, wnat_l1w2, wt_w2_out, D, F_L1, True, 3, True),
                (w1q_in, wnq1_out, None, D, D, False, 0, False),
                (w1k_in, wnk1_out, None, D, 512, False, 0, False),
                (w1v_in, wnv1_out, None, D, 512, False, 0, False),
                (w1o_in, wno1_out, None, D, D, False, 0, False),
            ]
            for (win, wnat, wtout, R, C, tern, wi, dot) in jobs:
                flatR = R * C // CW
                emit_weight_prep(
                    nc, block, sems,
                    win.rearrange("r c -> (r c)").rearrange("(q c) -> q c", c=CW),
                    wnat.rearrange("r c -> (r c)").rearrange("(q c) -> q c", c=CW),
                    wtout, R, C, tern, consts, wi, bufs,
                    flat_rows=flatR, passA_cols=CW, wnat_geo=wnat,
                    do_transpose=dot)

        k1T = st_.enter_context(nc.sbuf_tensor("k1T", [128, 16 * T_LOC], BF16))
        with (
            nc.sbuf_tensor("sqs", [128, D], F32) as sqs,
            nc.sbuf_tensor("ybuf", [128, D], F32) as ybuf,
            nc.sbuf_tensor("kibuf", [128, D], I32) as kibuf,
            nc.sbuf_tensor("dbuf", [128, D], F32) as dbuf,
            nc.sbuf_tensor("t1buf", [128, D], F32) as t1buf,
            nc.sbuf_tensor("nkbuf", [128, D], BF16) as nkbuf,
            nc.sbuf_tensor("tcs", [128, 12], F32) as tcs,
        ):
            prep_gates = [(sems[n2], sems[n2].n) for n2 in
                          ("wp_ld0", "wp_ld1", "wp_sg", "wp_cv0", "wp_cv1",
                           "wp_st0", "wp_st1", "wp_tt0", "wp_tt1", "wp_wo0",
                           "wp_wo1")]
            emit_lnq_tiles(nc, block, sems, xs, k1nat, consts, tcs, sqs, ybuf,
                           kibuf, dbuf, t1buf, nkbuf,
                           [(ld0, x_loaded)] + prep_gates,
                           quant=True, sc_out=scU, sc_wcol=2, xn_nat=None)

            tt0 = sems["tt0"]
            k1_stored = kn.n

            @block.sync
            def _(sync):
                sync.wait_ge(kn.h, k1_stored)
                for s2_ in ("wp_st0", "wp_st1", "wp_wo0", "wp_wo1"):
                    sync.wait_ge(sems[s2_].h, sems[s2_].n)
                for dt in range(16):
                    tt0.inc(sync.dma_start_transpose(
                        out=k1T[:, dt * T_LOC:(dt + 1) * T_LOC],
                        in_=k1nat[:, dt * 128:(dt + 1) * 128]), 16)

        with (
            nc.sbuf_tensor("wua", [128, 2 * 16 * 512], BF16) as wua,
            nc.sbuf_tensor("wub", [128, 2 * 16 * 512], BF16) as wub,
            nc.sbuf_tensor("gab", [128, 2 * 512], F32) as gab,
            nc.sbuf_tensor("gbb", [128, 2 * 512], F32) as gbb,
            nc.sbuf_tensor("rmaxb", [128, 2], F32) as rmaxb,
        ):
            emit_ffn_up_glu(nc, block, sems, k1T, wt_gu, g_dram, scU, m2,
                            Fh=F_L0, glu=True, silu=True,
                            bufs={"wua": wua, "wub": wub, "ga": gab, "gb": gbb,
                                  "rmax": rmaxb})

        with (
            nc.sbuf_tensor("grl", [128, 2 * 1024], F32) as grl,
            nc.sbuf_tensor("y2", [128, 1024], F32) as y2,
            nc.sbuf_tensor("ki2", [128, 1024], I32) as ki2,
            nc.sbuf_tensor("d2", [128, 1024], F32) as d2,
            nc.sbuf_tensor("t12", [128, 1024], F32) as t12,
            nc.sbuf_tensor("nk2", [128, 1024], BF16) as nk2,
            nc.sbuf_tensor("k2T", [128, 2 * F_L0], BF16) as k2T,
            nc.sbuf_tensor("wdn", [128, 8 * D], BF16) as wdn,
        ):
            emit_quant_g(nc, block, sems, g_dram, k2nat, m2, s2c, sc2, consts,
                         grl, y2, ki2, d2, t12, nk2, dn_wi=1, Fh=F_L0,
                         extra_half=True)
            emit_ffn_down(nc, block, sems, k2nat, wt_dn, xs, sc2, k2T, wdn,
                          Fh=F_L0, Eo=D)
        x1_ready = sems["dn_ev"].n

        with (
            nc.sbuf_tensor("sqs2", [128, D], F32) as sqs2,
            nc.sbuf_tensor("xn2b", [128, D], BF16) as xn2b,
            nc.sbuf_tensor("tcs2", [128, 12], F32) as tcs2,
            nc.sbuf_tensor("xn2T", [128, 16 * T_LOC], BF16) as xn2T,
            nc.sbuf_tensor("wqkv", [128, 2 * 16 * 512], BF16) as wqkv,
            nc.sbuf_tensor("pev", [128, 4 * 512], BF16) as pev,
        ):
            emit_lnq_tiles(nc, block, sems, xs, None, consts, tcs2, sqs2, None,
                           None, None, None, xn2b, (sems["dn_ev"], x1_ready),
                           quant=False, sc_out=None, sc_wcol=None,
                           xn_nat=xn2nat)

            tt1 = sems["tt1"]
            xn2_stored = kn.n

            @block.sync
            def _(sync):
                sync.wait_ge(kn.h, xn2_stored)
                for dt in range(16):
                    tt1.inc(sync.dma_start_transpose(
                        out=xn2T[:, dt * T_LOC:(dt + 1) * T_LOC],
                        in_=xn2nat[:, dt * 128:(dt + 1) * 128]), 16)

            ob = sems["ob"]

            @block.gpsimd
            def _(gpsimd):
                gpsimd.wait_ge(sems["dn_ev"].h, x1_ready)
                gpsimd.wait_ge(tt1.h, tt1.n)
                for t in range(NT):
                    ob.inc(gpsimd.dma_start(out=x1_out[t * 128:(t + 1) * 128, :],
                                            in_=xs[:, t * D:(t + 1) * D]), 16)

            emit_qkv_l0(nc, block, sems, xn2T, wt_q, wt_k, wt_v, qt_out,
                        kt_out, v_out, consts, wqkv, pev)

        dbg_sb = st_.enter_context(nc.sbuf_tensor("dbgsb", [128, 4 * NT], F32))

        @block.vector
        def _(vector):
            vector.wait_ge(sems["dn_ev"].h, sems["dn_ev"].n)
            for i, src in enumerate((scU, m2, sc2, s2c)):
                ch_v.inc(vector.tensor_copy(dbg_sb[:, i * NT:(i + 1) * NT], src[:, :]), 1)

        @block.sync
        def _(sync):
            sync.wait_ge(ch_v.h, ch_v.n)
            sems["ob"].inc(sync.dma_start(out=dbg_out[:, :], in_=dbg_sb[:, :]), 16)
            for s_ in (sems["ob"], sems["qe0"], sems["qe1"], sems["wp_wo0"],
                       sems["wp_wo1"]):
                sync.wait_ge(s_.h, s_.n)
    return nc



def build_p2():
    nc = bass.Bass("TRN2", target_bir_lowering=False)
    x1_in = nc.dram_tensor("x1", [T_LOC, D], F32, kind="ExternalInput").ap()
    partials = nc.dram_tensor("partials", [8, len(QW_NAMES)], F32, kind="ExternalInput").ap()
    qt_in = nc.dram_tensor("qt", [D, T_LOC], BF16, kind="ExternalInput").ap()
    ktf_in = nc.dram_tensor("ktf", [D, SEQ], BF16, kind="ExternalInput").ap()
    vf_in = nc.dram_tensor("vf", [SEQ, D], BF16, kind="ExternalInput").ap()
    mask_in = nc.dram_tensor("mask", [128, NT * 512], F32, kind="ExternalInput").ap()
    gw_in = nc.dram_tensor("gate_w", [128, 128], F32, kind="ExternalInput").ap()
    gb_in = nc.dram_tensor("gate_b", [1, 128], F32, kind="ExternalInput").ap()
    wt_gu = nc.dram_tensor("wt_L0gu", [D, EL0], BF16, kind="ExternalInput").ap()
    wt_dn = nc.dram_tensor("wt_L0dn", [F_L0, D], BF16, kind="ExternalInput").ap()
    wt_o = nc.dram_tensor("wt_o", [D, D], BF16, kind="ExternalInput").ap()
    wt_w1_in = nc.dram_tensor("wt_L1w1", [D, F_L1], BF16, kind="ExternalInput").ap()
    wt_w2_in = nc.dram_tensor("wt_L1w2", [F_L1, D], BF16, kind="ExternalInput").ap()
    wnq1_in = nc.dram_tensor("wnat_q1", [D, D], BF16, kind="ExternalInput").ap()
    wnk1_in = nc.dram_tensor("wnat_k1", [D, 512], BF16, kind="ExternalInput").ap()
    wnv1_in = nc.dram_tensor("wnat_v1", [D, 512], BF16, kind="ExternalInput").ap()
    gq_in = nc.dram_tensor("gq_row", [1, D], F32, kind="ExternalInput").ap()
    gk_in = nc.dram_tensor("gk_row", [1, 512], F32, kind="ExternalInput").ap()
    tabs = {nm: nc.dram_tensor("tab_" + nm, [T_LOC, D if nm[0] == "q" else 512],
                               F32, kind="ExternalInput").ap()
            for nm in ("qc", "qs", "kc", "ks")}

    x4_out = nc.dram_tensor("x4", [T_LOC, D], F32, kind="ExternalOutput").ap()
    qt1_out = nc.dram_tensor("qt1", [D, T_LOC], BF16, kind="ExternalOutput").ap()
    kt1_out = nc.dram_tensor("kt1", [512, T_LOC], BF16, kind="ExternalOutput").ap()
    v1_out = nc.dram_tensor("v1", [T_LOC, 512], BF16, kind="ExternalOutput").ap()
    dbg_out = nc.dram_tensor("dbg", [T_LOC, D], BF16, kind="ExternalOutput").ap()

    onat = nc.dram_tensor("onat", [T_LOC, D], BF16).ap()
    k1nat = nc.dram_tensor("k1nat", [T_LOC, D], BF16).ap()
    g_dram = nc.dram_tensor("g_dram", [T_LOC, F_L0], F32).ap()
    k2nat = nc.dram_tensor("k2nat", [T_LOC, F_L0], BF16).ap()
    xn2nat = nc.dram_tensor("xn2nat", [T_LOC, D], BF16).ap()
    qnat = nc.dram_tensor("qnat", [T_LOC, D], BF16).ap()
    knat = nc.dram_tensor("knat", [T_LOC, 512], BF16).ap()

    sem_names = ["ld0", "ch_v", "ch_s", "ch_p", "mm", "wp_ld0", "wp_ld1",
                 "wp_sg", "wp_cv0", "wp_cv1", "wp_st0", "wp_st1", "wp_tt0",
                 "wp_tt1", "wp_wo0", "wp_wo1", "wl0", "wl1", "tt0", "tt1",
                 "ue_s", "ue_v", "gst", "gr0", "gr1", "kn", "k2t0", "k2t1",
                 "wd0", "wd1", "dn_ev", "ob", "sd0", "sd1", "sd2", "sd3",
                 "gd0", "gd1", "gd2", "gd3", "kv0", "kv1", "oe"]

    import contextlib
    with contextlib.ExitStack() as st_:
        sems = {n: Sem(st_.enter_context(nc.semaphore(n))) for n in sem_names}
        consts = st_.enter_context(nc.sbuf_tensor("consts", [128, NCOL], F32))
        ones_col = st_.enter_context(nc.sbuf_tensor("ones_c", [128, 1], F32))
        ones_row = st_.enter_context(nc.sbuf_tensor("ones_r", [1, 128], F32))
        xs = st_.enter_context(nc.sbuf_tensor("xs", [128, NT * D], F32))
        scU = st_.enter_context(nc.sbuf_tensor("scU", [128, NT], F32))
        m2 = st_.enter_context(nc.sbuf_tensor("m2", [128, NT], F32))
        s2c = st_.enter_context(nc.sbuf_tensor("s2c", [128, NT], F32))
        sc2 = st_.enter_context(nc.sbuf_tensor("sc2", [128, NT], F32))
        block = st_.enter_context(nc.Block())

        ch_v, ch_s, mm = sems["ch_v"], sems["ch_s"], sems["mm"]
        ld0, kn = sems["ld0"], sems["kn"]

        @block.vector
        def _(vector):
            for name, val in (("neg1", -1.0), ("invsq", INV_SQRT_DH),
                              ("epsln", EPS_LN), ("epsq", EPS_Q), ("one", 1.0)):
                ch_v.inc(vector.memset(consts[:, CC[name]:CC[name] + 1], val), 1)

        counts = [QW_SHAPES[n][0] * QW_SHAPES[n][1] for n in QW_NAMES]
        emit_scales(nc, block, sems, partials, consts, ones_col, ones_row, counts)

        @block.gpsimd
        def _(gpsimd):
            for t in range(NT):
                ld0.inc(gpsimd.dma_start(out=xs[:, t * D:(t + 1) * D],
                                         in_=x1_in[t * 128:(t + 1) * 128, :]), 16)
        x_loaded = ld0.n

        es0 = contextlib.ExitStack()
        qt_sb = es0.enter_context(nc.sbuf_tensor("qt_sb", [128, 16 * T_LOC], BF16))
        mask_sb = es0.enter_context(nc.sbuf_tensor("mask_sb", [128, NT * 512], F32))
        gwf = es0.enter_context(nc.sbuf_tensor("gwf", [128, 128], F32))
        gw16 = es0.enter_context(nc.sbuf_tensor("gw16", [128, 128], BF16))
        gwT = es0.enter_context(nc.sbuf_tensor("gwT", [128, 128], BF16))
        gbB = es0.enter_context(nc.sbuf_tensor("gbB", [128, 128], F32))
        gbr = es0.enter_context(nc.sbuf_tensor("gbr", [1, 128], F32))
        kbuf = es0.enter_context(nc.sbuf_tensor("kbuf", [128, 2 * SEQ], BF16))
        vbuf = es0.enter_context(nc.sbuf_tensor("vbuf", [128, 2 * 32 * 132], BF16))
        qsq = es0.enter_context(nc.sbuf_tensor("qsq", [128, T_LOC], F32))
        ksq = es0.enter_context(nc.sbuf_tensor("ksq", [128, SEQ], F32))
        kn2 = es0.enter_context(nc.sbuf_tensor("kn2", [1, SEQ], F32))
        qn2 = es0.enter_context(nc.sbuf_tensor("qn2", [1, T_LOC], F32))
        ptb = es0.enter_context(nc.sbuf_tensor("ptb", [128, 2 * 512], BF16))
        gsb = es0.enter_context(nc.sbuf_tensor("gsb", [128, 2 * 128], F32))
        obuf = es0.enter_context(nc.sbuf_tensor("obuf", [128, 2 * 128], BF16))
        qm2 = es0.enter_context(nc.sbuf_tensor("qm2", [1, 16], F32))
        km2 = es0.enter_context(nc.sbuf_tensor("km2", [1, 16], F32))
        nbrow = es0.enter_context(nc.sbuf_tensor("nbrow", [1, 16], F32))
        negB = es0.enter_context(nc.sbuf_tensor("negB", [128, 16], F32))
        rcolf = es0.enter_context(nc.sbuf_tensor("rcolf", [128, 2], F32))
        brec = Rec(nc, block, sems, "p2load")
        qts = []
        for h in range(16):
            qts.append(brec.op("syncd", lambda E, h=h: E.dma_start(
                out=qt_sb[:, h * T_LOC:(h + 1) * T_LOC],
                in_=qt_in[h * 128:(h + 1) * 128, :])))
        mld = brec.op("syncd", lambda E: E.dma_start(
            out=mask_sb[:, :], in_=mask_in[:, :]))
        gld = brec.op("syncd", lambda E: E.dma_start(
            out=gwf[:, :], in_=gw_in[:, :]))
        gbl = brec.op("syncd", lambda E: E.dma_start(
            out=gbr[:, :], in_=gb_in[:, :]))
        gcv = brec.op("vector", lambda E: E.tensor_copy(gw16[:, :], gwf[:, :]),
                      waits=(gld,))
        gtt = brec.op("syncd", lambda E: E.dma_start_transpose(
            out=gwT[:, :], in_=gw16[:, :]), waits=(gcv,))
        with nc.psum_tensor("ps_gb", [128, 512], F32) as psGB:
            gbm = brec.op("tensor", lambda E: E.matmul(
                psGB[:, 0:128], ones_row[:, 0:128], gbr[:, :],
                start=True, stop=True), waits=(gbl,))
            gbc = brec.op("vector", lambda E: E.tensor_copy(
                gbB[:, :], psGB[:, 0:128]), waits=(gbm,))
            brec.emit()

        entry = [("raw", sems["sd%d" % r].h, sems["sd%d" % r].n)
                 for r in range(4)] + [("raw", ch_v.h, ch_v.n)]
        fbufs = {"kbuf": kbuf, "vbuf": vbuf, "qsq": qsq, "ksq": ksq,
                 "kn2": kn2, "qn2": qn2, "ptb": ptb, "gsb": gsb,
                 "obuf": obuf, "qm2": qm2, "km2": km2, "nbrow": nbrow,
                 "negB": negB, "rcol": rcolf, "ones_row": ones_row,
                 "ones_col": ones_col}
        gate_ready = ("raw", ch_v.h, ch_v.n)
        emit_flash(nc, block, sems, qt_sb, ktf_in, vf_in, mask_sb, onat,
                   16, 1, (gwT, gbB, gate_ready), fbufs, entry)
        es0.close()
        flash_done = [("raw", sems["sd%d" % r].h, sems["sd%d" % r].n)
                      for r in range(4)]

        @block.gpsimd
        def _(gpsimd):
            for r in range(4):
                gpsimd.wait_ge(sems["sd%d" % r].h, sems["sd%d" % r].n)
            for t in range(NT):
                sems["oe"].inc(gpsimd.dma_start(
                    out=dbg_out[t * 128:(t + 1) * 128, :],
                    in_=onat[t * 128:(t + 1) * 128, :]), 16)

        es1 = contextlib.ExitStack()
        oT = es1.enter_context(nc.sbuf_tensor("oT", [128, 16 * T_LOC], BF16))
        wcho = es1.enter_context(nc.sbuf_tensor("wcho", [128, 2 * 16 * 512], BF16))
        @block.vector
        def _(vector):
            vector.wait_ge(ld0.h, x_loaded)
            ch_v.inc(vector.tensor_copy(scU[:, 0:1], scU[:, 0:1]), 1)
        emit_oproj_residual(nc, block, sems, onat, wt_o, xs, oT, wcho,
                            flash_done + [("raw", ld0.h, x_loaded)])
        es1.close()
        x2_ready = (sems["ch_v"], ch_v.n)

        esK = contextlib.ExitStack()
        k1T = esK.enter_context(nc.sbuf_tensor("k1T", [128, 16 * T_LOC], BF16))
        es2 = contextlib.ExitStack()
        sqs = es2.enter_context(nc.sbuf_tensor("sqs", [128, D], F32))
        ybuf = es2.enter_context(nc.sbuf_tensor("ybuf", [128, D], F32))
        kibuf = es2.enter_context(nc.sbuf_tensor("kibuf", [128, D], I32))
        dbuf = es2.enter_context(nc.sbuf_tensor("dbuf", [128, D], F32))
        t1buf = es2.enter_context(nc.sbuf_tensor("t1buf", [128, D], F32))
        nkbuf = es2.enter_context(nc.sbuf_tensor("nkbuf", [128, D], BF16))
        tcs = es2.enter_context(nc.sbuf_tensor("tcs", [128, 12], F32))
        emit_lnq_tiles(nc, block, sems, xs, k1nat, consts, tcs, sqs, ybuf,
                       kibuf, dbuf, t1buf, nkbuf,
                       (x2_ready[0], x2_ready[1]),
                       quant=True, sc_out=scU, sc_wcol=2, xn_nat=None)
        tt0 = sems["tt0"]
        k1_stored = kn.n

        @block.sync
        def _(sync):
            sync.wait_ge(kn.h, k1_stored)
            for dt in range(16):
                tt0.inc(sync.dma_start_transpose(
                    out=k1T[:, dt * T_LOC:(dt + 1) * T_LOC],
                    in_=k1nat[:, dt * 128:(dt + 1) * 128]), 16)

        es2.close()
        es3 = contextlib.ExitStack()
        wua = es3.enter_context(nc.sbuf_tensor("wua", [128, 2 * 16 * 512], BF16))
        wub = es3.enter_context(nc.sbuf_tensor("wub", [128, 2 * 16 * 512], BF16))
        gab = es3.enter_context(nc.sbuf_tensor("gab", [128, 2 * 512], F32))
        gbb2 = es3.enter_context(nc.sbuf_tensor("gbb2", [128, 2 * 512], F32))
        rmaxb = es3.enter_context(nc.sbuf_tensor("rmaxb", [128, 2], F32))
        emit_ffn_up_glu(nc, block, sems, k1T, wt_gu, g_dram, scU, m2,
                        Fh=F_L0, glu=True, silu=True,
                        bufs={"wua": wua, "wub": wub, "ga": gab,
                              "gb": gbb2, "rmax": rmaxb})
        es3.close()
        es4 = contextlib.ExitStack()
        grl = es4.enter_context(nc.sbuf_tensor("grl", [128, 2 * 1024], F32))
        y2 = es4.enter_context(nc.sbuf_tensor("y2", [128, 1024], F32))
        ki2 = es4.enter_context(nc.sbuf_tensor("ki2", [128, 1024], I32))
        d2 = es4.enter_context(nc.sbuf_tensor("d2", [128, 1024], F32))
        t12 = es4.enter_context(nc.sbuf_tensor("t12", [128, 1024], F32))
        nk2 = es4.enter_context(nc.sbuf_tensor("nk2", [128, 1024], BF16))
        k2T = es4.enter_context(nc.sbuf_tensor("k2T", [128, 2 * F_L0], BF16))
        wdn = es4.enter_context(nc.sbuf_tensor("wdn", [128, 8 * D], BF16))
        emit_quant_g(nc, block, sems, g_dram, k2nat, m2, s2c, sc2, consts,
                     grl, y2, ki2, d2, t12, nk2, dn_wi=1, Fh=F_L0,
                     extra_half=True)
        emit_ffn_down(nc, block, sems, k2nat, wt_dn, xs, sc2, k2T, wdn,
                      Fh=F_L0, Eo=D)
        es4.close()
        x3_ready_n = sems["dn_ev"].n

        es6 = contextlib.ExitStack()
        sqs = es6.enter_context(nc.sbuf_tensor("sqsB", [128, D], F32))
        ybuf = es6.enter_context(nc.sbuf_tensor("ybufB", [128, D], F32))
        kibuf = es6.enter_context(nc.sbuf_tensor("kibufB", [128, D], I32))
        dbuf = es6.enter_context(nc.sbuf_tensor("dbufB", [128, D], F32))
        t1buf = es6.enter_context(nc.sbuf_tensor("t1bufB", [128, D], F32))
        nkbuf = es6.enter_context(nc.sbuf_tensor("nkbufB", [128, D], BF16))
        tcs = es6.enter_context(nc.sbuf_tensor("tcsB", [128, 12], F32))
        emit_lnq_tiles(nc, block, sems, xs, k1nat, consts, tcs, sqs, ybuf,
                       kibuf, dbuf, t1buf, nkbuf,
                       (sems["dn_ev"], x3_ready_n),
                       quant=True, sc_out=scU, sc_wcol=4 * 2 + 2,
                       xn_nat=None)
        tt0 = sems["tt0"]
        k1b_stored = kn.n

        @block.sync
        def _(sync):
            sync.wait_ge(kn.h, k1b_stored)
            sync.wait_ge(sems["ue_v"].h, sems["ue_v"].n)
            for dt in range(16):
                tt0.inc(sync.dma_start_transpose(
                    out=k1T[:, dt * T_LOC:(dt + 1) * T_LOC],
                    in_=k1nat[:, dt * 128:(dt + 1) * 128]), 16)

        es6.close()
        es7 = contextlib.ExitStack()
        wua = es7.enter_context(nc.sbuf_tensor("wuaB", [128, 2 * 16 * 512], BF16))
        gab = es7.enter_context(nc.sbuf_tensor("gabB", [128, 2 * 512], F32))
        gbb2 = es7.enter_context(nc.sbuf_tensor("gbbB", [128, 2 * 512], F32))
        rmaxb = es7.enter_context(nc.sbuf_tensor("rmaxB", [128, 2], F32))
        emit_ffn_up_glu(nc, block, sems, k1T, wt_w1_in, g_dram, scU, m2,
                        Fh=F_L1, glu=False, silu=False,
                        bufs={"wua": wua, "wub": wua, "ga": gab,
                              "gb": gbb2, "rmax": rmaxb})
        es7.close()
        es8 = contextlib.ExitStack()
        grl = es8.enter_context(nc.sbuf_tensor("grlB", [128, 2 * 1024], F32))
        y2 = es8.enter_context(nc.sbuf_tensor("y2B", [128, 1024], F32))
        ki2 = es8.enter_context(nc.sbuf_tensor("ki2B", [128, 1024], I32))
        d2 = es8.enter_context(nc.sbuf_tensor("d2B", [128, 1024], F32))
        t12 = es8.enter_context(nc.sbuf_tensor("t12B", [128, 1024], F32))
        nk2 = es8.enter_context(nc.sbuf_tensor("nk2B", [128, 1024], BF16))
        k2T = es8.enter_context(nc.sbuf_tensor("k2TB", [128, 2 * F_L1], BF16))
        wdn = es8.enter_context(nc.sbuf_tensor("wdnB", [128, 8 * D], BF16))
        emit_quant_g(nc, block, sems, g_dram, k2nat, m2, s2c, sc2, consts,
                     grl, y2, ki2, d2, t12, nk2, dn_wi=3, Fh=F_L1,
                     extra_half=True)
        emit_ffn_down(nc, block, sems, k2nat, wt_w2_in, xs, sc2, k2T, wdn,
                      Fh=F_L1, Eo=D)
        es8.close()
        x4_ready_n = sems["dn_ev"].n
        esK.close()

        ob = sems["ob"]

        es9 = contextlib.ExitStack()
        sqs = es9.enter_context(nc.sbuf_tensor("sqsC", [128, D], F32))
        xnb = es9.enter_context(nc.sbuf_tensor("xnb", [128, D], BF16))
        tcs = es9.enter_context(nc.sbuf_tensor("tcsC", [128, 12], F32))
        emit_lnq_tiles(nc, block, sems, xs, None, consts, tcs, sqs, None,
                       None, None, None, xnb, (sems["dn_ev"], x4_ready_n),
                       quant=False, sc_out=None, sc_wcol=None,
                       xn_nat=xn2nat)
        es9.close()
        xn_stored = kn.n

        es10 = contextlib.ExitStack()
        qfull = es10.enter_context(nc.sbuf_tensor("qfull", [128, NT * D], BF16))
        kfull = es10.enter_context(nc.sbuf_tensor("kfull", [128, NT * 512], BF16))
        v16 = es10.enter_context(nc.sbuf_tensor("v16", [128, 512], BF16))
        gqB = es10.enter_context(nc.sbuf_tensor("gqB", [128, D], F32))
        gkB = es10.enter_context(nc.sbuf_tensor("gkB", [128, 512], F32))
        gqr = es10.enter_context(nc.sbuf_tensor("gqr", [1, D], F32))
        gkr = es10.enter_context(nc.sbuf_tensor("gkr", [1, 512], F32))
        esPJ = contextlib.ExitStack()
        xn2T = esPJ.enter_context(nc.sbuf_tensor("xn2T", [128, 16 * T_LOC], BF16))
        wchL = esPJ.enter_context(nc.sbuf_tensor("wchL", [128, 2 * 16 * 512], BF16))
        tt1 = sems["tt1"]

        @block.sync
        def _(sync):
            sync.wait_ge(kn.h, xn_stored)
            for dt in range(16):
                tt1.inc(sync.dma_start_transpose(
                    out=xn2T[:, dt * T_LOC:(dt + 1) * T_LOC],
                    in_=xn2nat[:, dt * 128:(dt + 1) * 128]), 16)

        @block.gpsimd
        def _(gpsimd):
            gpsimd.wait_ge(sems["dn_ev"].h, x4_ready_n)
            gpsimd.wait_ge(tt1.h, tt1.n)
            for t in range(NT):
                ob.inc(gpsimd.dma_start(out=x4_out[t * 128:(t + 1) * 128, :],
                                        in_=xs[:, t * D:(t + 1) * D]), 16)

        grec = Rec(nc, block, sems, "gains")
        g1 = grec.op("syncd", lambda E: E.dma_start(out=gqr[:, :], in_=gq_in[:, :]))
        g2 = grec.op("syncd", lambda E: E.dma_start(out=gkr[:, :], in_=gk_in[:, :]))
        with nc.psum_tensor("ps_gq", [128, 4 * 512], F32) as psGQ:
            mms = []
            for c in range(4):
                mms.append(grec.op("tensor", lambda E, c=c: E.matmul(
                    psGQ[:, c * 512:(c + 1) * 512], ones_row[:, 0:128],
                    gqr[:, c * 512:(c + 1) * 512], start=True, stop=True),
                    waits=(g1,)))
            cps = []
            for c in range(4):
                cps.append(grec.op("vector", lambda E, c=c: E.tensor_copy(
                    gqB[:, c * 512:(c + 1) * 512],
                    psGQ[:, c * 512:(c + 1) * 512]), waits=(mms[c],)))
            mk = grec.op("tensor", lambda E: E.matmul(
                psGQ[:, 0:512], ones_row[:, 0:128], gkr[:, :],
                start=True, stop=True), waits=(g2, cps[0]))
            grec.op("vector", lambda E: E.tensor_copy(
                gkB[:, :], psGQ[:, 0:512]), waits=(mk,))
            grec.emit()

        entry = [("raw", tt1.h, tt1.n),
                 ("raw", ch_v.h, ch_v.n)]
        emit_l1_proj(nc, block, sems, xn2T, wnq1_in, wnk1_in, wnv1_in,
                     qfull, kfull, v1_out, v16, wchL, entry)
        esPJ.close()

        with (
            nc.sbuf_tensor("qbD", [128, D], F32) as qb,
            nc.sbuf_tensor("rotD", [128, D], F32) as rot,
            nc.sbuf_tensor("tqD", [128, 2 * D], F32) as tq,
            nc.sbuf_tensor("tkD", [128, 2 * 512], F32) as tk,
            nc.sbuf_tensor("qssD", [128, 16], F32) as qss,
            nc.sbuf_tensor("kssD", [128, 4], F32) as kss,
            nc.sbuf_tensor("qb16D", [128, D], BF16) as qb16,
            nc.sbuf_tensor("kb16D", [128, 512], BF16) as kb16,
            nc.sbuf_tensor("kbD", [128, 1024], F32) as kb,
            nc.sbuf_tensor("TTD", [128, T_LOC], BF16) as TT,
        ):
            nrentry = [("raw", ch_v.h, ch_v.n),
                       ("raw", ch_s.h, ch_s.n)]
            nrbufs = {"qb": qb, "rot": rot, "tq": tq, "tk": tk,
                      "qss": qss, "kss": kss, "qb16": qb16, "kb16": kb16,
                      "kb": kb, "TT": TT}
            emit_l1_normrope(nc, block, sems, qfull, kfull, gqB, gkB,
                             tabs, qnat, knat, qt1_out, kt1_out, nrbufs,
                             nrentry)

        es10.close()
        @block.sync
        def _(sync):
            for s_ in (sems["ob"], sems["oe"], sems["ch_v"]):
                sync.wait_ge(s_.h, s_.n)
            for r in range(4):
                sync.wait_ge(sems["sd%d" % r].h, sems["sd%d" % r].n)
    return nc



def build_p3():
    nc = bass.Bass("TRN2", target_bir_lowering=False)
    x4_in = nc.dram_tensor("x4", [T_LOC, D], F32, kind="ExternalInput").ap()
    partials = nc.dram_tensor("partials", [8, len(QW_NAMES)], F32, kind="ExternalInput").ap()
    qt_in = nc.dram_tensor("qt1", [D, T_LOC], BF16, kind="ExternalInput").ap()
    ktf_in = nc.dram_tensor("ktf1", [512, SEQ], BF16, kind="ExternalInput").ap()
    vf_in = nc.dram_tensor("vf1", [SEQ, 512], BF16, kind="ExternalInput").ap()
    mask_in = nc.dram_tensor("mask", [128, NT * 512], F32, kind="ExternalInput").ap()
    wno1_in = nc.dram_tensor("wnat_o1", [D, D], BF16, kind="ExternalInput").ap()
    wt_w1_in = nc.dram_tensor("wt_L1w1", [D, F_L1], BF16, kind="ExternalInput").ap()
    wt_w2_in = nc.dram_tensor("wt_L1w2", [F_L1, D], BF16, kind="ExternalInput").ap()
    x6_out = nc.dram_tensor("x6", [T_LOC, D], F32, kind="ExternalOutput").ap()

    onat = nc.dram_tensor("onat", [T_LOC, D], BF16).ap()
    k1nat = nc.dram_tensor("k1nat", [T_LOC, D], BF16).ap()
    g_dram = nc.dram_tensor("g_dram", [T_LOC, F_L1], F32).ap()
    k2nat = nc.dram_tensor("k2nat", [T_LOC, F_L1], BF16).ap()

    sem_names = ["ld0", "ch_v", "ch_s", "ch_p", "mm", "wl0", "wl1", "tt0",
                 "tt1", "ue_s", "ue_v", "gst", "gr0", "gr1", "kn", "k2t0",
                 "k2t1", "wd0", "wd1", "dn_ev", "ob", "sd0", "sd1", "sd2",
                 "sd3", "gd0", "gd1", "gd2", "gd3", "kv0", "kv1", "oe"]

    import contextlib
    with contextlib.ExitStack() as st_:
        sems = {n: Sem(st_.enter_context(nc.semaphore(n))) for n in sem_names}
        consts = st_.enter_context(nc.sbuf_tensor("consts", [128, NCOL], F32))
        ones_col = st_.enter_context(nc.sbuf_tensor("ones_c", [128, 1], F32))
        ones_row = st_.enter_context(nc.sbuf_tensor("ones_r", [1, 128], F32))
        xs = st_.enter_context(nc.sbuf_tensor("xs", [128, NT * D], F32))
        scU = st_.enter_context(nc.sbuf_tensor("scU", [128, NT], F32))
        m2 = st_.enter_context(nc.sbuf_tensor("m2", [128, NT], F32))
        s2c = st_.enter_context(nc.sbuf_tensor("s2c", [128, NT], F32))
        sc2 = st_.enter_context(nc.sbuf_tensor("sc2", [128, NT], F32))
        block = st_.enter_context(nc.Block())

        ch_v, ch_s, mm = sems["ch_v"], sems["ch_s"], sems["mm"]
        ld0, kn = sems["ld0"], sems["kn"]

        @block.vector
        def _(vector):
            for name, val in (("neg1", -1.0), ("invsq", INV_SQRT_DH),
                              ("epsln", EPS_LN), ("epsq", EPS_Q), ("one", 1.0)):
                ch_v.inc(vector.memset(consts[:, CC[name]:CC[name] + 1], val), 1)

        counts = [QW_SHAPES[n][0] * QW_SHAPES[n][1] for n in QW_NAMES]
        emit_scales(nc, block, sems, partials, consts, ones_col, ones_row, counts)

        @block.gpsimd
        def _(gpsimd):
            for t in range(NT):
                ld0.inc(gpsimd.dma_start(out=xs[:, t * D:(t + 1) * D],
                                         in_=x4_in[t * 128:(t + 1) * 128, :]), 16)
        x_loaded = ld0.n

        esF = contextlib.ExitStack()
        qt_sb = esF.enter_context(nc.sbuf_tensor("qt_sb", [128, 16 * T_LOC], BF16))
        mask_sb = esF.enter_context(nc.sbuf_tensor("mask_sb", [128, NT * 512], F32))
        kbuf = esF.enter_context(nc.sbuf_tensor("kbuf", [128, 2 * SEQ], BF16))
        vbuf = esF.enter_context(nc.sbuf_tensor("vbuf", [128, 2 * 32 * 132], BF16))
        qsq = esF.enter_context(nc.sbuf_tensor("qsq", [128, T_LOC], F32))
        ksq = esF.enter_context(nc.sbuf_tensor("ksq", [128, SEQ], F32))
        kn2 = esF.enter_context(nc.sbuf_tensor("kn2", [1, SEQ], F32))
        qn2 = esF.enter_context(nc.sbuf_tensor("qn2", [1, T_LOC], F32))
        ptb = esF.enter_context(nc.sbuf_tensor("ptb", [128, 2 * 512], BF16))
        gsb = esF.enter_context(nc.sbuf_tensor("gsb", [128, 2 * 128], F32))
        obuf = esF.enter_context(nc.sbuf_tensor("obuf", [128, 2 * 128], BF16))
        qm2 = esF.enter_context(nc.sbuf_tensor("qm2", [1, 16], F32))
        km2 = esF.enter_context(nc.sbuf_tensor("km2", [1, 16], F32))
        nbrow = esF.enter_context(nc.sbuf_tensor("nbrow", [1, 16], F32))
        negB = esF.enter_context(nc.sbuf_tensor("negB", [128, 16], F32))
        rcolf = esF.enter_context(nc.sbuf_tensor("rcolf", [128, 2], F32))

        brec = Rec(nc, block, sems, "p3load")
        for h in range(16):
            brec.op("syncd", lambda E, h=h: E.dma_start(
                out=qt_sb[:, h * T_LOC:(h + 1) * T_LOC],
                in_=qt_in[h * 128:(h + 1) * 128, :]))
        brec.op("syncd", lambda E: E.dma_start(out=mask_sb[:, :], in_=mask_in[:, :]))
        brec.emit()

        entry = [("raw", sems["sd%d" % r].h, sems["sd%d" % r].n)
                 for r in range(4)] + [("raw", ch_v.h, ch_v.n)]
        fbufs = {"kbuf": kbuf, "vbuf": vbuf, "qsq": qsq, "ksq": ksq,
                 "kn2": kn2, "qn2": qn2, "ptb": ptb, "gsb": gsb,
                 "obuf": obuf, "qm2": qm2, "km2": km2, "nbrow": nbrow,
                 "negB": negB, "rcol": rcolf, "ones_row": ones_row,
                 "ones_col": ones_col}
        emit_flash(nc, block, sems, qt_sb, ktf_in, vf_in, mask_sb, onat,
                   16, 4, None, fbufs, entry)
        esF.close()
        flash_done = [("raw", sems["sd%d" % r].h, sems["sd%d" % r].n)
                      for r in range(4)]

        esO = contextlib.ExitStack()
        oT = esO.enter_context(nc.sbuf_tensor("oT", [128, 16 * T_LOC], BF16))
        wcho = esO.enter_context(nc.sbuf_tensor("wcho", [128, 2 * 16 * 512], BF16))

        @block.vector
        def _(vector):
            vector.wait_ge(ld0.h, x_loaded)
            ch_v.inc(vector.tensor_copy(scU[:, 0:1], scU[:, 0:1]), 1)
        emit_oproj_residual(nc, block, sems, onat, wno1_in, xs, oT, wcho,
                            flash_done + [("raw", ld0.h, x_loaded)])
        esO.close()
        x5_ready = (sems["ch_v"], ch_v.n)

        esK = contextlib.ExitStack()
        k1T = esK.enter_context(nc.sbuf_tensor("k1T", [128, 16 * T_LOC], BF16))
        esA = contextlib.ExitStack()
        sqs = esA.enter_context(nc.sbuf_tensor("sqs", [128, D], F32))
        ybuf = esA.enter_context(nc.sbuf_tensor("ybuf", [128, D], F32))
        kibuf = esA.enter_context(nc.sbuf_tensor("kibuf", [128, D], I32))
        dbuf = esA.enter_context(nc.sbuf_tensor("dbuf", [128, D], F32))
        t1buf = esA.enter_context(nc.sbuf_tensor("t1buf", [128, D], F32))
        nkbuf = esA.enter_context(nc.sbuf_tensor("nkbuf", [128, D], BF16))
        tcs = esA.enter_context(nc.sbuf_tensor("tcs", [128, 12], F32))
        emit_lnq_tiles(nc, block, sems, xs, k1nat, consts, tcs, sqs, ybuf,
                       kibuf, dbuf, t1buf, nkbuf, x5_ready,
                       quant=True, sc_out=scU, sc_wcol=4 * 2 + 2, xn_nat=None)
        tt0 = sems["tt0"]
        k1_stored = kn.n

        @block.sync
        def _(sync):
            sync.wait_ge(kn.h, k1_stored)
            for dt in range(16):
                tt0.inc(sync.dma_start_transpose(
                    out=k1T[:, dt * T_LOC:(dt + 1) * T_LOC],
                    in_=k1nat[:, dt * 128:(dt + 1) * 128]), 16)
        esA.close()

        esB = contextlib.ExitStack()
        wua = esB.enter_context(nc.sbuf_tensor("wua", [128, 2 * 16 * 512], BF16))
        gab = esB.enter_context(nc.sbuf_tensor("gab", [128, 2 * 512], F32))
        gbb2 = esB.enter_context(nc.sbuf_tensor("gbb2", [128, 2 * 512], F32))
        rmaxb = esB.enter_context(nc.sbuf_tensor("rmaxb", [128, 2], F32))
        emit_ffn_up_glu(nc, block, sems, k1T, wt_w1_in, g_dram, scU, m2,
                        Fh=F_L1, glu=False, silu=False,
                        bufs={"wua": wua, "wub": wua, "ga": gab, "gb": gbb2,
                              "rmax": rmaxb})
        esB.close()

        esC = contextlib.ExitStack()
        grl = esC.enter_context(nc.sbuf_tensor("grl", [128, 2 * 1024], F32))
        y2 = esC.enter_context(nc.sbuf_tensor("y2", [128, 1024], F32))
        ki2 = esC.enter_context(nc.sbuf_tensor("ki2", [128, 1024], I32))
        d2 = esC.enter_context(nc.sbuf_tensor("d2", [128, 1024], F32))
        t12 = esC.enter_context(nc.sbuf_tensor("t12", [128, 1024], F32))
        nk2 = esC.enter_context(nc.sbuf_tensor("nk2", [128, 1024], BF16))
        k2T = esC.enter_context(nc.sbuf_tensor("k2T", [128, 2 * F_L1], BF16))
        wdn = esC.enter_context(nc.sbuf_tensor("wdn", [128, 8 * D], BF16))
        emit_quant_g(nc, block, sems, g_dram, k2nat, m2, s2c, sc2, consts,
                     grl, y2, ki2, d2, t12, nk2, dn_wi=3, Fh=F_L1,
                     extra_half=True)
        emit_ffn_down(nc, block, sems, k2nat, wt_w2_in, xs, sc2, k2T, wdn,
                      Fh=F_L1, Eo=D)
        esC.close()
        esK.close()
        x6_ready_n = sems["dn_ev"].n

        ob = sems["ob"]

        @block.gpsimd
        def _(gpsimd):
            gpsimd.wait_ge(sems["dn_ev"].h, x6_ready_n)
            for t in range(NT):
                ob.inc(gpsimd.dma_start(out=x6_out[t * 128:(t + 1) * 128, :],
                                        in_=xs[:, t * D:(t + 1) * D]), 16)

        @block.sync
        def _(sync):
            sync.wait_ge(ob.h, ob.n)
            for r in range(4):
                sync.wait_ge(sems["sd%d" % r].h, sems["sd%d" % r].n)
    return nc



def _rope_tables(core, width, scale=1.0):
    k = core % 4
    pos = np.array([(k + 4 * (r // 128)) * 128 + r % 128 for r in range(T_LOC)],
                   dtype=np.float64)
    ang = 10000.0 ** (-np.arange(64) / 64.0)
    rad = pos[:, None] * ang[None, :]
    c = np.cos(rad)
    sn = np.sin(rad)
    nh = width // 128
    C = np.tile(np.concatenate([c, c], 1), (1, nh)) * scale
    S = np.tile(np.concatenate([sn, sn], 1), (1, nh)) * scale
    return (np.ascontiguousarray(C, dtype=np.float32),
            np.ascontiguousarray(S, dtype=np.float32))


def _build_mask(core):
    k = core % 4
    m = np.zeros((128, NT * 512), dtype=np.float32)
    for j in range(NT):
        tq = k + 4 * j
        for i in range(4):
            kt = 4 * j + i
            key_abs = kt * 128 + np.arange(128)[:, None]
            q_abs = tq * 128 + np.arange(128)[None, :]
            m[:, j * 512 + i * 128:j * 512 + (i + 1) * 128] = np.where(
                key_abs <= q_abs, 0.0, -1e30).astype(np.float32)
    return m


def kernel(L0_attn_x, L0_attn_q_w, L0_attn_k_w, L0_attn_v_w, L0_attn_o_w,
           L0_attn_gate_w, L0_attn_gate_b, L0_ffn_w_gate_up, L0_ffn_w_down,
           L1_attn_wq, L1_attn_wk, L1_attn_wv, L1_attn_wo, L1_attn_gq,
           L1_attn_gk, L1_ffn_w1, L1_ffn_w2):
    f32 = lambda a: np.ascontiguousarray(np.asarray(a), dtype=np.float32)
    x_full = f32(L0_attn_x).reshape(B, SEQ, D)
    ws = {"L0gu": f32(L0_ffn_w_gate_up), "L0dn": f32(L0_ffn_w_down),
          "L1w1": f32(L1_ffn_w1), "L1w2": f32(L1_ffn_w2)}
    wqkv0 = {"w_q": f32(L0_attn_q_w), "w_k": f32(L0_attn_k_w),
             "w_v": f32(L0_attn_v_w), "w_o": f32(L0_attn_o_w)}
    wl1 = {"w1q": f32(L1_attn_wq), "w1k": f32(L1_attn_wk),
           "w1v": f32(L1_attn_wv), "w1o": f32(L1_attn_wo)}

    nc0 = build_p0()
    in_maps0 = []
    for c in range(N_CORES):
        m = {}
        for n in QW_NAMES:
            r = QW_SHAPES[n][0] // 8
            m["w_" + n] = np.ascontiguousarray(ws[n][c * r:(c + 1) * r])
        in_maps0.append(m)
    res0 = run_bass_kernel_spmd(nc0, in_maps0, core_ids=list(range(N_CORES)))
    partials = np.concatenate([res0.results[c]["partials"]
                               for c in range(N_CORES)], 0)

    nc1 = build_p1()
    in_maps1 = []
    for c in range(N_CORES):
        sq = c // 4
        tiles = owned_tiles(c)
        xc = np.concatenate([x_full[sq, t * 128:(t + 1) * 128] for t in tiles], 0)
        im = {"x": np.ascontiguousarray(xc), "partials": partials,
              "w_L0gu": ws["L0gu"], "w_L0dn": ws["L0dn"],
              "w_L1w1": ws["L1w1"], "w_L1w2": ws["L1w2"]}
        im.update(wqkv0)
        im.update(wl1)
        in_maps1.append(im)
    res1 = run_bass_kernel_spmd(nc1, in_maps1, core_ids=list(range(N_CORES)))

    ktf, vf = {}, {}
    for sq in range(B):
        ktfull = np.zeros((D, SEQ), dtype=ml_dtypes.bfloat16)
        vfull = np.zeros((SEQ, D), dtype=ml_dtypes.bfloat16)
        for k in range(4):
            c = sq * 4 + k
            for j in range(NT):
                T = k + 4 * j
                ktfull[:, T * 128:(T + 1) * 128] = \
                    res1.results[c]["kt"][:, j * 128:(j + 1) * 128]
                vfull[T * 128:(T + 1) * 128, :] = \
                    res1.results[c]["v"][j * 128:(j + 1) * 128, :]
        ktf[sq] = ktfull
        vf[sq] = vfull

    nc2 = build_p2()
    in_maps2 = []
    for c in range(N_CORES):
        sq = c // 4
        qc, qs_ = _rope_tables(c, D)
        kc, ks_ = _rope_tables(c, 512, scale=INV_SQRT_DH)
        r1 = res1.results[c]
        in_maps2.append({
            "x1": r1["x1"], "partials": partials, "qt": r1["qt"],
            "ktf": ktf[sq], "vf": vf[sq], "mask": _build_mask(c),
            "gate_w": f32(L0_attn_gate_w),
            "gate_b": f32(L0_attn_gate_b).reshape(1, 128),
            "wt_L0gu": r1["wt_L0gu"], "wt_L0dn": r1["wt_L0dn"],
            "wt_o": r1["wt_o"], "wt_L1w1": r1["wt_L1w1"],
            "wt_L1w2": r1["wt_L1w2"], "wnat_q1": r1["wnat_q1"],
            "wnat_k1": r1["wnat_k1"], "wnat_v1": r1["wnat_v1"],
            "gq_row": f32(L1_attn_gq).reshape(1, D),
            "gk_row": f32(L1_attn_gk).reshape(1, 512),
            "tab_qc": qc, "tab_qs": qs_, "tab_kc": kc, "tab_ks": ks_,
        })
    res2 = run_bass_kernel_spmd(nc2, in_maps2, core_ids=list(range(N_CORES)))

    ktf1, vf1 = {}, {}
    for sq in range(B):
        ktfull = np.zeros((512, SEQ), dtype=ml_dtypes.bfloat16)
        vfull = np.zeros((SEQ, 512), dtype=ml_dtypes.bfloat16)
        for k in range(4):
            c = sq * 4 + k
            for j in range(NT):
                T = k + 4 * j
                ktfull[:, T * 128:(T + 1) * 128] = \
                    res2.results[c]["kt1"][:, j * 128:(j + 1) * 128]
                vfull[T * 128:(T + 1) * 128, :] = \
                    res2.results[c]["v1"][j * 128:(j + 1) * 128, :]
        ktf1[sq] = ktfull
        vf1[sq] = vfull

    nc3 = build_p3()
    in_maps3 = []
    for c in range(N_CORES):
        sq = c // 4
        r1 = res1.results[c]
        r2 = res2.results[c]
        in_maps3.append({
            "x4": r2["x4"], "partials": partials, "qt1": r2["qt1"],
            "ktf1": ktf1[sq], "vf1": vf1[sq], "mask": _build_mask(c),
            "wnat_o1": r1["wnat_o1"], "wt_L1w1": r1["wt_L1w1"],
            "wt_L1w2": r1["wt_L1w2"],
        })
    res3 = run_bass_kernel_spmd(nc3, in_maps3, core_ids=list(range(N_CORES)))

    out = np.zeros((B, SEQ, D), dtype=np.float32)
    for c in range(N_CORES):
        sq = c // 4
        x6 = res3.results[c]["x6"]
        for j, t in enumerate(owned_tiles(c)):
            out[sq, t * 128:(t + 1) * 128] = x6[j * 128:(j + 1) * 128]
    return out.reshape(np.asarray(L0_attn_x).shape)


# version 06
# speedup vs baseline: 1.8456x; 1.0330x over previous
import numpy as np
import ml_dtypes
import concourse.bass as bass
import concourse.mybir as mybir
from concourse.bass_utils import run_bass_kernel_spmd

F32 = mybir.dt.float32
BF16 = mybir.dt.bfloat16
I32 = mybir.dt.int32
AF = mybir.ActivationFunctionType
ALU = mybir.AluOpType
AX = mybir.AxisListType

N_CORES = 8
D = 2048
T_LOC = 1024
NT = T_LOC // 128
SEQ = 4096
B = 2
DH = 128
NH = 16
EPS_LN = 1e-5
EPS_Q = 1e-5
EPS_RMS = 1e-6
INV_SQRT_DH = float(1.0 / np.sqrt(128.0))

F_L0 = 8192
F_L1 = 8192
EL0 = 16384

QW_SHAPES = {
    "L0gu": (EL0, D),
    "L0dn": (D, F_L0),
    "L1w1": (F_L1, D),
    "L1w2": (D, F_L1),
}
QW_NAMES = ["L0gu", "L0dn", "L1w1", "L1w2"]


def owned_tiles(core):
    k = core % 4
    return [k + 4 * j for j in range(NT)]



class Sem:

    def __init__(self, handle):
        self.h = handle
        self.n = 0

    def inc(self, instr, k=1):
        instr.then_inc(self.h, k)
        self.n += k
        return self.n


class Ctx:
    def __init__(self, nc, block):
        self.nc = nc
        self.block = block
        self.sems = {}
        self._stack = None

    def sem(self, name):
        return self.sems[name]


def ceil_div(a, b):
    return (a + b - 1) // b


_UID = [0]


def _u(name):
    _UID[0] += 1
    return f"{name}_{_UID[0]}" 



def build_p0():
    nc = bass.Bass("TRN2", target_bir_lowering=False)
    ins = {}
    for name, (r, c) in QW_SHAPES.items():
        ins[name] = nc.dram_tensor("w_" + name, [r // 8, c], F32, kind="ExternalInput").ap()
    out = nc.dram_tensor("partials", [1, len(QW_NAMES)], F32, kind="ExternalOutput").ap()

    CH = 4096
    with (
        nc.sbuf_tensor("wbuf", [128, 2 * CH], F32) as wbuf,
        nc.sbuf_tensor("acc", [128, len(QW_NAMES)], F32) as acc,
        nc.sbuf_tensor("accs", [128, 1], F32) as accs,
        nc.sbuf_tensor("ones", [128, 1], F32) as ones,
        nc.sbuf_tensor("red", [128, 2], F32) as red,
        nc.sbuf_tensor("outt", [1, len(QW_NAMES)], F32) as outt,
        nc.psum_tensor("ps1", [1, len(QW_NAMES)], F32) as ps1,
        nc.semaphore("ld0") as ld0_h,
        nc.semaphore("ld1") as ld1_h,
        nc.semaphore("cm") as cm_h,
        nc.semaphore("ch") as ch_h,
        nc.semaphore("st") as st_h,
        nc.Block() as block,
    ):
        ld = [Sem(ld0_h), Sem(ld1_h)]
        cm = Sem(cm_h)
        ch = Sem(ch_h)
        st = Sem(st_h)

        loads = []
        for wi, name in enumerate(QW_NAMES):
            r, c = QW_SHAPES[name]
            r //= 8
            flat = ins[name].rearrange("r c -> (r c)")
            total = r * c
            assert total % 128 == 0
            per_row = total // 128
            view = flat.rearrange("(p f) -> p f", p=128)
            for off in range(0, per_row, CH):
                w = min(CH, per_row - off)
                loads.append((wi, view[:, off:off + w], w))

        @block.sync
        def _(sync):
            for i, (wi, ap, w) in enumerate(loads):
                buf = wbuf[:, (i % 2) * CH:(i % 2) * CH + w]
                if i >= 2:
                    sync.wait_ge(cm.h, i - 1)
                ld[i % 2].inc(sync.dma_start(out=buf, in_=ap), 16)

        @block.vector
        def _(vector):
            ch.inc(vector.memset(acc[:], 0.0), 1)
            vector.memset(ones[:], 1.0)
            for i, (wi, ap, w) in enumerate(loads):
                buf = wbuf[:, (i % 2) * CH:(i % 2) * CH + w]
                vector.wait_ge(ld[i % 2].h, 16 * (i // 2 + 1))
                rr = red[:, (i % 2):(i % 2) + 1]
                ch.inc(vector.tensor_reduce(out=rr, in_=buf, axis=AX.X, op=ALU.add,
                                            apply_absolute_value=True), 1)
                vector.wait_ge(ch.h, ch.n)
                cm.inc(vector.tensor_tensor(out=acc[:, wi:wi + 1], in0=acc[:, wi:wi + 1],
                                            in1=rr, op=ALU.add), 1)
            vector.wait_ge(cm.h, cm.n)
            cm.inc(vector.tensor_copy(accs[:], acc[:, 0:1]), 1)

        @block.tensor
        def _(tensor):
            tensor.wait_ge(cm.h, len(loads) + 1)
            tensor.matmul(ps1[:, :], ones[:, :], acc[:, :], start=True, stop=True).then_inc(cm.h, 1)
        cm.n += 1

        @block.vector
        def _(vector):
            vector.wait_ge(cm.h, len(loads) + 2)
            cm.inc(vector.tensor_copy(outt[:], ps1[:, :]), 1)

        @block.sync
        def _(sync):
            sync.wait_ge(cm.h, len(loads) + 3)
            st.inc(sync.dma_start(out=out[:, :], in_=outt[:]), 16)
            sync.wait_ge(st.h, 16)
    return nc




def emit_scales(nc, block, sems, partials_ap, consts, ones_col, ones_row, counts):
    ld, ch = sems["ld0"], sems["ch_v"]
    mm = sems["mm"]
    NW = len(QW_NAMES)

    with (
        nc.sbuf_tensor(_u("sc_part"), [8, NW], F32) as part,
        nc.sbuf_tensor(_u("sc_row"), [1, NW], F32) as row,
        nc.sbuf_tensor(_u("sc_cnt"), [1, NW], F32) as cnt,
        nc.psum_tensor(_u("sc_ps"), [1, NW], F32) as ps,
        nc.psum_tensor(_u("sc_psb"), [128, NW], F32) as psb,
    ):
        @block.sync
        def _(sync):
            ld.inc(sync.dma_start(out=part[:, :], in_=partials_ap), 16)

        @block.vector
        def _(vector):
            for wi, name in enumerate(QW_NAMES):
                ch.inc(vector.memset(cnt[:, wi:wi + 1], 1.0 / counts[wi]), 1)
            ch.inc(vector.memset(ones_col[:], 1.0), 1)
            ch.inc(vector.memset(ones_row[:], 1.0), 1)

        @block.tensor
        def _(tensor):
            tensor.wait_ge(ld.h, ld.n)
            tensor.wait_ge(ch.h, ch.n)
            mm.inc(tensor.matmul(ps[:, :], ones_col[0:8, :], part[:, :],
                                 start=True, stop=True), 1)

        @block.vector
        def _(vector):
            vector.wait_ge(mm.h, mm.n)
            ch.inc(vector.tensor_tensor(out=row[:, :], in0=ps[:, :], in1=cnt[:, :],
                                        op=ALU.mult), 1)
            vector.wait_ge(ch.h, ch.n)
            ch.inc(vector.tensor_scalar(out=row[:, :], in0=row[:, :], scalar1=EPS_Q,
                                        scalar2=None, op0=ALU.max), 1)

        @block.tensor
        def _(tensor):
            tensor.wait_ge(ch.h, ch.n)
            mm.inc(tensor.matmul(psb[:, :], ones_row[:, 0:128], row[:, :],
                                 start=True, stop=True), 1)

        @block.vector
        def _(vector):
            vector.wait_ge(mm.h, mm.n)
            for wi in range(NW):
                src = psb[:, wi:wi + 1]
                ch.inc(vector.tensor_scalar(out=consts[:, 4 * wi:4 * wi + 1], in0=src,
                                            scalar1=0.5, scalar2=None, op0=ALU.mult), 1)
                ch.inc(vector.tensor_scalar(out=consts[:, 4 * wi + 1:4 * wi + 2], in0=src,
                                            scalar1=-0.5, scalar2=None, op0=ALU.mult), 1)
                ch.inc(vector.tensor_scalar(out=consts[:, 4 * wi + 2:4 * wi + 3], in0=src,
                                            scalar1=1.0 / 254.0, scalar2=None, op0=ALU.mult), 1)
                ch.inc(vector.tensor_copy(consts[:, 4 * wi + 3:4 * wi + 4], src), 1)


def emit_weight_prep(nc, block, sems, w_in, wnat_dram, wt_out, R, C, ternary,
                     consts, wi, bufs, flat_rows, passA_cols, wnat_geo,
                     do_transpose=True):
    ld = [sems["wp_ld0"], sems["wp_ld1"]]
    sg = sems["wp_sg"]
    cv = [sems["wp_cv0"], sems["wp_cv1"]]
    stg = [sems["wp_st0"], sems["wp_st1"]]
    tt = [sems["wp_tt0"], sems["wp_tt1"]]
    wo = [sems["wp_wo0"], sems["wp_wo1"]]
    wpf, sg1, sg2, wpb, wtt = (bufs["wpf"], bufs["sg1"], bufs["sg2"],
                               bufs["wpb"], bufs["wtt"])
    CW = passA_cols
    nrb = flat_rows // 128
    nct = C // 128
    wtt_cols = wtt.shape[1]
    dbl_tt = (2 * R <= wtt_cols)

    ldb = [ld[0].n, ld[1].n]
    sgb = sg.n
    cvb = [cv[0].n, cv[1].n]
    stb = [stg[0].n, stg[1].n]
    ttb = [tt[0].n, tt[1].n]
    wob = [wo[0].n, wo[1].n]

    @block.sync
    def _(sync):
        for rb in range(nrb):
            p = rb % 2
            if rb >= 2:
                sync.wait_ge(cv[p].h, cvb[p] + (rb // 2))
            sync.dma_start(out=wpf[:, p * CW:p * CW + CW],
                           in_=w_in[rb * 128:(rb + 1) * 128, :]).then_inc(ld[p].h, 16)
    ld[0].n += 16 * ((nrb + 1) // 2)
    ld[1].n += 16 * (nrb // 2)

    if ternary:
        @block.scalar
        def _(scalar):
            for rb in range(nrb):
                p = rb % 2
                scalar.wait_ge(ld[p].h, ldb[p] + 16 * (rb // 2 + 1))
                if rb >= 1:
                    scalar.wait_ge(cv[0].h if (rb - 1) % 2 == 0 else cv[1].h,
                                   (cvb[0] + (rb - 1) // 2 + 1) if (rb - 1) % 2 == 0
                                   else (cvb[1] + (rb - 1) // 2 + 1))
                src = wpf[:, p * CW:p * CW + CW]
                scalar.activation(out=sg1[:, :], in_=src, func=AF.Sign,
                                  bias=consts[:, 4 * wi:4 * wi + 1],
                                  scale=consts[:, CC["neg1"]:CC["neg1"] + 1])
                scalar.activation(out=sg2[:, :], in_=src, func=AF.Sign,
                                  bias=consts[:, 4 * wi + 1:4 * wi + 2],
                                  scale=consts[:, CC["neg1"]:CC["neg1"] + 1]
                                  ).then_inc(sg.h, 1)
        sg.n += nrb

        @block.vector
        def _(vector):
            for rb in range(nrb):
                p = rb % 2
                vector.wait_ge(sg.h, sgb + rb + 1)
                if rb >= 2:
                    vector.wait_ge(stg[p].h, stb[p] + 16 * (rb // 2))
                vector.tensor_tensor(out=wpb[:, p * CW:p * CW + CW],
                                     in0=sg1[:, :], in1=sg2[:, :],
                                     op=ALU.add).then_inc(cv[p].h, 1)
    else:
        @block.vector
        def _(vector):
            for rb in range(nrb):
                p = rb % 2
                vector.wait_ge(ld[p].h, ldb[p] + 16 * (rb // 2 + 1))
                if rb >= 2:
                    vector.wait_ge(stg[p].h, stb[p] + 16 * (rb // 2))
                vector.tensor_copy(wpb[:, p * CW:p * CW + CW],
                                   wpf[:, p * CW:p * CW + CW]).then_inc(cv[p].h, 1)
    cv[0].n += (nrb + 1) // 2
    cv[1].n += nrb // 2

    @block.gpsimd
    def _(gpsimd):
        for rb in range(nrb):
            p = rb % 2
            gpsimd.wait_ge(cv[p].h, cvb[p] + (rb // 2 + 1))
            gpsimd.dma_start(out=wnat_dram[rb * 128:(rb + 1) * 128, :],
                             in_=wpb[:, p * CW:p * CW + CW]).then_inc(stg[p].h, 16)
    stg[0].n += 16 * ((nrb + 1) // 2)
    stg[1].n += 16 * (nrb // 2)

    if not do_transpose:
        return

    wtt_cols = wtt.shape[1]
    dblB = (2 * R <= wtt_cols)
    @block.sync
    def _(sync):
        sync.wait_ge(stg[0].h, stg[0].n)
        sync.wait_ge(stg[1].h, stg[1].n)
        for ct in range(nct):
            if dblB:
                p = ct % 2
                if ct >= 2:
                    sync.wait_ge(wo[p].h, wob[p] + 16 * (ct // 2))
                elif ct == 0:
                    sync.wait_ge(wo[0].h, wob[0])
                    sync.wait_ge(wo[1].h, wob[1])
                dst = wtt[:, p * R:(p + 1) * R]
            else:
                p = 0
                sync.wait_ge(wo[0].h, wob[0] + 16 * ct)
                dst = wtt[:, 0:R]
            sync.dma_start_transpose(out=dst,
                                     in_=wnat_geo[:, ct * 128:(ct + 1) * 128]
                                     ).then_inc(tt[p].h, 16)
            sync.wait_ge(tt[p].h, (ttb[p] + 16 * (ct // 2 + 1)) if dblB
                         else (ttb[0] + 16 * (ct + 1)))
            sync.dma_start(out=wt_out[ct * 128:(ct + 1) * 128, :],
                           in_=dst).then_inc(wo[p].h, 16)
    if dblB:
        tt[0].n += 16 * ((nct + 1) // 2)
        tt[1].n += 16 * (nct // 2)
        wo[0].n += 16 * ((nct + 1) // 2)
        wo[1].n += 16 * (nct // 2)
    else:
        tt[0].n += 16 * nct
        wo[0].n += 16 * nct


def emit_lnq_tiles(nc, block, sems, xs, k1nat, consts, tc, sqs, ybuf, kibuf,
                   dbuf, t1buf, nkbuf, ready, quant, sc_out, sc_wcol, xn_nat):
    ch_v, ch_s, kn = sems["ch_v"], sems["ch_s"], sems["kn"]
    rlist = ready if isinstance(ready, list) else [ready]
    invD = 1.0 / D

    for t in range(NT):
        xt = xs[:, t * D:(t + 1) * D]

        @block.vector
        def _(vector, t=t, xt=xt):
            for rsem, rcnt in rlist:
                vector.wait_ge(rsem.h, rcnt)
            ch_v.inc(vector.tensor_reduce(out=tc[:, 0:1], in_=xt, axis=AX.X,
                                          op=ALU.add), 1)
            ch_v.inc(vector.tensor_reduce(out=tc[:, 3:4], in_=xt, axis=AX.X,
                                          op=ALU.max), 1)
            ch_v.inc(vector.tensor_reduce(out=tc[:, 4:5], in_=xt, axis=AX.X,
                                          op=ALU.min), 1)
            vector.wait_ge(ch_v.h, ch_v.n)
            ch_v.inc(vector.tensor_scalar(out=tc[:, 0:1], in0=tc[:, 0:1],
                                          scalar1=invD, scalar2=None,
                                          op0=ALU.mult), 1)
            ch_v.inc(vector.memset(tc[:, 1:2], 0.0), 1)

        @block.scalar
        def _(scalar, t=t, xt=xt):
            for rsem, rcnt in rlist:
                scalar.wait_ge(rsem.h, rcnt)
            scalar.wait_ge(ch_v.h, ch_v.n)
            ch_s.inc(scalar.activation(out=sqs[:, :], in_=xt, func=AF.Square,
                                       accum_out=tc[:, 1:2]), 1)

        @block.vector
        def _(vector, t=t):
            vector.wait_ge(ch_s.h, ch_s.n)
            vector.wait_ge(ch_v.h, ch_v.n)
            ch_v.inc(vector.tensor_tensor(out=tc[:, 5:6], in0=tc[:, 0:1],
                                          in1=tc[:, 0:1], op=ALU.mult), 1)
            ch_v.inc(vector.tensor_scalar(out=tc[:, 1:2], in0=tc[:, 1:2],
                                          scalar1=invD, scalar2=None,
                                          op0=ALU.mult), 1)
            vector.wait_ge(ch_v.h, ch_v.n)
            ch_v.inc(vector.tensor_tensor(out=tc[:, 1:2], in0=tc[:, 1:2],
                                          in1=tc[:, 5:6], op=ALU.subtract), 1)

        @block.scalar
        def _(scalar, t=t):
            scalar.wait_ge(ch_v.h, ch_v.n)
            ch_s.inc(scalar.activation(out=tc[:, 5:6], in_=tc[:, 1:2],
                                       func=AF.Sqrt,
                                       bias=consts[:, CC["epsln"]:CC["epsln"] + 1]), 1)

        @block.vector
        def _(vector, t=t, xt=xt):
            vector.wait_ge(ch_s.h, ch_s.n)
            ch_v.inc(vector.reciprocal(tc[:, 2:3], tc[:, 5:6]), 1)
            vector.wait_ge(ch_v.h, ch_v.n)
            if quant:
                ch_v.inc(vector.tensor_tensor(out=tc[:, 3:4], in0=tc[:, 3:4],
                                              in1=tc[:, 0:1], op=ALU.subtract), 1)
                ch_v.inc(vector.tensor_tensor(out=tc[:, 4:5], in0=tc[:, 0:1],
                                              in1=tc[:, 4:5], op=ALU.subtract), 1)
                vector.wait_ge(ch_v.h, ch_v.n)
                ch_v.inc(vector.tensor_tensor(out=tc[:, 3:4], in0=tc[:, 3:4],
                                              in1=tc[:, 4:5], op=ALU.max), 1)
                vector.wait_ge(ch_v.h, ch_v.n)
                ch_v.inc(vector.tensor_tensor(out=tc[:, 3:4], in0=tc[:, 3:4],
                                              in1=tc[:, 2:3], op=ALU.mult), 1)
                vector.wait_ge(ch_v.h, ch_v.n)
                ch_v.inc(vector.tensor_scalar(out=tc[:, 3:4], in0=tc[:, 3:4],
                                              scalar1=EPS_Q, scalar2=None,
                                              op0=ALU.max), 1)
                vector.wait_ge(ch_v.h, ch_v.n)
                ch_v.inc(vector.tensor_tensor(out=sc_out[:, t:t + 1],
                                              in0=tc[:, 3:4],
                                              in1=consts[:, sc_wcol:sc_wcol + 1],
                                              op=ALU.mult), 1)
                ch_v.inc(vector.reciprocal(tc[:, 6:7], tc[:, 3:4]), 1)
                vector.wait_ge(ch_v.h, ch_v.n)
                ch_v.inc(vector.tensor_scalar(out=tc[:, 6:7], in0=tc[:, 6:7],
                                              scalar1=127.0, scalar2=None,
                                              op0=ALU.mult), 1)
                vector.wait_ge(ch_v.h, ch_v.n)
                ch_v.inc(vector.tensor_tensor(out=tc[:, 7:8], in0=tc[:, 2:3],
                                              in1=tc[:, 6:7], op=ALU.mult), 1)
                vector.wait_ge(ch_v.h, ch_v.n)
                ch_v.inc(vector.scalar_tensor_tensor(out=tc[:, 8:9],
                                                     in0=tc[:, 0:1], scalar=-1.0,
                                                     in1=tc[:, 7:8],
                                                     op0=ALU.mult, op1=ALU.mult), 1)
            else:
                ch_v.inc(vector.tensor_copy(tc[:, 7:8], tc[:, 2:3]), 1)
                ch_v.inc(vector.scalar_tensor_tensor(out=tc[:, 8:9],
                                                     in0=tc[:, 0:1], scalar=-1.0,
                                                     in1=tc[:, 2:3],
                                                     op0=ALU.mult, op1=ALU.mult), 1)

        @block.scalar
        def _(scalar, t=t, xt=xt):
            scalar.wait_ge(ch_v.h, ch_v.n)
            out = ybuf[:, :] if quant else nkbuf[:, :]
            ch_s.inc(scalar.activation(out=out, in_=xt, func=AF.Identity,
                                       scale=tc[:, 7:8], bias=tc[:, 8:9]), 1)

        if quant:
            @block.vector
            def _(vector, t=t):
                vector.wait_ge(ch_s.h, ch_s.n)
                if t >= 1:
                    vector.wait_ge(kn.h, kn.n)
                ch_v.inc(vector.tensor_copy(kibuf[:, :], ybuf[:, :]), 1)
                vector.wait_ge(ch_v.h, ch_v.n)
                ch_v.inc(vector.tensor_tensor(out=dbuf[:, :], in0=ybuf[:, :],
                                              in1=kibuf[:, :], op=ALU.subtract), 1)
                vector.wait_ge(ch_v.h, ch_v.n)
                ch_v.inc(vector.scalar_tensor_tensor(out=t1buf[:, :], in0=dbuf[:, :],
                                                     scalar=0.5, in1=kibuf[:, :],
                                                     op0=ALU.is_ge, op1=ALU.add), 1)
                vector.wait_ge(ch_v.h, ch_v.n)
                ch_v.inc(vector.scalar_tensor_tensor(out=nkbuf[:, :], in0=dbuf[:, :],
                                                     scalar=-0.5, in1=t1buf[:, :],
                                                     op0=ALU.is_le, op1=ALU.subtract), 1)

            @block.gpsimd
            def _(gpsimd, t=t):
                gpsimd.wait_ge(ch_v.h, ch_v.n)
                kn.inc(gpsimd.dma_start(out=k1nat[t * 128:(t + 1) * 128, :],
                                        in_=nkbuf[:, :]), 16)
        else:
            @block.gpsimd
            def _(gpsimd, t=t):
                gpsimd.wait_ge(ch_s.h, ch_s.n)
                kn.inc(gpsimd.dma_start(out=xn_nat[t * 128:(t + 1) * 128, :],
                                        in_=nkbuf[:, :]), 16)


def emit_ffn_up_glu(nc, block, sems, k1T, wt_up, g_dram, scU, m2, Fh, glu,
                    silu, bufs):
    wl = [sems["wl0"], sems["wl1"]]
    mm, ue_s, ue_v, gst = sems["mm"], sems["ue_s"], sems["ue_v"], sems["gst"]
    ch_v = sems["ch_v"]
    nch = Fh // 512
    KT = 16
    wlb = [wl[0].n, wl[1].n]
    mmb = mm.n
    usb, uvb = ue_s.n, ue_v.n
    gsb = gst.n
    wua, wub, ga, gb, rmax = (bufs["wua"], bufs["wub"], bufs["ga"], bufs["gb"],
                              bufs["rmax"])

    with (
        nc.psum_tensor(_u("ps_upA"), [128, 2 * 512], F32) as psA,
        nc.psum_tensor(_u("ps_upB"), [128, 2 * 512], F32) as psB,
    ):
        @block.vector
        def _(vector):
            ch_v.inc(vector.memset(m2[:, :], 0.0), 1)
        m2_init = ch_v.n

        per_c = 32 if glu else 16

        @block.sync
        def _(sync):
            sync.wait_ge(sems["tt0"].h, sems["tt0"].n)
            for c in range(nch):
                p = c % 2
                if c >= 2:
                    sync.wait_ge(ue_v.h, uvb + (c - 1) * NT)
                ap_a = wt_up[:, c * 512:(c + 1) * 512].rearrange(
                    "(dt q) f -> q dt f", q=128)
                sync.dma_start(out=wua[:, p * KT * 512:(p + 1) * KT * 512]
                               .rearrange("q (dt f) -> q dt f", f=512),
                               in_=ap_a).then_inc(wl[p].h, 16)
                if glu:
                    ap_b = wt_up[:, Fh + c * 512:Fh + (c + 1) * 512].rearrange(
                        "(dt q) f -> q dt f", q=128)
                    sync.dma_start(out=wub[:, p * KT * 512:(p + 1) * KT * 512]
                                   .rearrange("q (dt f) -> q dt f", f=512),
                                   in_=ap_b).then_inc(wl[p].h, 16)
        wl[0].n += per_c * ((nch + 1) // 2)
        wl[1].n += per_c * (nch // 2)

        @block.tensor
        def _(tensor):
            tensor.wait_ge(sems["ch_v"].h, m2_init)
            for c in range(nch):
                p = c % 2
                tensor.wait_ge(wl[p].h, wlb[p] + per_c * (c // 2 + 1))
                for t in range(NT):
                    q = t % 2
                    u = c * NT + t
                    if u >= 2:
                        tensor.wait_ge(ue_s.h, usb + u - 1)
                        tensor.wait_ge(ue_v.h, uvb + u - 1)
                    last = None
                    for dt in range(KT):
                        stat = k1T[:, dt * T_LOC + t * 128:dt * T_LOC + (t + 1) * 128]
                        mva = wua[:, p * KT * 512 + dt * 512:p * KT * 512 + (dt + 1) * 512]
                        last = tensor.matmul(psA[:, q * 512:(q + 1) * 512], stat,
                                             mva, start=(dt == 0), stop=(dt == KT - 1))
                    if glu:
                        for dt in range(KT):
                            stat = k1T[:, dt * T_LOC + t * 128:dt * T_LOC + (t + 1) * 128]
                            mvb = wub[:, p * KT * 512 + dt * 512:p * KT * 512 + (dt + 1) * 512]
                            last = tensor.matmul(psB[:, q * 512:(q + 1) * 512], stat,
                                                 mvb, start=(dt == 0), stop=(dt == KT - 1))
                    mm.inc(last, 1)

        @block.scalar
        def _(scalar):
            for c in range(nch):
                for t in range(NT):
                    q = t % 2
                    u = c * NT + t
                    scalar.wait_ge(mm.h, mmb + u + 1)
                    if u >= 2:
                        scalar.wait_ge(ue_v.h, uvb + u - 1)
                    fn = AF.Silu if silu else AF.Gelu
                    scalar.activation(out=ga[:, q * 512:(q + 1) * 512],
                                      in_=psA[:, q * 512:(q + 1) * 512],
                                      func=fn,
                                      scale=scU[:, t:t + 1]).then_inc(ue_s.h, 1)
        ue_s.n += nch * NT

        @block.vector
        def _(vector):
            vector.wait_ge(ch_v.h, m2_init)
            for c in range(nch):
                for t in range(NT):
                    q = t % 2
                    u = c * NT + t
                    vector.wait_ge(ue_s.h, usb + u + 1)
                    if u >= 2:
                        vector.wait_ge(gst.h, gsb + 16 * (u - 1))
                    gbq = gb[:, q * 512:(q + 1) * 512]
                    if glu:
                        vector.tensor_scalar(out=gbq,
                                             in0=psB[:, q * 512:(q + 1) * 512],
                                             scalar1=scU[:, t:t + 1], scalar2=None,
                                             op0=ALU.mult).then_inc(ch_v.h, 1)
                        ch_v.n += 1
                        vector.wait_ge(ch_v.h, ch_v.n)
                        vector.tensor_tensor(out=gbq,
                                             in0=ga[:, q * 512:(q + 1) * 512],
                                             in1=gbq,
                                             op=ALU.mult).then_inc(ch_v.h, 1)
                        ch_v.n += 1
                    else:
                        vector.tensor_copy(gbq, ga[:, q * 512:(q + 1) * 512]
                                           ).then_inc(ch_v.h, 1)
                        ch_v.n += 1
                    vector.wait_ge(ch_v.h, ch_v.n)
                    vector.tensor_reduce(out=rmax[:, q:q + 1], in_=gbq,
                                         axis=AX.X, op=ALU.max,
                                         apply_absolute_value=True).then_inc(ch_v.h, 1)
                    ch_v.n += 1
                    vector.wait_ge(ch_v.h, ch_v.n)
                    vector.tensor_tensor(out=m2[:, t:t + 1], in0=m2[:, t:t + 1],
                                         in1=rmax[:, q:q + 1],
                                         op=ALU.max).then_inc(ue_v.h, 1)
        ue_v.n += nch * NT

        @block.gpsimd
        def _(gpsimd):
            for c in range(nch):
                for t in range(NT):
                    q = t % 2
                    u = c * NT + t
                    gpsimd.wait_ge(ue_v.h, uvb + u + 1)
                    gpsimd.dma_start(out=g_dram[t * 128:(t + 1) * 128,
                                                c * 512:(c + 1) * 512],
                                     in_=gb[:, q * 512:(q + 1) * 512]
                                     ).then_inc(gst.h, 16)
        gst.n += 16 * nch * NT


def emit_quant_g(nc, block, sems, g_dram, k2nat, m2, s2c, sc2, consts, grl,
                 y2, ki2, d2, t12, nk2, dn_wi, Fh, extra_half):
    ch_v, ch_s, kn = sems["ch_v"], sems["ch_s"], sems["kn"]
    gr = [sems["gr0"], sems["gr1"]]
    gst = sems["gst"]

    @block.vector
    def _(vector):
        vector.wait_ge(sems["ue_v"].h, sems["ue_v"].n)
        ch_v.inc(vector.tensor_scalar(out=m2[:, :], in0=m2[:, :], scalar1=EPS_Q,
                                      scalar2=None, op0=ALU.max), 1)
        vector.wait_ge(ch_v.h, ch_v.n)
        ch_v.inc(vector.reciprocal(s2c[:, :], m2[:, :]), 1)
        vector.wait_ge(ch_v.h, ch_v.n)
        ch_v.inc(vector.tensor_scalar(out=s2c[:, :], in0=s2c[:, :], scalar1=127.0,
                                      scalar2=None, op0=ALU.mult), 1)
        hf = 0.5 if extra_half else 1.0
        ch_v.inc(vector.tensor_scalar(out=sc2[:, :], in0=m2[:, :], scalar1=hf,
                                      scalar2=None, op0=ALU.mult), 1)
        vector.wait_ge(ch_v.h, ch_v.n)
        for t in range(NT):
            ch_v.inc(vector.tensor_tensor(out=sc2[:, t:t + 1], in0=sc2[:, t:t + 1],
                                          in1=consts[:, 4 * dn_wi + 2:4 * dn_wi + 3],
                                          op=ALU.mult), 1)

    grb = [gr[0].n, gr[1].n]
    knb = kn.n
    nsec = Fh // 1024
    units = [(t, sec) for t in range(NT) for sec in range(nsec)]

    @block.sync
    def _(sync):
        sync.wait_ge(gst.h, gst.n)
        for i, (t, sec) in enumerate(units):
            p = i % 2
            if i >= 2:
                sync.wait_ge(kn.h, knb + 16 * (i - 1))
            sync.dma_start(out=grl[:, p * 1024:(p + 1) * 1024],
                           in_=g_dram[t * 128:(t + 1) * 128,
                                      sec * 1024:(sec + 1) * 1024]
                           ).then_inc(gr[p].h, 16)
    gr[0].n += 16 * ((len(units) + 1) // 2)
    gr[1].n += 16 * (len(units) // 2)

    for i, (t, sec) in enumerate(units):
        p = i % 2

        @block.scalar
        def _(scalar, i=i, t=t, p=p):
            scalar.wait_ge(gr[p].h, grb[p] + 16 * (i // 2 + 1))
            scalar.wait_ge(ch_v.h, ch_v.n)
            scalar.activation(out=y2[:, :], in_=grl[:, p * 1024:(p + 1) * 1024],
                              func=AF.Copy, scale=s2c[:, t:t + 1]).then_inc(ch_s.h, 1)
        ch_s.n += 1

        @block.vector
        def _(vector, i=i):
            vector.wait_ge(ch_s.h, ch_s.n)
            if i >= 1:
                vector.wait_ge(kn.h, knb + 16 * i)
            ch_v.inc(vector.tensor_copy(ki2[:, :], y2[:, :]), 1)
            vector.wait_ge(ch_v.h, ch_v.n)
            ch_v.inc(vector.tensor_tensor(out=d2[:, :], in0=y2[:, :], in1=ki2[:, :],
                                          op=ALU.subtract), 1)
            vector.wait_ge(ch_v.h, ch_v.n)
            ch_v.inc(vector.scalar_tensor_tensor(out=t12[:, :], in0=d2[:, :],
                                                 scalar=0.5, in1=ki2[:, :],
                                                 op0=ALU.is_ge, op1=ALU.add), 1)
            vector.wait_ge(ch_v.h, ch_v.n)
            ch_v.inc(vector.scalar_tensor_tensor(out=nk2[:, :], in0=d2[:, :],
                                                 scalar=-0.5, in1=t12[:, :],
                                                 op0=ALU.is_le, op1=ALU.subtract), 1)

        @block.gpsimd
        def _(gpsimd, i=i, t=t, sec=sec):
            gpsimd.wait_ge(ch_v.h, ch_v.n)
            kn.inc(gpsimd.dma_start(out=k2nat[t * 128:(t + 1) * 128,
                                              sec * 1024:(sec + 1) * 1024],
                                    in_=nk2[:, :]), 16)


def emit_ffn_down(nc, block, sems, k2nat, wt_dn, xs, sc2, k2T, wdn, Fh,
                  Eo=D):
    k2t = [sems["k2t0"], sems["k2t1"]]
    wd = [sems["wd0"], sems["wd1"]]
    mm, dn_ev, kn = sems["mm"], sems["dn_ev"], sems["kn"]
    ch_v = sems["ch_v"]
    nsec = Fh // 1024
    nec = Eo // 512
    k2b = [k2t[0].n, k2t[1].n]
    wdb = [wd[0].n, wd[1].n]
    mmb = mm.n
    evb = dn_ev.n
    NU = nsec * 4

    with nc.psum_tensor(_u("ps_dn"), [128, 8 * 512], F32) as psD:
        @block.sync
        def _(sync):
            sync.wait_ge(kn.h, kn.n)
            for sec in range(nsec):
                p = sec % 2
                if sec >= 2:
                    sync.wait_ge(dn_ev.h, evb + 8 * 4 * (sec - 1))
                for ft in range(8):
                    sync.dma_start_transpose(
                        out=k2T[:, p * Fh + ft * T_LOC:p * Fh + (ft + 1) * T_LOC],
                        in_=k2nat[:, (sec * 8 + ft) * 128:(sec * 8 + ft + 1) * 128]
                    ).then_inc(k2t[p].h, 16)
        k2t[0].n += 16 * 8 * ((nsec + 1) // 2)
        k2t[1].n += 16 * 8 * (nsec // 2)

        @block.gpsimd
        def _(gpsimd):
            gpsimd.wait_ge(kn.h, kn.n)
            for sec in range(nsec):
                if sec >= 1:
                    gpsimd.wait_ge(mm.h, mmb + 32 * sec)
                rows = wt_dn[sec * 1024:(sec + 1) * 1024, :]
                gpsimd.dma_start(out=wdn[:, :].rearrange("q (ft e) -> q ft e", e=Eo),
                                 in_=rows.rearrange("(ft q) e -> q ft e", q=128)
                                 ).then_inc(wd[0].h, 16)
        wd[0].n += 16 * nsec

        @block.tensor
        def _(tensor):
            gu = 0
            for sec in range(nsec):
                p = sec % 2
                tensor.wait_ge(k2t[p].h, k2b[p] + 16 * 8 * (sec // 2 + 1))
                tensor.wait_ge(wd[0].h, wdb[0] + 16 * (sec + 1))
                for tg in range(4):
                    un = sec * 4 + tg
                    if un >= 2:
                        tensor.wait_ge(dn_ev.h, evb + 8 * (un - 1))
                    for t2 in range(2):
                        t = tg * 2 + t2
                        for ec in range(nec):
                            bank = (tg % 2) * 4 + t2 * nec // 1 * 0 + t2 * 2 + ec                                 if nec == 2 else (tg % 2) * 4 + ec
                            if nec == 4:
                                bank = (tg % 2) * 4 + ec
                            last = None
                            for ft in range(8):
                                stat = k2T[:, p * Fh + ft * T_LOC + t * 128:
                                           p * Fh + ft * T_LOC + (t + 1) * 128]
                                mv = wdn[:, ft * Eo + ec * 512:
                                         ft * Eo + (ec + 1) * 512]
                                last = tensor.matmul(
                                    psD[:, bank * 512:(bank + 1) * 512], stat, mv,
                                    start=(ft == 0), stop=(ft == 7))
                            mm.inc(last, 1)
                        if nec == 4 and t2 == 0:
                            tensor.wait_ge(dn_ev.h, evb + un * 8 + 4)

        @block.vector
        def _(vector):
            g = 0
            for sec in range(nsec):
                for tg in range(4):
                    for t2 in range(2):
                        t = tg * 2 + t2
                        for ec in range(nec):
                            bank = (tg % 2) * 4 + (t2 * 2 + ec if nec == 2 else ec)
                            g += 1
                            vector.wait_ge(mm.h, mmb + g)
                            xv = xs[:, t * Eo + ec * 512:t * Eo + (ec + 1) * 512]
                            vector.scalar_tensor_tensor(
                                out=xv, in0=psD[:, bank * 512:(bank + 1) * 512],
                                scalar=sc2[:, t:t + 1], in1=xv,
                                op0=ALU.mult, op1=ALU.add).then_inc(dn_ev.h, 1)
                            dn_ev.n += 1


def emit_qkv_l0(nc, block, sems, xn2T, wt_q, wt_k, wt_v, qt_out, kt_out,
                v_out, consts, wqkv, pev):
    wl = [sems["wl0"], sems["wl1"]]
    mm, qe = sems["mm"], [sems["qe0"], sems["qe1"]]
    ch_s = sems["ch_s"]
    tt1 = sems["tt1"]
    wlb = [wl[0].n, wl[1].n]
    mmb = mm.n
    qeb = [qe[0].n, qe[1].n]
    KT = 16

    with nc.psum_tensor(_u("ps_qkv"), [128, 4 * 512], F32) as psQ:
        jobs = []
        jobs.append((wt_q, qt_out, None))
        jobs.append((wt_k, kt_out, CC["invsq"]))

        @block.sync
        def _(sync):
            sync.wait_ge(tt1.h, tt1.n)
            for ji, (wt, outd, sc) in enumerate(jobs):
                for et in range(16):
                    i = ji * 16 + et
                    p = i % 2
                    if i >= 2:
                        sync.wait_ge(mm.h, mmb + (i - 1) * 2)
                    ap = wt[:, et * 128:(et + 1) * 128].rearrange(
                        "(dt q) e -> q dt e", q=128)
                    sync.dma_start(out=wqkv[:, p * KT * 512:p * KT * 512 + KT * 128]
                                   .rearrange("q (dt e) -> q dt e", e=128),
                                   in_=ap).then_inc(wl[p].h, 16)
        wl[0].n += 16 * 16
        wl[1].n += 16 * 16
        ch_s.qkv_base = ch_s.n

        @block.tensor
        def _(tensor):
            for ji, (wt, outd, sc) in enumerate(jobs):
                for et in range(16):
                    i = ji * 16 + et
                    p = i % 2
                    tensor.wait_ge(wl[p].h, wlb[p] + 16 * (i // 2 + 1))
                    for tc in range(2):
                        bank = (i % 2) * 2 + tc
                        if i >= 2:
                            tensor.wait_ge(ch_s.h, sems["ch_s"].qkv_base + (i - 2) * 2 + tc + 1)
                        last = None
                        for dt in range(KT):
                            stat = wqkv[:, p * KT * 512 + dt * 128:
                                        p * KT * 512 + (dt + 1) * 128]
                            mv = xn2T[:, dt * T_LOC + tc * 512:dt * T_LOC + (tc + 1) * 512]
                            last = tensor.matmul(psQ[:, bank * 512:(bank + 1) * 512],
                                                 stat, mv, start=(dt == 0),
                                                 stop=(dt == KT - 1))
                        mm.inc(last, 1)

        @block.scalar
        def _(scalar):
            for ji, (wt, outd, sc) in enumerate(jobs):
                for et in range(16):
                    i = ji * 16 + et
                    for tc in range(2):
                        bank = (i % 2) * 2 + tc
                        scalar.wait_ge(mm.h, mmb + i * 2 + tc + 1)
                        ev = pev[:, bank * 512:(bank + 1) * 512]
                        if i >= 2:
                            scalar.wait_ge(qe[i % 2].h, qeb[i % 2] + 16 * ((i // 2 - 1) * 2 + tc + 1))
                        if sc is None:
                            scalar.activation(out=ev, in_=psQ[:, bank * 512:(bank + 1) * 512],
                                              func=AF.Copy,
                                              scale=consts[:, CC["one"]:CC["one"] + 1]
                                              ).then_inc(ch_s.h, 1)
                        else:
                            scalar.activation(out=ev, in_=psQ[:, bank * 512:(bank + 1) * 512],
                                              func=AF.Copy,
                                              scale=consts[:, sc:sc + 1]).then_inc(ch_s.h, 1)
                        ch_s.n += 1

        @block.gpsimd
        def _(gpsimd):
            for ji, (wt, outd, sc) in enumerate(jobs):
                for et in range(16):
                    i = ji * 16 + et
                    for tc in range(2):
                        bank = (i % 2) * 2 + tc
                        gpsimd.wait_ge(ch_s.h, ch_s.qkv_base + i * 2 + tc + 1)
                        gpsimd.dma_start(out=outd[et * 128:(et + 1) * 128,
                                                  tc * 512:(tc + 1) * 512],
                                         in_=pev[:, bank * 512:(bank + 1) * 512]
                                         ).then_inc(qe[i % 2].h, 16)
        qe[0].n += 16 * 32
        qe[1].n += 16 * 32

    with nc.psum_tensor(_u("ps_v"), [128, 4 * 512], F32) as psV:
        vbase_mm = mm.n
        vbase_chs = ch_s.n
        vwlb = [wl[0].n, wl[1].n]
        qe0_final, qe1_final = qe[0].n, qe[1].n
        qe[0].n_v_base = qe[0].n

        @block.sync
        def _(sync):
            sync.wait_ge(mm.h, vbase_mm)
            for ec in range(4):
                p = ec % 2
                if ec >= 2:
                    sync.wait_ge(mm.h, vbase_mm + (ec - 1) * NT)
                ap = wt_v[:, ec * 512:(ec + 1) * 512].rearrange(
                    "(dt q) e -> q dt e", q=128)
                sync.dma_start(out=wqkv[:, p * KT * 512:(p + 1) * KT * 512]
                               .rearrange("q (dt e) -> q dt e", e=512),
                               in_=ap).then_inc(wl[p].h, 16)
        wl[0].n += 16 * 2
        wl[1].n += 16 * 2

        @block.tensor
        def _(tensor):
            for ec in range(4):
                p = ec % 2
                tensor.wait_ge(wl[p].h, vwlb[p] + 16 * (ec // 2 + 1))
                for t in range(NT):
                    u = ec * NT + t
                    bank = u % 4
                    if u >= 4:
                        tensor.wait_ge(ch_s.h, vbase_chs + u - 3)
                    last = None
                    for dt in range(KT):
                        stat = xn2T[:, dt * T_LOC + t * 128:dt * T_LOC + (t + 1) * 128]
                        mv = wqkv[:, p * KT * 512 + dt * 512:p * KT * 512 + (dt + 1) * 512]
                        last = tensor.matmul(psV[:, bank * 512:(bank + 1) * 512],
                                             stat, mv, start=(dt == 0),
                                             stop=(dt == KT - 1))
                    mm.inc(last, 1)

        @block.scalar
        def _(scalar):
            for ec in range(4):
                for t in range(NT):
                    u = ec * NT + t
                    bank = u % 4
                    scalar.wait_ge(mm.h, vbase_mm + u + 1)
                    ev = pev[:, bank * 512:(bank + 1) * 512]
                    if u < 4:
                        scalar.wait_ge(qe[0].h, qe0_final)
                        scalar.wait_ge(qe[1].h, qe1_final)
                    else:
                        scalar.wait_ge(qe[0].h, qe[0].n_v_base + 16 * (u - 3))
                    scalar.activation(out=ev, in_=psV[:, bank * 512:(bank + 1) * 512],
                                      func=AF.Copy,
                                      scale=consts[:, CC["one"]:CC["one"] + 1]
                                      ).then_inc(ch_s.h, 1)
                    ch_s.n += 1

        @block.gpsimd
        def _(gpsimd):
            for ec in range(4):
                for t in range(NT):
                    u = ec * NT + t
                    bank = u % 4
                    gpsimd.wait_ge(ch_s.h, vbase_chs + u + 1)
                    gpsimd.dma_start(out=v_out[t * 128:(t + 1) * 128,
                                               ec * 512:(ec + 1) * 512],
                                     in_=pev[:, bank * 512:(bank + 1) * 512]
                                     ).then_inc(qe[0].h, 16)
        qe[0].n += 16 * 32



class Rec:
    NROT = 4

    def __init__(self, nc, block, sems, tag):
        self.nc = nc
        self.block = block
        self.ops = {e: [] for e in ("tensor", "scalar", "vector", "pool",
                                    "syncd", "gpsd")}
        self.tag = tag
        self.chain = {
            "tensor": sems["mm"], "scalar": sems["ch_s"],
            "vector": sems["ch_v"], "pool": sems["ch_p"],
        }
        self.dma = {
            "syncd": [sems["sd0"], sems["sd1"], sems["sd2"], sems["sd3"]],
            "gpsd": [sems["gd0"], sems["gd1"], sems["gd2"], sems["gd3"]],
        }

    def op(self, eng, fn, waits=()):
        lst = self.ops[eng]
        tok = (eng, len(lst))
        lst.append((fn, tuple(w for w in waits if w is not None)))
        return tok

    def _tok_sem_val(self, tok, bases):
        if tok[0] == "raw":
            return (tok[1], tok[2])
        eng, idx = tok
        if eng in self.chain:
            return (self.chain[eng].h, bases[eng] + idx + 1)
        sems = self.dma[eng]
        r = idx % self.NROT
        return (sems[r].h, bases[eng][r] + 16 * (idx // self.NROT + 1))

    def emit(self):
        bases = {}
        for e, s_ in self.chain.items():
            bases[e] = s_.n
        for e, lst in self.dma.items():
            bases[e] = [x.n for x in lst]

        engmap = {"tensor": self.block.tensor, "scalar": self.block.scalar,
                  "vector": self.block.vector, "pool": self.block.gpsimd,
                  "syncd": self.block.sync, "gpsd": self.block.gpsimd}

        for eng in ("syncd", "gpsd", "tensor", "scalar", "vector", "pool"):
            lst = self.ops[eng]
            if not lst:
                continue

            def body(E, eng=eng, lst=lst):
                hw = {}
                for i, (fn, waits) in enumerate(lst):
                    for tok in waits:
                        h, v = self._tok_sem_val(tok, bases)
                        if hw.get(id(h), -1) < v:
                            E.wait_ge(h, v)
                            hw[id(h)] = v
                    ins = fn(E)
                    if eng in self.chain:
                        ins.then_inc(self.chain[eng].h, 1)
                    else:
                        r = i % self.NROT
                        ins.then_inc(self.dma[eng][r].h, 16)
            engmap[eng](body)

        toks = {}
        for e, s_ in self.chain.items():
            n = len(self.ops[e])
            s_.n = bases[e] + n
        for e, lst2 in self.dma.items():
            n = len(self.ops[e])
            for r in range(self.NROT):
                lst2[r].n = bases[e][r] + 16 * ((n - r + self.NROT - 1) // self.NROT)

    def raw_of(self, tok):
        eng, idx = tok
        assert eng in self.chain
        return ("raw", self.chain[eng].h, self._final_base[eng] + idx + 1)



def emit_flash(nc, block, sems, qt_sb, ktf, vf, mask_sb, onat, n_heads,
               kv_div, gate, bufs, entry_waits):
    VW = 132
    n_kvh = n_heads // kv_div
    kbuf, vbuf, qsq, ksq = bufs["kbuf"], bufs["vbuf"], bufs["qsq"], bufs["ksq"]
    kn2, qn2, ptb, gsb = bufs["kn2"], bufs["qn2"], bufs["ptb"], bufs["gsb"]
    obuf, qm2, km2, nbrow = bufs["obuf"], bufs["qm2"], bufs["km2"], bufs["nbrow"]
    negB, rcol = bufs["negB"], bufs["rcol"]
    ones_row = bufs["ones_row"]
    ew = tuple(entry_waits)

    rec = Rec(nc, block, sems, "flash")
    last_exp = {}
    last_pv = {}
    gate_free = {}
    ost = {}
    epi = {}

    with (
        nc.psum_tensor(_u("ps_st"), [128, 3 * 512], F32) as psS,
        nc.psum_tensor(_u("ps_o"), [128, 2 * 512], F32) as psO,
        nc.psum_tensor(_u("ps_g"), [128, 512], F32) as psG,
        nc.psum_tensor(_u("ps_nb"), [128, 512], F32) as psNB,
        nc.psum_tensor(_u("ps_r"), [1, 512], F32) as psR,
    ):
        vinit = []
        for p in range(2):
            vv = vbuf[:, p * 32 * VW:(p + 1) * 32 * VW].rearrange(
                "q (kt w) -> q kt w", w=VW)
            vinit.append(rec.op("vector", lambda E, vv=vv: E.memset(
                vv[:, :, 128:129], 1.0), waits=ew))
            vinit.append(rec.op("vector", lambda E, vv=vv: E.memset(
                vv[:, :, 129:132], 0.0)))

        ones_col = bufs["ones_col"]
        qtok = {}
        prev_mm = None
        qmz = rec.op("vector", lambda E: E.memset(qm2[:, :], 0.0), waits=ew)
        for h in range(n_heads):
            sq = rec.op("scalar", lambda E, h=h: E.activation(
                out=qsq[:, :], in_=qt_sb[:, h * T_LOC:(h + 1) * T_LOC],
                func=AF.Square), waits=ew + (prev_mm,))
            mx = qmz
            for chx in range(T_LOC // 512):
                mmq = rec.op("tensor", lambda E, chx=chx: E.matmul(
                    psR[:, :], ones_col[:, :],
                    qsq[:, chx * 512:(chx + 1) * 512], start=True, stop=True),
                    waits=(sq, mx))
                r1 = rec.op("vector", lambda E: E.tensor_reduce(
                    out=qn2[:, 0:1], in_=psR[:, :], axis=AX.X, op=ALU.max),
                    waits=(mmq,))
                mx = rec.op("vector", lambda E, h=h: E.tensor_tensor(
                    out=qm2[:, h:h + 1], in0=qm2[:, h:h + 1], in1=qn2[:, 0:1],
                    op=ALU.max), waits=(r1,))
                prev_mm = mmq
            qtok[h] = mx

        kload, vload = {}, {}

        def rec_loads(kvh):
            p = kvh % 2
            guard = None
            if kvh >= 2:
                guard = epi.get(((kvh - 2) + 1) * kv_div * NT - 1)
                assert guard is not None, kvh
            kload[kvh] = rec.op("syncd", lambda E, kvh=kvh, p=p: E.dma_start(
                out=kbuf[:, p * SEQ:(p + 1) * SEQ],
                in_=ktf[kvh * 128:(kvh + 1) * 128, :]), waits=(guard,) + ew)
            vv = vbuf[:, p * 32 * VW:(p + 1) * 32 * VW].rearrange(
                "q (kt w) -> q kt w", w=VW)
            vload[kvh] = rec.op("syncd", lambda E, kvh=kvh, vv=vv: E.dma_start(
                out=vv[:, :, 0:128],
                in_=vf[:, kvh * 128:(kvh + 1) * 128].rearrange(
                    "(kt q) d -> q kt d", q=128)),
                waits=(guard, vinit[2 * p], vinit[2 * p + 1]))

        rec_loads(0)

        nbready = {}
        for kvh in range(n_kvh):
            p = kvh % 2
            if kvh + 1 < n_kvh:
                rec_loads(kvh + 1)

            ksqt = rec.op("scalar", lambda E, p=p: E.activation(
                out=ksq[:, :], in_=kbuf[:, p * SEQ:(p + 1) * SEQ],
                func=AF.Square), waits=(kload[kvh], nbready.get(kvh - 1)))
            cols = slice(kvh * kv_div, (kvh + 1) * kv_div)
            c0_first = kvh * kv_div
            kz = rec.op("vector", lambda E, c0=c0_first: E.memset(
                km2[:, c0:c0 + 1], 0.0), waits=(nbready.get(kvh - 1),))
            kmx = kz
            for rnd in range(SEQ // 512):
                off = rnd * 512
                mmr = rec.op("tensor", lambda E, off=off:
                    E.matmul(psR[:, :], ones_col[:, :], ksq[:, off:off + 512],
                             start=True, stop=True),
                    waits=(ksqt, kmx, qtok[n_heads - 1]))
                kmx = rec.op("vector", lambda E: E.tensor_reduce(
                    out=qn2[:, 0:1], in_=psR[:, :], axis=AX.X,
                    op=ALU.max), waits=(mmr,))
                kmx = rec.op("vector", lambda E, c0=c0_first: E.tensor_tensor(
                    out=km2[:, c0:c0 + 1], in0=km2[:, c0:c0 + 1],
                    in1=qn2[:, 0:1], op=ALU.max), waits=(kmx,))
            for t in range(1, kv_div):
                c1 = kvh * kv_div + t
                kmx = rec.op("vector", lambda E, c1=c1, c0=c0_first:
                    E.tensor_copy(km2[:, c1:c1 + 1], km2[:, c0:c0 + 1]),
                    waits=(kmx,))
            tmul = rec.op("vector", lambda E, cols=cols: E.tensor_tensor(
                out=nbrow[:, cols], in0=qm2[:, cols], in1=km2[:, cols],
                op=ALU.mult), waits=(kmx,) + tuple(
                    qtok[h] for h in range(kvh * kv_div, (kvh + 1) * kv_div)))
            tsq = rec.op("scalar", lambda E, cols=cols: E.activation(
                out=nbrow[:, cols], in_=nbrow[:, cols], func=AF.Sqrt),
                waits=(tmul,))
            tneg = rec.op("vector", lambda E, cols=cols: E.tensor_scalar(
                out=nbrow[:, cols], in0=nbrow[:, cols], scalar1=-1.0,
                scalar2=None, op0=ALU.mult), waits=(tsq,))
            bc = rec.op("tensor", lambda E, cols=cols, kvh=kvh: E.matmul(
                psNB[:, kvh * kv_div:(kvh + 1) * kv_div], ones_row[:, 0:128],
                nbrow[:, cols], start=True, stop=True),
                waits=(tneg, nbready.get(kvh - 1)))
            nbready[kvh] = rec.op("vector", lambda E, cols=cols, kvh=kvh:
                E.tensor_copy(negB[:, cols],
                              psNB[:, kvh * kv_div:(kvh + 1) * kv_div]),
                waits=(bc,))

            for hq in range(kvh * kv_div, (kvh + 1) * kv_div):
                for j in range(NT):
                    sidx = hq * NT + j
                    po = sidx % 2
                    st_tok, exp_tok, pv_tok = {}, {}, {}

                    def rec_pv(g):
                        ps_par = g % 3
                        pt_par = g % 2
                        pvs = None
                        for kt in range(4):
                            ktile = 4 * g + kt
                            stat = ptb[:, pt_par * 512 + kt * 128:
                                       pt_par * 512 + (kt + 1) * 128]
                            mv = vbuf[:, p * 32 * VW + ktile * VW:
                                      p * 32 * VW + (ktile + 1) * VW]
                            pvs = rec.op("tensor", lambda E, po=po, stat=stat,
                                         mv=mv, s=(g == 0 and kt == 0),
                                         e2=(g == j and kt == 3): E.matmul(
                                psO[:, po * 512:po * 512 + VW], stat, mv,
                                start=s, stop=e2),
                                waits=(exp_tok[g], vload[kvh],
                                       epi.get(sidx - 2)))
                        pv_tok[g] = pvs
                        last_pv[(sidx, pt_par)] = pvs

                    for g in range(j + 1):
                        ps_par = g % 3
                        pt_par = g % 2
                        dep_exp = (exp_tok.get(g - 3) or
                                   last_exp.get((sidx - 1, ps_par)) or
                                   last_exp.get((sidx - 2, ps_par)) or
                                   last_exp.get((sidx - 3, ps_par)))
                        sts = None
                        for kt in range(4):
                            ktile = 4 * g + kt
                            stat = kbuf[:, p * SEQ + ktile * 128:
                                        p * SEQ + (ktile + 1) * 128]
                            mv = qt_sb[:, hq * T_LOC + j * 128:
                                       hq * T_LOC + (j + 1) * 128]
                            sts = rec.op("tensor", lambda E, ps_par=ps_par,
                                         kt=kt, stat=stat, mv=mv: E.matmul(
                                psS[:, ps_par * 512 + kt * 128:
                                    ps_par * 512 + (kt + 1) * 128],
                                stat, mv, start=True, stop=True),
                                waits=(kload[kvh], dep_exp))
                        st_tok[g] = sts
                        mtok = None
                        if g == j:
                            mtok = rec.op("vector", lambda E, ps_par=ps_par,
                                          j=j: E.tensor_tensor(
                                out=psS[:, ps_par * 512:(ps_par + 1) * 512],
                                in0=psS[:, ps_par * 512:(ps_par + 1) * 512],
                                in1=mask_sb[:, j * 512:(j + 1) * 512],
                                op=ALU.add), waits=(sts,))
                        if g >= 2:
                            rec_pv(g - 2)
                        pt_dep = (pv_tok.get(g - 2) or
                                  last_pv.get((sidx - 1, pt_par)) or
                                  last_pv.get((sidx - 2, pt_par)))
                        ex = rec.op("scalar", lambda E, ps_par=ps_par,
                                    pt_par=pt_par, hq=hq:
                            E.activation(
                                out=ptb[:, pt_par * 512:(pt_par + 1) * 512],
                                in_=psS[:, ps_par * 512:(ps_par + 1) * 512],
                                func=AF.Exp, bias=negB[:, hq:hq + 1]),
                            waits=(sts, mtok, nbready[kvh], pt_dep))
                        exp_tok[g] = ex
                        last_exp[(sidx, ps_par)] = ex
                    if j >= 1:
                        rec_pv(j - 1)
                    rec_pv(j)

                    rc = rec.op("vector", lambda E, po=po: E.reciprocal(
                        rcol[:, po:po + 1],
                        psO[:, po * 512 + 128:po * 512 + 129]),
                        waits=(pv_tok[j], epi.get(sidx - 2)))
                    if gate is not None:
                        gwT, gbB, gate_ready = gate
                        gm = rec.op("tensor", lambda E, po=po, hq=hq, j=j:
                            E.matmul(psG[:, po * 256:po * 256 + 128],
                                     qt_sb[:, hq * T_LOC + j * 128:
                                           hq * T_LOC + (j + 1) * 128],
                                     gwT[:, :], start=True, stop=True),
                            waits=(gate_free.get(sidx - 2), gate_ready))
                        gt = rec.op("vector", lambda E, po=po: E.tensor_tensor(
                            out=psG[:, po * 256:po * 256 + 128],
                            in0=psG[:, po * 256:po * 256 + 128],
                            in1=gbB[:, :], op=ALU.add), waits=(gm,))
                        sg = rec.op("scalar", lambda E, po=po: E.activation(
                            out=gsb[:, po * 128:(po + 1) * 128],
                            in_=psG[:, po * 256:po * 256 + 128],
                            func=AF.Sigmoid), waits=(gt,))
                        gate_free[sidx] = sg
                        ep = rec.op("vector", lambda E, po=po:
                            E.scalar_tensor_tensor(
                                out=obuf[:, po * 128:(po + 1) * 128],
                                in0=psO[:, po * 512:po * 512 + 128],
                                scalar=rcol[:, po:po + 1],
                                in1=gsb[:, po * 128:(po + 1) * 128],
                                op0=ALU.mult, op1=ALU.mult),
                            waits=(rc, sg, ost.get(sidx - 2)))
                    else:
                        ep = rec.op("vector", lambda E, po=po: E.tensor_scalar(
                            out=obuf[:, po * 128:(po + 1) * 128],
                            in0=psO[:, po * 512:po * 512 + 128],
                            scalar1=rcol[:, po:po + 1], scalar2=None,
                            op0=ALU.mult),
                            waits=(rc, ost.get(sidx - 2)))
                    epi[sidx] = ep
                    ost[sidx] = rec.op("syncd", lambda E, po=po, hq=hq, j=j:
                        E.dma_start(
                            out=onat[j * 128:(j + 1) * 128,
                                     hq * 128:(hq + 1) * 128],
                            in_=obuf[:, po * 128:(po + 1) * 128]),
                        waits=(ep,))

        rec.op("vector", lambda E: E.tensor_copy(rcol[:, 0:1], rcol[:, 0:1]),
               waits=tuple(ost[s2] for s2 in range(n_heads * NT - 2,
                                                   n_heads * NT)))
        rec.emit()


def emit_oproj_residual(nc, block, sems, onat, wt_o, xs, oT_sb, wch, entry):
    rec = Rec(nc, block, sems, "oproj")
    with nc.psum_tensor(_u("ps_op"), [128, 4 * 512], F32) as psP:
        tts = []
        for dt in range(16):
            tts.append(rec.op("syncd", lambda E, dt=dt: E.dma_start_transpose(
                out=oT_sb[:, dt * T_LOC:(dt + 1) * T_LOC],
                in_=onat[:, dt * 128:(dt + 1) * 128]), waits=tuple(entry)))
        wl = {}
        evac = {}
        for ec in range(4):
            pw = ec % 2
            wl[ec] = rec.op("syncd", lambda E, ec=ec, pw=pw: E.dma_start(
                out=wch[:, pw * 16 * 512:(pw + 1) * 16 * 512].rearrange(
                    "q (dt e) -> q dt e", e=512),
                in_=wt_o[:, ec * 512:(ec + 1) * 512].rearrange(
                    "(dt q) e -> q dt e", q=128)),
                waits=(evac.get((ec - 2, NT - 1)),))
            for t in range(NT):
                bank = (ec % 2) * 2 + (t % 2)
                mmt = None
                for dt in range(16):
                    mmt = rec.op("tensor", lambda E, bank=bank, dt=dt, t=t,
                                 pw=pw, s=(dt == 0), e2=(dt == 15): E.matmul(
                        psP[:, bank * 512:(bank + 1) * 512],
                        oT_sb[:, dt * T_LOC + t * 128:dt * T_LOC + (t + 1) * 128],
                        wch[:, pw * 16 * 512 + dt * 512:
                            pw * 16 * 512 + (dt + 1) * 512],
                        start=s, stop=e2),
                        waits=tuple(tts) + (wl[ec], evac.get((ec - 1, t - 1)),
                                            evac.get((ec - 2, t)),
                                            evac.get((ec, t - 2))))
                evac[(ec, t)] = rec.op("vector", lambda E, bank=bank, t=t,
                                       ec=ec: E.tensor_tensor(
                    out=xs[:, t * D + ec * 512:t * D + (ec + 1) * 512],
                    in0=psP[:, bank * 512:(bank + 1) * 512],
                    in1=xs[:, t * D + ec * 512:t * D + (ec + 1) * 512],
                    op=ALU.add), waits=(mmt,))
        rec.emit()


def emit_l1_proj(nc, block, sems, xn2T, wq_n, wk_n, wv_n, qfull, kfull,
                 v1_out, v16, wch, entry):
    rec = Rec(nc, block, sems, "l1proj")
    jobs = [("q", wq_n, c) for c in range(4)] + [("k", wk_n, 0), ("v", wv_n, 0)]
    with nc.psum_tensor(_u("ps_l1p"), [128, 4 * 512], F32) as psL:
        evac = {}
        wload = {}
        for ji, (nm, w, c) in enumerate(jobs):
            pw = ji % 2
            wload[ji] = rec.op("syncd", lambda E, w=w, c=c, pw=pw:
                E.dma_start(
                    out=wch[:, pw * 16 * 512:(pw + 1) * 16 * 512].rearrange(
                        "q (dt e) -> q dt e", e=512),
                    in_=w[:, c * 512:(c + 1) * 512].rearrange(
                        "(dt q) e -> q dt e", q=128)),
                waits=tuple(entry) + (evac.get((ji - 2, NT - 1)),))
            for t in range(NT):
                bank = (ji % 2) * 2 + (t % 2)
                mmt = None
                for dt in range(16):
                    mmt = rec.op("tensor", lambda E, bank=bank, dt=dt, t=t,
                                 pw=pw, s=(dt == 0), e2=(dt == 15): E.matmul(
                        psL[:, bank * 512:(bank + 1) * 512],
                        xn2T[:, dt * T_LOC + t * 128:dt * T_LOC + (t + 1) * 128],
                        wch[:, pw * 16 * 512 + dt * 512:
                            pw * 16 * 512 + (dt + 1) * 512],
                        start=s, stop=e2),
                        waits=(wload[ji], evac.get((ji - 1, t - 1)),
                               evac.get((ji - 2, t)),
                               evac.get((ji, t - 2))) + tuple(entry))
                if nm == "q":
                    dst = qfull[:, t * D + c * 512:t * D + (c + 1) * 512]
                elif nm == "k":
                    dst = kfull[:, t * 512:(t + 1) * 512]
                else:
                    dst = v16[:, :]
                ev = rec.op("scalar", lambda E, bank=bank, dst=dst: E.activation(
                    out=dst, in_=psL[:, bank * 512:(bank + 1) * 512],
                    func=AF.Copy),
                    waits=(mmt,) + ((evac.get((ji, t - 1)),) if nm == "v" else ()))
                if nm == "v":
                    ev = rec.op("syncd", lambda E, t=t: E.dma_start(
                        out=v1_out[t * 128:(t + 1) * 128, :], in_=v16[:, :]),
                        waits=(ev,))
                evac[(ji, t)] = ev
        rec.emit()


def emit_l1_normrope(nc, block, sems, qfull, kfull, gqB, gkB, tabs, qnat,
                     knat, qt1_out, kt1_out, bufs, entry):
    rec = Rec(nc, block, sems, "l1nr")
    qb, rot, tq, tk = bufs["qb"], bufs["rot"], bufs["tq"], bufs["tk"]
    qss, kss = bufs["qss"], bufs["kss"]
    qb16, kb16, kb = bufs["qb16"], bufs["kb16"], bufs["kb"]
    TT = bufs["TT"]

    tabtok = {}
    tfree = {}
    qdone, kdone = {}, {}
    for t in range(NT):
        toks = []
        for i, (nm, buf, wdt) in enumerate((("qc", tq, D), ("qs", tq, D),
                                            ("kc", tk, 512), ("ks", tk, 512))):
            off = (i % 2) * (D if nm[0] == "q" else 512)
            toks.append(rec.op("syncd", lambda E, nm=nm, buf=buf, off=off,
                               wdt=wdt, t=t: E.dma_start(
                out=buf[:, off:off + wdt],
                in_=tabs[nm][t * 128:(t + 1) * 128, :]),
                waits=tuple(entry) + (tfree.get(t - 1),)))
        tabtok[t] = toks

        cp = rec.op("vector", lambda E, t=t: E.tensor_copy(
            qb[:, :], qfull[:, t * D:(t + 1) * D]),
            waits=tuple(entry) + (qdone.get(t - 1),))
        sq = rec.op("scalar", lambda E: E.activation(
            out=rot[:, :], in_=qb[:, :], func=AF.Square),
            waits=(cp, qdone.get(t - 1)))
        rd = rec.op("vector", lambda E: E.tensor_reduce(
            out=qss[:, :], in_=rot[:, :].rearrange("q (h d) -> q h d", d=128),
            axis=AX.X, op=ALU.add), waits=(sq,))
        n1 = rec.op("vector", lambda E: E.tensor_scalar(
            out=qss[:, :], in0=qss[:, :], scalar1=1.0 / 128.0,
            scalar2=EPS_RMS, op0=ALU.mult, op1=ALU.add), waits=(rd,))
        n2 = rec.op("scalar", lambda E: E.activation(
            out=qss[:, :], in_=qss[:, :], func=AF.Sqrt), waits=(n1,))
        n3 = rec.op("vector", lambda E: E.reciprocal(qss[:, :], qss[:, :]),
                    waits=(n2,))
        hops = []
        for h in range(16):
            hops.append(rec.op("vector", lambda E, h=h:
                E.scalar_tensor_tensor(
                    out=qb[:, h * 128:(h + 1) * 128],
                    in0=qb[:, h * 128:(h + 1) * 128],
                    scalar=qss[:, h:h + 1],
                    in1=gqB[:, h * 128:(h + 1) * 128],
                    op0=ALU.mult, op1=ALU.mult), waits=(n3, sq)))
        rops = []
        for h in range(16):
            rops.append(rec.op("vector", lambda E, h=h: E.tensor_scalar(
                out=rot[:, h * 128:h * 128 + 64],
                in0=qb[:, h * 128 + 64:h * 128 + 128],
                scalar1=-1.0, scalar2=None, op0=ALU.mult),
                waits=(hops[h], rd)))
            rops.append(rec.op("vector", lambda E, h=h: E.tensor_copy(
                rot[:, h * 128 + 64:h * 128 + 128],
                qb[:, h * 128:h * 128 + 64]), waits=(hops[h],)))
        m1 = rec.op("vector", lambda E: E.tensor_tensor(
            out=qb[:, :], in0=qb[:, :], in1=tq[:, 0:D],
            op=ALU.mult), waits=tuple(rops) + tuple(toks))
        m2_ = rec.op("vector", lambda E: E.tensor_tensor(
            out=rot[:, :], in0=rot[:, :], in1=tq[:, D:2 * D],
            op=ALU.mult), waits=(m1,) + tuple(toks))
        ad = rec.op("vector", lambda E: E.tensor_tensor(
            out=qb16[:, :], in0=qb[:, :], in1=rot[:, :], op=ALU.add),
            waits=(m1, m2_, qdone.get(t - 1)))
        qdone[t] = rec.op("syncd", lambda E, t=t: E.dma_start(
            out=qnat[t * 128:(t + 1) * 128, :], in_=qb16[:, :]),
            waits=(ad,))

        kcp = rec.op("vector", lambda E, t=t: E.tensor_copy(
            kb[:, 0:512], kfull[:, t * 512:(t + 1) * 512]),
            waits=tuple(entry) + (kdone.get(t - 1),))
        sqk = rec.op("scalar", lambda E: E.activation(
            out=kb[:, 512:1024], in_=kb[:, 0:512], func=AF.Square),
            waits=(kcp, kdone.get(t - 1)))
        rdk = rec.op("vector", lambda E: E.tensor_reduce(
            out=kss[:, :], in_=kb[:, 512:1024].rearrange(
                "q (h d) -> q h d", d=128),
            axis=AX.X, op=ALU.add), waits=(sqk,))
        nk1 = rec.op("vector", lambda E: E.tensor_scalar(
            out=kss[:, :], in0=kss[:, :], scalar1=1.0 / 128.0,
            scalar2=EPS_RMS, op0=ALU.mult, op1=ALU.add), waits=(rdk,))
        nk2 = rec.op("scalar", lambda E: E.activation(
            out=kss[:, :], in_=kss[:, :], func=AF.Sqrt), waits=(nk1,))
        nk3 = rec.op("vector", lambda E: E.reciprocal(kss[:, :], kss[:, :]),
                     waits=(nk2,))
        khops = []
        for h in range(4):
            khops.append(rec.op("vector", lambda E, h=h:
                E.scalar_tensor_tensor(
                    out=kb[:, h * 128:(h + 1) * 128],
                    in0=kb[:, h * 128:(h + 1) * 128],
                    scalar=kss[:, h:h + 1],
                    in1=gkB[:, h * 128:(h + 1) * 128],
                    op0=ALU.mult, op1=ALU.mult), waits=(nk3, rdk)))
        krops = []
        for h in range(4):
            krops.append(rec.op("vector", lambda E, h=h: E.tensor_scalar(
                out=kb[:, 512 + h * 128:512 + h * 128 + 64],
                in0=kb[:, h * 128 + 64:h * 128 + 128],
                scalar1=-1.0, scalar2=None, op0=ALU.mult),
                waits=(khops[h],)))
            krops.append(rec.op("vector", lambda E, h=h: E.tensor_copy(
                kb[:, 512 + h * 128 + 64:512 + h * 128 + 128],
                kb[:, h * 128:h * 128 + 64]), waits=(khops[h],)))
        km1 = rec.op("vector", lambda E: E.tensor_tensor(
            out=kb[:, 0:512], in0=kb[:, 0:512], in1=tk[:, 0:512],
            op=ALU.mult), waits=tuple(krops) + tuple(toks))
        km2 = rec.op("vector", lambda E: E.tensor_tensor(
            out=kb[:, 512:1024], in0=kb[:, 512:1024], in1=tk[:, 512:1024],
            op=ALU.mult), waits=(km1,) + tuple(toks))
        kad = rec.op("vector", lambda E: E.tensor_tensor(
            out=kb16[:, :], in0=kb[:, 0:512], in1=kb[:, 512:1024],
            op=ALU.add), waits=(km1, km2, kdone.get(t - 1)))
        kdone[t] = rec.op("syncd", lambda E, t=t: E.dma_start(
            out=knat[t * 128:(t + 1) * 128, :], in_=kb16[:, :]),
            waits=(kad,))
        tfree[t] = kad

    allq = tuple(qdone.values()) + tuple(kdone.values())
    prev_st = None
    for dt in range(16):
        tt_ = rec.op("syncd", lambda E, dt=dt: E.dma_start_transpose(
            out=TT[:, 0:T_LOC], in_=qnat[:, dt * 128:(dt + 1) * 128]),
            waits=allq + ((prev_st,) if prev_st else ()))
        prev_st = rec.op("syncd", lambda E, dt=dt: E.dma_start(
            out=qt1_out[dt * 128:(dt + 1) * 128, :], in_=TT[:, 0:T_LOC]),
            waits=(tt_,))
    for dt in range(4):
        tt_ = rec.op("syncd", lambda E, dt=dt: E.dma_start_transpose(
            out=TT[:, 0:T_LOC], in_=knat[:, dt * 128:(dt + 1) * 128]),
            waits=(prev_st,))
        prev_st = rec.op("syncd", lambda E, dt=dt: E.dma_start(
            out=kt1_out[dt * 128:(dt + 1) * 128, :], in_=TT[:, 0:T_LOC]),
            waits=(tt_,))
    rec.op("vector", lambda E: E.tensor_copy(kss[:, 0:1], kss[:, 0:1]),
           waits=(prev_st,))
    rec.emit()



CC = {"neg1": 16, "invsq": 17, "epsln": 18, "epsq": 19, "one": 20}
NCOL = 24

P1_QW = [("L0gu", 0), ("L0dn", 1)]


def build_p1():
    nc = bass.Bass("TRN2", target_bir_lowering=False)
    x_in = nc.dram_tensor("x", [T_LOC, D], F32, kind="ExternalInput").ap()
    partials = nc.dram_tensor("partials", [8, len(QW_NAMES)], F32, kind="ExternalInput").ap()
    w_gu = nc.dram_tensor("w_L0gu", [EL0, D], F32, kind="ExternalInput").ap()
    w_dn = nc.dram_tensor("w_L0dn", [D, F_L0], F32, kind="ExternalInput").ap()
    w_q = nc.dram_tensor("w_q", [D, D], F32, kind="ExternalInput").ap()
    w_k = nc.dram_tensor("w_k", [D, D], F32, kind="ExternalInput").ap()
    w_v = nc.dram_tensor("w_v", [D, D], F32, kind="ExternalInput").ap()
    w_o = nc.dram_tensor("w_o", [D, D], F32, kind="ExternalInput").ap()

    x1_out = nc.dram_tensor("x1", [T_LOC, D], F32, kind="ExternalOutput").ap()
    qt_out = nc.dram_tensor("qt", [D, T_LOC], BF16, kind="ExternalOutput").ap()
    kt_out = nc.dram_tensor("kt", [D, T_LOC], BF16, kind="ExternalOutput").ap()
    v_out = nc.dram_tensor("v", [T_LOC, D], BF16, kind="ExternalOutput").ap()
    wt_gu = nc.dram_tensor("wt_L0gu", [D, EL0], BF16, kind="ExternalOutput").ap()
    wt_dn = nc.dram_tensor("wt_L0dn", [F_L0, D], BF16, kind="ExternalOutput").ap()
    wt_o = nc.dram_tensor("wt_o", [D, D], BF16, kind="ExternalOutput").ap()
    wt_w1_out = nc.dram_tensor("wt_L1w1", [D, F_L1], BF16, kind="ExternalOutput").ap()
    wt_w2_out = nc.dram_tensor("wt_L1w2", [F_L1, D], BF16, kind="ExternalOutput").ap()
    wnq1_out = nc.dram_tensor("wnat_q1", [D, D], BF16, kind="ExternalOutput").ap()
    wnk1_out = nc.dram_tensor("wnat_k1", [D, 512], BF16, kind="ExternalOutput").ap()
    wnv1_out = nc.dram_tensor("wnat_v1", [D, 512], BF16, kind="ExternalOutput").ap()
    wno1_out = nc.dram_tensor("wnat_o1", [D, D], BF16, kind="ExternalOutput").ap()
    dbg_out = nc.dram_tensor("dbg", [128, 4 * NT], F32, kind="ExternalOutput").ap()
    w_l1w1 = nc.dram_tensor("w_L1w1", [F_L1, D], F32, kind="ExternalInput").ap()
    w_l1w2 = nc.dram_tensor("w_L1w2", [D, F_L1], F32, kind="ExternalInput").ap()
    w1q_in = nc.dram_tensor("w1q", [D, D], F32, kind="ExternalInput").ap()
    w1k_in = nc.dram_tensor("w1k", [D, 512], F32, kind="ExternalInput").ap()
    w1v_in = nc.dram_tensor("w1v", [D, 512], F32, kind="ExternalInput").ap()
    w1o_in = nc.dram_tensor("w1o", [D, D], F32, kind="ExternalInput").ap()
    wnat_l1w1 = nc.dram_tensor("wnat_l1w1", [F_L1, D], BF16).ap()
    wnat_l1w2 = nc.dram_tensor("wnat_l1w2", [D, F_L1], BF16).ap()

    wnat_gu = nc.dram_tensor("wnat_gu", [EL0, D], BF16).ap()
    wnat_dn = nc.dram_tensor("wnat_dn", [D, F_L0], BF16).ap()
    wnat_q = nc.dram_tensor("wnat_q", [D, D], BF16).ap()
    wnat_k = nc.dram_tensor("wnat_k", [D, D], BF16).ap()
    wnat_v = nc.dram_tensor("wnat_v", [D, D], BF16).ap()
    wnat_o = nc.dram_tensor("wnat_o", [D, D], BF16).ap()
    wt_q = nc.dram_tensor("wtq", [D, D], BF16).ap()
    wt_k = nc.dram_tensor("wtk", [D, D], BF16).ap()
    wt_v = nc.dram_tensor("wtv", [D, D], BF16).ap()
    k1nat = nc.dram_tensor("k1nat", [T_LOC, D], BF16).ap()
    g_dram = nc.dram_tensor("g_dram", [T_LOC, F_L0], F32).ap()
    k2nat = nc.dram_tensor("k2nat", [T_LOC, F_L0], BF16).ap()
    xn2nat = nc.dram_tensor("xn2nat", [T_LOC, D], BF16).ap()

    sem_names = ["ld0", "ch_v", "ch_s", "mm", "wp_ld0", "wp_ld1", "wp_sg",
                 "wp_cv0", "wp_cv1", "wp_st0", "wp_st1", "wp_tt0", "wp_tt1",
                 "wp_wo0", "wp_wo1", "wl0", "wl1", "tt0", "tt1", "ue_s",
                 "ue_v", "gst", "gr0", "gr1", "kn", "k2t0", "k2t1",
                 "wd0", "wd1", "dn_ev", "ob", "qe0", "qe1"]

    import contextlib
    with contextlib.ExitStack() as st_:
        sems = {n: Sem(st_.enter_context(nc.semaphore(n))) for n in sem_names}
        consts = st_.enter_context(nc.sbuf_tensor("consts", [128, NCOL], F32))
        ones_col = st_.enter_context(nc.sbuf_tensor("ones_c", [128, 1], F32))
        ones_row = st_.enter_context(nc.sbuf_tensor("ones_r", [1, 128], F32))
        xs = st_.enter_context(nc.sbuf_tensor("xs", [128, NT * D], F32))
        scU = st_.enter_context(nc.sbuf_tensor("scU", [128, NT], F32))
        m2 = st_.enter_context(nc.sbuf_tensor("m2", [128, NT], F32))
        s2c = st_.enter_context(nc.sbuf_tensor("s2c", [128, NT], F32))
        sc2 = st_.enter_context(nc.sbuf_tensor("sc2", [128, NT], F32))
        block = st_.enter_context(nc.Block())

        ch_v, ch_s, mm = sems["ch_v"], sems["ch_s"], sems["mm"]
        ld0, kn = sems["ld0"], sems["kn"]

        @block.vector
        def _(vector):
            for name, val in (("neg1", -1.0), ("invsq", INV_SQRT_DH),
                              ("epsln", EPS_LN), ("epsq", EPS_Q), ("one", 1.0)):
                ch_v.inc(vector.memset(consts[:, CC[name]:CC[name] + 1], val), 1)

        counts = [QW_SHAPES[n][0] * QW_SHAPES[n][1] for n in QW_NAMES]
        emit_scales(nc, block, sems, partials, consts, ones_col, ones_row, counts)

        @block.gpsimd
        def _(gpsimd):
            for t in range(NT):
                ld0.inc(gpsimd.dma_start(out=xs[:, t * D:(t + 1) * D],
                                         in_=x_in[t * 128:(t + 1) * 128, :]), 16)
        x_loaded = ld0.n

        CW = 4096
        with (
            nc.sbuf_tensor("wpf", [128, 2 * CW], F32) as wpf,
            nc.sbuf_tensor("sg1", [128, CW], F32) as sg1,
            nc.sbuf_tensor("sg2", [128, CW], F32) as sg2,
            nc.sbuf_tensor("wpb", [128, 2 * CW], BF16) as wpb,
            nc.sbuf_tensor("wtt", [128, EL0], BF16) as wtt,
        ):
            bufs = {"wpf": wpf, "sg1": sg1, "sg2": sg2, "wpb": wpb, "wtt": wtt}
            consts_ready = ch_v.n

            @block.scalar
            def _(scalar):
                scalar.wait_ge(ch_v.h, consts_ready)

            @block.vector
            def _(vector):
                vector.wait_ge(ch_v.h, consts_ready)

            jobs = [
                (w_gu, wnat_gu, wt_gu, EL0, D, True, 0, True),
                (w_dn, wnat_dn, wt_dn, D, F_L0, True, 1, True),
                (w_q, wnat_q, wt_q, D, D, False, 0, True),
                (w_k, wnat_k, wt_k, D, D, False, 0, True),
                (w_v, wnat_v, wt_v, D, D, False, 0, True),
                (w_o, wnat_o, wt_o, D, D, False, 0, True),
                (w_l1w1, wnat_l1w1, wt_w1_out, F_L1, D, True, 2, True),
                (w_l1w2, wnat_l1w2, wt_w2_out, D, F_L1, True, 3, True),
                (w1q_in, wnq1_out, None, D, D, False, 0, False),
                (w1k_in, wnk1_out, None, D, 512, False, 0, False),
                (w1v_in, wnv1_out, None, D, 512, False, 0, False),
                (w1o_in, wno1_out, None, D, D, False, 0, False),
            ]
            for (win, wnat, wtout, R, C, tern, wi, dot) in jobs:
                flatR = R * C // CW
                emit_weight_prep(
                    nc, block, sems,
                    win.rearrange("r c -> (r c)").rearrange("(q c) -> q c", c=CW),
                    wnat.rearrange("r c -> (r c)").rearrange("(q c) -> q c", c=CW),
                    wtout, R, C, tern, consts, wi, bufs,
                    flat_rows=flatR, passA_cols=CW, wnat_geo=wnat,
                    do_transpose=dot)

        k1T = st_.enter_context(nc.sbuf_tensor("k1T", [128, 16 * T_LOC], BF16))
        with (
            nc.sbuf_tensor("sqs", [128, D], F32) as sqs,
            nc.sbuf_tensor("ybuf", [128, D], F32) as ybuf,
            nc.sbuf_tensor("kibuf", [128, D], I32) as kibuf,
            nc.sbuf_tensor("dbuf", [128, D], F32) as dbuf,
            nc.sbuf_tensor("t1buf", [128, D], F32) as t1buf,
            nc.sbuf_tensor("nkbuf", [128, D], BF16) as nkbuf,
            nc.sbuf_tensor("tcs", [128, 12], F32) as tcs,
        ):
            prep_gates = [(sems[n2], sems[n2].n) for n2 in
                          ("wp_ld0", "wp_ld1", "wp_sg", "wp_cv0", "wp_cv1",
                           "wp_st0", "wp_st1", "wp_tt0", "wp_tt1", "wp_wo0",
                           "wp_wo1")]
            emit_lnq_tiles(nc, block, sems, xs, k1nat, consts, tcs, sqs, ybuf,
                           kibuf, dbuf, t1buf, nkbuf,
                           [(ld0, x_loaded)] + prep_gates,
                           quant=True, sc_out=scU, sc_wcol=2, xn_nat=None)

            tt0 = sems["tt0"]
            k1_stored = kn.n

            @block.sync
            def _(sync):
                sync.wait_ge(kn.h, k1_stored)
                for s2_ in ("wp_st0", "wp_st1", "wp_wo0", "wp_wo1"):
                    sync.wait_ge(sems[s2_].h, sems[s2_].n)
                for dt in range(16):
                    tt0.inc(sync.dma_start_transpose(
                        out=k1T[:, dt * T_LOC:(dt + 1) * T_LOC],
                        in_=k1nat[:, dt * 128:(dt + 1) * 128]), 16)

        with (
            nc.sbuf_tensor("wua", [128, 2 * 16 * 512], BF16) as wua,
            nc.sbuf_tensor("wub", [128, 2 * 16 * 512], BF16) as wub,
            nc.sbuf_tensor("gab", [128, 2 * 512], F32) as gab,
            nc.sbuf_tensor("gbb", [128, 2 * 512], F32) as gbb,
            nc.sbuf_tensor("rmaxb", [128, 2], F32) as rmaxb,
        ):
            emit_ffn_up_glu(nc, block, sems, k1T, wt_gu, g_dram, scU, m2,
                            Fh=F_L0, glu=True, silu=True,
                            bufs={"wua": wua, "wub": wub, "ga": gab, "gb": gbb,
                                  "rmax": rmaxb})

        with (
            nc.sbuf_tensor("grl", [128, 2 * 1024], F32) as grl,
            nc.sbuf_tensor("y2", [128, 1024], F32) as y2,
            nc.sbuf_tensor("ki2", [128, 1024], I32) as ki2,
            nc.sbuf_tensor("d2", [128, 1024], F32) as d2,
            nc.sbuf_tensor("t12", [128, 1024], F32) as t12,
            nc.sbuf_tensor("nk2", [128, 1024], BF16) as nk2,
            nc.sbuf_tensor("k2T", [128, 2 * F_L0], BF16) as k2T,
            nc.sbuf_tensor("wdn", [128, 8 * D], BF16) as wdn,
        ):
            emit_quant_g(nc, block, sems, g_dram, k2nat, m2, s2c, sc2, consts,
                         grl, y2, ki2, d2, t12, nk2, dn_wi=1, Fh=F_L0,
                         extra_half=True)
            emit_ffn_down(nc, block, sems, k2nat, wt_dn, xs, sc2, k2T, wdn,
                          Fh=F_L0, Eo=D)
        x1_ready = sems["dn_ev"].n

        with (
            nc.sbuf_tensor("sqs2", [128, D], F32) as sqs2,
            nc.sbuf_tensor("xn2b", [128, D], BF16) as xn2b,
            nc.sbuf_tensor("tcs2", [128, 12], F32) as tcs2,
            nc.sbuf_tensor("xn2T", [128, 16 * T_LOC], BF16) as xn2T,
            nc.sbuf_tensor("wqkv", [128, 2 * 16 * 512], BF16) as wqkv,
            nc.sbuf_tensor("pev", [128, 4 * 512], BF16) as pev,
        ):
            emit_lnq_tiles(nc, block, sems, xs, None, consts, tcs2, sqs2, None,
                           None, None, None, xn2b, (sems["dn_ev"], x1_ready),
                           quant=False, sc_out=None, sc_wcol=None,
                           xn_nat=xn2nat)

            tt1 = sems["tt1"]
            xn2_stored = kn.n

            @block.sync
            def _(sync):
                sync.wait_ge(kn.h, xn2_stored)
                for dt in range(16):
                    tt1.inc(sync.dma_start_transpose(
                        out=xn2T[:, dt * T_LOC:(dt + 1) * T_LOC],
                        in_=xn2nat[:, dt * 128:(dt + 1) * 128]), 16)

            ob = sems["ob"]

            @block.gpsimd
            def _(gpsimd):
                gpsimd.wait_ge(sems["dn_ev"].h, x1_ready)
                gpsimd.wait_ge(tt1.h, tt1.n)
                for t in range(NT):
                    ob.inc(gpsimd.dma_start(out=x1_out[t * 128:(t + 1) * 128, :],
                                            in_=xs[:, t * D:(t + 1) * D]), 16)

            emit_qkv_l0(nc, block, sems, xn2T, wt_q, wt_k, wt_v, qt_out,
                        kt_out, v_out, consts, wqkv, pev)

        dbg_sb = st_.enter_context(nc.sbuf_tensor("dbgsb", [128, 4 * NT], F32))

        @block.vector
        def _(vector):
            vector.wait_ge(sems["dn_ev"].h, sems["dn_ev"].n)
            for i, src in enumerate((scU, m2, sc2, s2c)):
                ch_v.inc(vector.tensor_copy(dbg_sb[:, i * NT:(i + 1) * NT], src[:, :]), 1)

        @block.sync
        def _(sync):
            sync.wait_ge(ch_v.h, ch_v.n)
            sems["ob"].inc(sync.dma_start(out=dbg_out[:, :], in_=dbg_sb[:, :]), 16)
            for s_ in (sems["ob"], sems["qe0"], sems["qe1"], sems["wp_wo0"],
                       sems["wp_wo1"]):
                sync.wait_ge(s_.h, s_.n)
    return nc



def build_p2():
    nc = bass.Bass("TRN2", target_bir_lowering=False)
    x1_in = nc.dram_tensor("x1", [T_LOC, D], F32, kind="ExternalInput").ap()
    partials = nc.dram_tensor("partials", [8, len(QW_NAMES)], F32, kind="ExternalInput").ap()
    qt_in = nc.dram_tensor("qt", [D, T_LOC], BF16, kind="ExternalInput").ap()
    ktf_in = nc.dram_tensor("ktf", [D, SEQ], BF16, kind="ExternalInput").ap()
    vf_in = nc.dram_tensor("vf", [SEQ, D], BF16, kind="ExternalInput").ap()
    mask_in = nc.dram_tensor("mask", [128, NT * 512], F32, kind="ExternalInput").ap()
    gw_in = nc.dram_tensor("gate_w", [128, 128], F32, kind="ExternalInput").ap()
    gb_in = nc.dram_tensor("gate_b", [1, 128], F32, kind="ExternalInput").ap()
    wt_gu = nc.dram_tensor("wt_L0gu", [D, EL0], BF16, kind="ExternalInput").ap()
    wt_dn = nc.dram_tensor("wt_L0dn", [F_L0, D], BF16, kind="ExternalInput").ap()
    wt_o = nc.dram_tensor("wt_o", [D, D], BF16, kind="ExternalInput").ap()
    wt_w1_in = nc.dram_tensor("wt_L1w1", [D, F_L1], BF16, kind="ExternalInput").ap()
    wt_w2_in = nc.dram_tensor("wt_L1w2", [F_L1, D], BF16, kind="ExternalInput").ap()
    wnq1_in = nc.dram_tensor("wnat_q1", [D, D], BF16, kind="ExternalInput").ap()
    wnk1_in = nc.dram_tensor("wnat_k1", [D, 512], BF16, kind="ExternalInput").ap()
    wnv1_in = nc.dram_tensor("wnat_v1", [D, 512], BF16, kind="ExternalInput").ap()
    gq_in = nc.dram_tensor("gq_row", [1, D], F32, kind="ExternalInput").ap()
    gk_in = nc.dram_tensor("gk_row", [1, 512], F32, kind="ExternalInput").ap()
    tabs = {nm: nc.dram_tensor("tab_" + nm, [T_LOC, D if nm[0] == "q" else 512],
                               F32, kind="ExternalInput").ap()
            for nm in ("qc", "qs", "kc", "ks")}

    x4_out = nc.dram_tensor("x4", [T_LOC, D], F32, kind="ExternalOutput").ap()
    qt1_out = nc.dram_tensor("qt1", [D, T_LOC], BF16, kind="ExternalOutput").ap()
    kt1_out = nc.dram_tensor("kt1", [512, T_LOC], BF16, kind="ExternalOutput").ap()
    v1_out = nc.dram_tensor("v1", [T_LOC, 512], BF16, kind="ExternalOutput").ap()
    dbg_out = nc.dram_tensor("dbg", [T_LOC, D], BF16, kind="ExternalOutput").ap()

    onat = nc.dram_tensor("onat", [T_LOC, D], BF16).ap()
    k1nat = nc.dram_tensor("k1nat", [T_LOC, D], BF16).ap()
    g_dram = nc.dram_tensor("g_dram", [T_LOC, F_L0], F32).ap()
    k2nat = nc.dram_tensor("k2nat", [T_LOC, F_L0], BF16).ap()
    xn2nat = nc.dram_tensor("xn2nat", [T_LOC, D], BF16).ap()
    qnat = nc.dram_tensor("qnat", [T_LOC, D], BF16).ap()
    knat = nc.dram_tensor("knat", [T_LOC, 512], BF16).ap()

    sem_names = ["ld0", "ch_v", "ch_s", "ch_p", "mm", "wp_ld0", "wp_ld1",
                 "wp_sg", "wp_cv0", "wp_cv1", "wp_st0", "wp_st1", "wp_tt0",
                 "wp_tt1", "wp_wo0", "wp_wo1", "wl0", "wl1", "tt0", "tt1",
                 "ue_s", "ue_v", "gst", "gr0", "gr1", "kn", "k2t0", "k2t1",
                 "wd0", "wd1", "dn_ev", "ob", "sd0", "sd1", "sd2", "sd3",
                 "gd0", "gd1", "gd2", "gd3", "kv0", "kv1", "oe"]

    import contextlib
    with contextlib.ExitStack() as st_:
        sems = {n: Sem(st_.enter_context(nc.semaphore(n))) for n in sem_names}
        consts = st_.enter_context(nc.sbuf_tensor("consts", [128, NCOL], F32))
        ones_col = st_.enter_context(nc.sbuf_tensor("ones_c", [128, 1], F32))
        ones_row = st_.enter_context(nc.sbuf_tensor("ones_r", [1, 128], F32))
        xs = st_.enter_context(nc.sbuf_tensor("xs", [128, NT * D], F32))
        scU = st_.enter_context(nc.sbuf_tensor("scU", [128, NT], F32))
        m2 = st_.enter_context(nc.sbuf_tensor("m2", [128, NT], F32))
        s2c = st_.enter_context(nc.sbuf_tensor("s2c", [128, NT], F32))
        sc2 = st_.enter_context(nc.sbuf_tensor("sc2", [128, NT], F32))
        block = st_.enter_context(nc.Block())

        ch_v, ch_s, mm = sems["ch_v"], sems["ch_s"], sems["mm"]
        ld0, kn = sems["ld0"], sems["kn"]

        @block.vector
        def _(vector):
            for name, val in (("neg1", -1.0), ("invsq", INV_SQRT_DH),
                              ("epsln", EPS_LN), ("epsq", EPS_Q), ("one", 1.0)):
                ch_v.inc(vector.memset(consts[:, CC[name]:CC[name] + 1], val), 1)

        counts = [QW_SHAPES[n][0] * QW_SHAPES[n][1] for n in QW_NAMES]
        emit_scales(nc, block, sems, partials, consts, ones_col, ones_row, counts)

        @block.gpsimd
        def _(gpsimd):
            for t in range(NT):
                ld0.inc(gpsimd.dma_start(out=xs[:, t * D:(t + 1) * D],
                                         in_=x1_in[t * 128:(t + 1) * 128, :]), 16)
        x_loaded = ld0.n

        es0 = contextlib.ExitStack()
        qt_sb = es0.enter_context(nc.sbuf_tensor("qt_sb", [128, 16 * T_LOC], BF16))
        mask_sb = es0.enter_context(nc.sbuf_tensor("mask_sb", [128, NT * 512], F32))
        gwf = es0.enter_context(nc.sbuf_tensor("gwf", [128, 128], F32))
        gw16 = es0.enter_context(nc.sbuf_tensor("gw16", [128, 128], BF16))
        gwT = es0.enter_context(nc.sbuf_tensor("gwT", [128, 128], BF16))
        gbB = es0.enter_context(nc.sbuf_tensor("gbB", [128, 128], F32))
        gbr = es0.enter_context(nc.sbuf_tensor("gbr", [1, 128], F32))
        kbuf = es0.enter_context(nc.sbuf_tensor("kbuf", [128, 2 * SEQ], BF16))
        vbuf = es0.enter_context(nc.sbuf_tensor("vbuf", [128, 2 * 32 * 132], BF16))
        qsq = es0.enter_context(nc.sbuf_tensor("qsq", [128, T_LOC], F32))
        ksq = es0.enter_context(nc.sbuf_tensor("ksq", [128, SEQ], F32))
        kn2 = es0.enter_context(nc.sbuf_tensor("kn2", [1, SEQ], F32))
        qn2 = es0.enter_context(nc.sbuf_tensor("qn2", [1, T_LOC], F32))
        ptb = es0.enter_context(nc.sbuf_tensor("ptb", [128, 2 * 512], BF16))
        gsb = es0.enter_context(nc.sbuf_tensor("gsb", [128, 2 * 128], F32))
        obuf = es0.enter_context(nc.sbuf_tensor("obuf", [128, 2 * 128], BF16))
        qm2 = es0.enter_context(nc.sbuf_tensor("qm2", [1, 16], F32))
        km2 = es0.enter_context(nc.sbuf_tensor("km2", [1, 16], F32))
        nbrow = es0.enter_context(nc.sbuf_tensor("nbrow", [1, 16], F32))
        negB = es0.enter_context(nc.sbuf_tensor("negB", [128, 16], F32))
        rcolf = es0.enter_context(nc.sbuf_tensor("rcolf", [128, 2], F32))
        brec = Rec(nc, block, sems, "p2load")
        qts = []
        for h in range(16):
            qts.append(brec.op("syncd", lambda E, h=h: E.dma_start(
                out=qt_sb[:, h * T_LOC:(h + 1) * T_LOC],
                in_=qt_in[h * 128:(h + 1) * 128, :])))
        mld = brec.op("syncd", lambda E: E.dma_start(
            out=mask_sb[:, :], in_=mask_in[:, :]))
        gld = brec.op("syncd", lambda E: E.dma_start(
            out=gwf[:, :], in_=gw_in[:, :]))
        gbl = brec.op("syncd", lambda E: E.dma_start(
            out=gbr[:, :], in_=gb_in[:, :]))
        gcv = brec.op("vector", lambda E: E.tensor_copy(gw16[:, :], gwf[:, :]),
                      waits=(gld,))
        gtt = brec.op("syncd", lambda E: E.dma_start_transpose(
            out=gwT[:, :], in_=gw16[:, :]), waits=(gcv,))
        with nc.psum_tensor("ps_gb", [128, 512], F32) as psGB:
            gbm = brec.op("tensor", lambda E: E.matmul(
                psGB[:, 0:128], ones_row[:, 0:128], gbr[:, :],
                start=True, stop=True), waits=(gbl,))
            gbc = brec.op("vector", lambda E: E.tensor_copy(
                gbB[:, :], psGB[:, 0:128]), waits=(gbm,))
            brec.emit()

        entry = [("raw", sems["sd%d" % r].h, sems["sd%d" % r].n)
                 for r in range(4)] + [("raw", ch_v.h, ch_v.n)]
        fbufs = {"kbuf": kbuf, "vbuf": vbuf, "qsq": qsq, "ksq": ksq,
                 "kn2": kn2, "qn2": qn2, "ptb": ptb, "gsb": gsb,
                 "obuf": obuf, "qm2": qm2, "km2": km2, "nbrow": nbrow,
                 "negB": negB, "rcol": rcolf, "ones_row": ones_row,
                 "ones_col": ones_col}
        gate_ready = ("raw", ch_v.h, ch_v.n)
        emit_flash(nc, block, sems, qt_sb, ktf_in, vf_in, mask_sb, onat,
                   16, 1, (gwT, gbB, gate_ready), fbufs, entry)
        es0.close()
        flash_done = [("raw", sems["sd%d" % r].h, sems["sd%d" % r].n)
                      for r in range(4)]

        @block.gpsimd
        def _(gpsimd):
            for r in range(4):
                gpsimd.wait_ge(sems["sd%d" % r].h, sems["sd%d" % r].n)
            for t in range(NT):
                sems["oe"].inc(gpsimd.dma_start(
                    out=dbg_out[t * 128:(t + 1) * 128, :],
                    in_=onat[t * 128:(t + 1) * 128, :]), 16)

        es1 = contextlib.ExitStack()
        oT = es1.enter_context(nc.sbuf_tensor("oT", [128, 16 * T_LOC], BF16))
        wcho = es1.enter_context(nc.sbuf_tensor("wcho", [128, 2 * 16 * 512], BF16))
        @block.vector
        def _(vector):
            vector.wait_ge(ld0.h, x_loaded)
            ch_v.inc(vector.tensor_copy(scU[:, 0:1], scU[:, 0:1]), 1)
        emit_oproj_residual(nc, block, sems, onat, wt_o, xs, oT, wcho,
                            flash_done + [("raw", ld0.h, x_loaded)])
        es1.close()
        x2_ready = (sems["ch_v"], ch_v.n)

        esK = contextlib.ExitStack()
        k1T = esK.enter_context(nc.sbuf_tensor("k1T", [128, 16 * T_LOC], BF16))
        es2 = contextlib.ExitStack()
        sqs = es2.enter_context(nc.sbuf_tensor("sqs", [128, D], F32))
        ybuf = es2.enter_context(nc.sbuf_tensor("ybuf", [128, D], F32))
        kibuf = es2.enter_context(nc.sbuf_tensor("kibuf", [128, D], I32))
        dbuf = es2.enter_context(nc.sbuf_tensor("dbuf", [128, D], F32))
        t1buf = es2.enter_context(nc.sbuf_tensor("t1buf", [128, D], F32))
        nkbuf = es2.enter_context(nc.sbuf_tensor("nkbuf", [128, D], BF16))
        tcs = es2.enter_context(nc.sbuf_tensor("tcs", [128, 12], F32))
        emit_lnq_tiles(nc, block, sems, xs, k1nat, consts, tcs, sqs, ybuf,
                       kibuf, dbuf, t1buf, nkbuf,
                       (x2_ready[0], x2_ready[1]),
                       quant=True, sc_out=scU, sc_wcol=2, xn_nat=None)
        tt0 = sems["tt0"]
        k1_stored = kn.n

        @block.sync
        def _(sync):
            sync.wait_ge(kn.h, k1_stored)
            for dt in range(16):
                tt0.inc(sync.dma_start_transpose(
                    out=k1T[:, dt * T_LOC:(dt + 1) * T_LOC],
                    in_=k1nat[:, dt * 128:(dt + 1) * 128]), 16)

        es2.close()
        es3 = contextlib.ExitStack()
        wua = es3.enter_context(nc.sbuf_tensor("wua", [128, 2 * 16 * 512], BF16))
        wub = es3.enter_context(nc.sbuf_tensor("wub", [128, 2 * 16 * 512], BF16))
        gab = es3.enter_context(nc.sbuf_tensor("gab", [128, 2 * 512], F32))
        gbb2 = es3.enter_context(nc.sbuf_tensor("gbb2", [128, 2 * 512], F32))
        rmaxb = es3.enter_context(nc.sbuf_tensor("rmaxb", [128, 2], F32))
        emit_ffn_up_glu(nc, block, sems, k1T, wt_gu, g_dram, scU, m2,
                        Fh=F_L0, glu=True, silu=True,
                        bufs={"wua": wua, "wub": wub, "ga": gab,
                              "gb": gbb2, "rmax": rmaxb})
        es3.close()
        es4 = contextlib.ExitStack()
        grl = es4.enter_context(nc.sbuf_tensor("grl", [128, 2 * 1024], F32))
        y2 = es4.enter_context(nc.sbuf_tensor("y2", [128, 1024], F32))
        ki2 = es4.enter_context(nc.sbuf_tensor("ki2", [128, 1024], I32))
        d2 = es4.enter_context(nc.sbuf_tensor("d2", [128, 1024], F32))
        t12 = es4.enter_context(nc.sbuf_tensor("t12", [128, 1024], F32))
        nk2 = es4.enter_context(nc.sbuf_tensor("nk2", [128, 1024], BF16))
        k2T = es4.enter_context(nc.sbuf_tensor("k2T", [128, 2 * F_L0], BF16))
        wdn = es4.enter_context(nc.sbuf_tensor("wdn", [128, 8 * D], BF16))
        emit_quant_g(nc, block, sems, g_dram, k2nat, m2, s2c, sc2, consts,
                     grl, y2, ki2, d2, t12, nk2, dn_wi=1, Fh=F_L0,
                     extra_half=True)
        emit_ffn_down(nc, block, sems, k2nat, wt_dn, xs, sc2, k2T, wdn,
                      Fh=F_L0, Eo=D)
        es4.close()
        x3_ready_n = sems["dn_ev"].n

        es6 = contextlib.ExitStack()
        sqs = es6.enter_context(nc.sbuf_tensor("sqsB", [128, D], F32))
        ybuf = es6.enter_context(nc.sbuf_tensor("ybufB", [128, D], F32))
        kibuf = es6.enter_context(nc.sbuf_tensor("kibufB", [128, D], I32))
        dbuf = es6.enter_context(nc.sbuf_tensor("dbufB", [128, D], F32))
        t1buf = es6.enter_context(nc.sbuf_tensor("t1bufB", [128, D], F32))
        nkbuf = es6.enter_context(nc.sbuf_tensor("nkbufB", [128, D], BF16))
        tcs = es6.enter_context(nc.sbuf_tensor("tcsB", [128, 12], F32))
        emit_lnq_tiles(nc, block, sems, xs, k1nat, consts, tcs, sqs, ybuf,
                       kibuf, dbuf, t1buf, nkbuf,
                       (sems["dn_ev"], x3_ready_n),
                       quant=True, sc_out=scU, sc_wcol=4 * 2 + 2,
                       xn_nat=None)
        tt0 = sems["tt0"]
        k1b_stored = kn.n

        @block.sync
        def _(sync):
            sync.wait_ge(kn.h, k1b_stored)
            sync.wait_ge(sems["ue_v"].h, sems["ue_v"].n)
            for dt in range(16):
                tt0.inc(sync.dma_start_transpose(
                    out=k1T[:, dt * T_LOC:(dt + 1) * T_LOC],
                    in_=k1nat[:, dt * 128:(dt + 1) * 128]), 16)

        es6.close()
        es7 = contextlib.ExitStack()
        wua = es7.enter_context(nc.sbuf_tensor("wuaB", [128, 2 * 16 * 512], BF16))
        gab = es7.enter_context(nc.sbuf_tensor("gabB", [128, 2 * 512], F32))
        gbb2 = es7.enter_context(nc.sbuf_tensor("gbbB", [128, 2 * 512], F32))
        rmaxb = es7.enter_context(nc.sbuf_tensor("rmaxB", [128, 2], F32))
        emit_ffn_up_glu(nc, block, sems, k1T, wt_w1_in, g_dram, scU, m2,
                        Fh=F_L1, glu=False, silu=False,
                        bufs={"wua": wua, "wub": wua, "ga": gab,
                              "gb": gbb2, "rmax": rmaxb})
        es7.close()
        es8 = contextlib.ExitStack()
        grl = es8.enter_context(nc.sbuf_tensor("grlB", [128, 2 * 1024], F32))
        y2 = es8.enter_context(nc.sbuf_tensor("y2B", [128, 1024], F32))
        ki2 = es8.enter_context(nc.sbuf_tensor("ki2B", [128, 1024], I32))
        d2 = es8.enter_context(nc.sbuf_tensor("d2B", [128, 1024], F32))
        t12 = es8.enter_context(nc.sbuf_tensor("t12B", [128, 1024], F32))
        nk2 = es8.enter_context(nc.sbuf_tensor("nk2B", [128, 1024], BF16))
        k2T = es8.enter_context(nc.sbuf_tensor("k2TB", [128, 2 * F_L1], BF16))
        wdn = es8.enter_context(nc.sbuf_tensor("wdnB", [128, 8 * D], BF16))
        emit_quant_g(nc, block, sems, g_dram, k2nat, m2, s2c, sc2, consts,
                     grl, y2, ki2, d2, t12, nk2, dn_wi=3, Fh=F_L1,
                     extra_half=True)
        emit_ffn_down(nc, block, sems, k2nat, wt_w2_in, xs, sc2, k2T, wdn,
                      Fh=F_L1, Eo=D)
        es8.close()
        x4_ready_n = sems["dn_ev"].n
        esK.close()

        ob = sems["ob"]

        es9 = contextlib.ExitStack()
        sqs = es9.enter_context(nc.sbuf_tensor("sqsC", [128, D], F32))
        xnb = es9.enter_context(nc.sbuf_tensor("xnb", [128, D], BF16))
        tcs = es9.enter_context(nc.sbuf_tensor("tcsC", [128, 12], F32))
        emit_lnq_tiles(nc, block, sems, xs, None, consts, tcs, sqs, None,
                       None, None, None, xnb, (sems["dn_ev"], x4_ready_n),
                       quant=False, sc_out=None, sc_wcol=None,
                       xn_nat=xn2nat)
        es9.close()
        xn_stored = kn.n

        es10 = contextlib.ExitStack()
        qfull = es10.enter_context(nc.sbuf_tensor("qfull", [128, NT * D], BF16))
        kfull = es10.enter_context(nc.sbuf_tensor("kfull", [128, NT * 512], BF16))
        v16 = es10.enter_context(nc.sbuf_tensor("v16", [128, 512], BF16))
        gqB = es10.enter_context(nc.sbuf_tensor("gqB", [128, D], F32))
        gkB = es10.enter_context(nc.sbuf_tensor("gkB", [128, 512], F32))
        gqr = es10.enter_context(nc.sbuf_tensor("gqr", [1, D], F32))
        gkr = es10.enter_context(nc.sbuf_tensor("gkr", [1, 512], F32))
        esPJ = contextlib.ExitStack()
        xn2T = esPJ.enter_context(nc.sbuf_tensor("xn2T", [128, 16 * T_LOC], BF16))
        wchL = esPJ.enter_context(nc.sbuf_tensor("wchL", [128, 2 * 16 * 512], BF16))
        tt1 = sems["tt1"]

        @block.sync
        def _(sync):
            sync.wait_ge(kn.h, xn_stored)
            for dt in range(16):
                tt1.inc(sync.dma_start_transpose(
                    out=xn2T[:, dt * T_LOC:(dt + 1) * T_LOC],
                    in_=xn2nat[:, dt * 128:(dt + 1) * 128]), 16)

        @block.gpsimd
        def _(gpsimd):
            gpsimd.wait_ge(sems["dn_ev"].h, x4_ready_n)
            gpsimd.wait_ge(tt1.h, tt1.n)
            for t in range(NT):
                ob.inc(gpsimd.dma_start(out=x4_out[t * 128:(t + 1) * 128, :],
                                        in_=xs[:, t * D:(t + 1) * D]), 16)

        grec = Rec(nc, block, sems, "gains")
        g1 = grec.op("syncd", lambda E: E.dma_start(out=gqr[:, :], in_=gq_in[:, :]))
        g2 = grec.op("syncd", lambda E: E.dma_start(out=gkr[:, :], in_=gk_in[:, :]))
        with nc.psum_tensor("ps_gq", [128, 4 * 512], F32) as psGQ:
            mms = []
            for c in range(4):
                mms.append(grec.op("tensor", lambda E, c=c: E.matmul(
                    psGQ[:, c * 512:(c + 1) * 512], ones_row[:, 0:128],
                    gqr[:, c * 512:(c + 1) * 512], start=True, stop=True),
                    waits=(g1,)))
            cps = []
            for c in range(4):
                cps.append(grec.op("vector", lambda E, c=c: E.tensor_copy(
                    gqB[:, c * 512:(c + 1) * 512],
                    psGQ[:, c * 512:(c + 1) * 512]), waits=(mms[c],)))
            mk = grec.op("tensor", lambda E: E.matmul(
                psGQ[:, 0:512], ones_row[:, 0:128], gkr[:, :],
                start=True, stop=True), waits=(g2, cps[0]))
            grec.op("vector", lambda E: E.tensor_copy(
                gkB[:, :], psGQ[:, 0:512]), waits=(mk,))
            grec.emit()

        entry = [("raw", tt1.h, tt1.n),
                 ("raw", ch_v.h, ch_v.n)]
        emit_l1_proj(nc, block, sems, xn2T, wnq1_in, wnk1_in, wnv1_in,
                     qfull, kfull, v1_out, v16, wchL, entry)
        esPJ.close()

        with (
            nc.sbuf_tensor("qbD", [128, D], F32) as qb,
            nc.sbuf_tensor("rotD", [128, D], F32) as rot,
            nc.sbuf_tensor("tqD", [128, 2 * D], F32) as tq,
            nc.sbuf_tensor("tkD", [128, 2 * 512], F32) as tk,
            nc.sbuf_tensor("qssD", [128, 16], F32) as qss,
            nc.sbuf_tensor("kssD", [128, 4], F32) as kss,
            nc.sbuf_tensor("qb16D", [128, D], BF16) as qb16,
            nc.sbuf_tensor("kb16D", [128, 512], BF16) as kb16,
            nc.sbuf_tensor("kbD", [128, 1024], F32) as kb,
            nc.sbuf_tensor("TTD", [128, T_LOC], BF16) as TT,
        ):
            nrentry = [("raw", ch_v.h, ch_v.n),
                       ("raw", ch_s.h, ch_s.n)]
            nrbufs = {"qb": qb, "rot": rot, "tq": tq, "tk": tk,
                      "qss": qss, "kss": kss, "qb16": qb16, "kb16": kb16,
                      "kb": kb, "TT": TT}
            emit_l1_normrope(nc, block, sems, qfull, kfull, gqB, gkB,
                             tabs, qnat, knat, qt1_out, kt1_out, nrbufs,
                             nrentry)

        es10.close()
        @block.sync
        def _(sync):
            for s_ in (sems["ob"], sems["oe"], sems["ch_v"]):
                sync.wait_ge(s_.h, s_.n)
            for r in range(4):
                sync.wait_ge(sems["sd%d" % r].h, sems["sd%d" % r].n)
    return nc



def build_p3():
    nc = bass.Bass("TRN2", target_bir_lowering=False)
    x4_in = nc.dram_tensor("x4", [T_LOC, D], F32, kind="ExternalInput").ap()
    partials = nc.dram_tensor("partials", [8, len(QW_NAMES)], F32, kind="ExternalInput").ap()
    qt_in = nc.dram_tensor("qt1", [D, T_LOC], BF16, kind="ExternalInput").ap()
    ktf_in = nc.dram_tensor("ktf1", [512, SEQ], BF16, kind="ExternalInput").ap()
    vf_in = nc.dram_tensor("vf1", [SEQ, 512], BF16, kind="ExternalInput").ap()
    mask_in = nc.dram_tensor("mask", [128, NT * 512], F32, kind="ExternalInput").ap()
    wno1_in = nc.dram_tensor("wnat_o1", [D, D], BF16, kind="ExternalInput").ap()
    wt_w1_in = nc.dram_tensor("wt_L1w1", [D, F_L1], BF16, kind="ExternalInput").ap()
    wt_w2_in = nc.dram_tensor("wt_L1w2", [F_L1, D], BF16, kind="ExternalInput").ap()
    x6_out = nc.dram_tensor("x6", [T_LOC, D], F32, kind="ExternalOutput").ap()

    onat = nc.dram_tensor("onat", [T_LOC, D], BF16).ap()
    k1nat = nc.dram_tensor("k1nat", [T_LOC, D], BF16).ap()
    g_dram = nc.dram_tensor("g_dram", [T_LOC, F_L1], F32).ap()
    k2nat = nc.dram_tensor("k2nat", [T_LOC, F_L1], BF16).ap()

    sem_names = ["ld0", "ch_v", "ch_s", "ch_p", "mm", "wl0", "wl1", "tt0",
                 "tt1", "ue_s", "ue_v", "gst", "gr0", "gr1", "kn", "k2t0",
                 "k2t1", "wd0", "wd1", "dn_ev", "ob", "sd0", "sd1", "sd2",
                 "sd3", "gd0", "gd1", "gd2", "gd3", "kv0", "kv1", "oe"]

    import contextlib
    with contextlib.ExitStack() as st_:
        sems = {n: Sem(st_.enter_context(nc.semaphore(n))) for n in sem_names}
        consts = st_.enter_context(nc.sbuf_tensor("consts", [128, NCOL], F32))
        ones_col = st_.enter_context(nc.sbuf_tensor("ones_c", [128, 1], F32))
        ones_row = st_.enter_context(nc.sbuf_tensor("ones_r", [1, 128], F32))
        xs = st_.enter_context(nc.sbuf_tensor("xs", [128, NT * D], F32))
        scU = st_.enter_context(nc.sbuf_tensor("scU", [128, NT], F32))
        m2 = st_.enter_context(nc.sbuf_tensor("m2", [128, NT], F32))
        s2c = st_.enter_context(nc.sbuf_tensor("s2c", [128, NT], F32))
        sc2 = st_.enter_context(nc.sbuf_tensor("sc2", [128, NT], F32))
        block = st_.enter_context(nc.Block())

        ch_v, ch_s, mm = sems["ch_v"], sems["ch_s"], sems["mm"]
        ld0, kn = sems["ld0"], sems["kn"]

        @block.vector
        def _(vector):
            for name, val in (("neg1", -1.0), ("invsq", INV_SQRT_DH),
                              ("epsln", EPS_LN), ("epsq", EPS_Q), ("one", 1.0)):
                ch_v.inc(vector.memset(consts[:, CC[name]:CC[name] + 1], val), 1)

        counts = [QW_SHAPES[n][0] * QW_SHAPES[n][1] for n in QW_NAMES]
        emit_scales(nc, block, sems, partials, consts, ones_col, ones_row, counts)

        @block.gpsimd
        def _(gpsimd):
            for t in range(NT):
                ld0.inc(gpsimd.dma_start(out=xs[:, t * D:(t + 1) * D],
                                         in_=x4_in[t * 128:(t + 1) * 128, :]), 16)
        x_loaded = ld0.n

        esF = contextlib.ExitStack()
        qt_sb = esF.enter_context(nc.sbuf_tensor("qt_sb", [128, 16 * T_LOC], BF16))
        mask_sb = esF.enter_context(nc.sbuf_tensor("mask_sb", [128, NT * 512], F32))
        kbuf = esF.enter_context(nc.sbuf_tensor("kbuf", [128, 2 * SEQ], BF16))
        vbuf = esF.enter_context(nc.sbuf_tensor("vbuf", [128, 2 * 32 * 132], BF16))
        qsq = esF.enter_context(nc.sbuf_tensor("qsq", [128, T_LOC], F32))
        ksq = esF.enter_context(nc.sbuf_tensor("ksq", [128, SEQ], F32))
        kn2 = esF.enter_context(nc.sbuf_tensor("kn2", [1, SEQ], F32))
        qn2 = esF.enter_context(nc.sbuf_tensor("qn2", [1, T_LOC], F32))
        ptb = esF.enter_context(nc.sbuf_tensor("ptb", [128, 2 * 512], BF16))
        gsb = esF.enter_context(nc.sbuf_tensor("gsb", [128, 2 * 128], F32))
        obuf = esF.enter_context(nc.sbuf_tensor("obuf", [128, 2 * 128], BF16))
        qm2 = esF.enter_context(nc.sbuf_tensor("qm2", [1, 16], F32))
        km2 = esF.enter_context(nc.sbuf_tensor("km2", [1, 16], F32))
        nbrow = esF.enter_context(nc.sbuf_tensor("nbrow", [1, 16], F32))
        negB = esF.enter_context(nc.sbuf_tensor("negB", [128, 16], F32))
        rcolf = esF.enter_context(nc.sbuf_tensor("rcolf", [128, 2], F32))

        brec = Rec(nc, block, sems, "p3load")
        for h in range(16):
            brec.op("syncd", lambda E, h=h: E.dma_start(
                out=qt_sb[:, h * T_LOC:(h + 1) * T_LOC],
                in_=qt_in[h * 128:(h + 1) * 128, :]))
        brec.op("syncd", lambda E: E.dma_start(out=mask_sb[:, :], in_=mask_in[:, :]))
        brec.emit()

        entry = [("raw", sems["sd%d" % r].h, sems["sd%d" % r].n)
                 for r in range(4)] + [("raw", ch_v.h, ch_v.n)]
        fbufs = {"kbuf": kbuf, "vbuf": vbuf, "qsq": qsq, "ksq": ksq,
                 "kn2": kn2, "qn2": qn2, "ptb": ptb, "gsb": gsb,
                 "obuf": obuf, "qm2": qm2, "km2": km2, "nbrow": nbrow,
                 "negB": negB, "rcol": rcolf, "ones_row": ones_row,
                 "ones_col": ones_col}
        emit_flash(nc, block, sems, qt_sb, ktf_in, vf_in, mask_sb, onat,
                   16, 4, None, fbufs, entry)
        esF.close()
        flash_done = [("raw", sems["sd%d" % r].h, sems["sd%d" % r].n)
                      for r in range(4)]

        esO = contextlib.ExitStack()
        oT = esO.enter_context(nc.sbuf_tensor("oT", [128, 16 * T_LOC], BF16))
        wcho = esO.enter_context(nc.sbuf_tensor("wcho", [128, 2 * 16 * 512], BF16))

        @block.vector
        def _(vector):
            vector.wait_ge(ld0.h, x_loaded)
            ch_v.inc(vector.tensor_copy(scU[:, 0:1], scU[:, 0:1]), 1)
        emit_oproj_residual(nc, block, sems, onat, wno1_in, xs, oT, wcho,
                            flash_done + [("raw", ld0.h, x_loaded)])
        esO.close()
        x5_ready = (sems["ch_v"], ch_v.n)

        esK = contextlib.ExitStack()
        k1T = esK.enter_context(nc.sbuf_tensor("k1T", [128, 16 * T_LOC], BF16))
        esA = contextlib.ExitStack()
        sqs = esA.enter_context(nc.sbuf_tensor("sqs", [128, D], F32))
        ybuf = esA.enter_context(nc.sbuf_tensor("ybuf", [128, D], F32))
        kibuf = esA.enter_context(nc.sbuf_tensor("kibuf", [128, D], I32))
        dbuf = esA.enter_context(nc.sbuf_tensor("dbuf", [128, D], F32))
        t1buf = esA.enter_context(nc.sbuf_tensor("t1buf", [128, D], F32))
        nkbuf = esA.enter_context(nc.sbuf_tensor("nkbuf", [128, D], BF16))
        tcs = esA.enter_context(nc.sbuf_tensor("tcs", [128, 12], F32))
        emit_lnq_tiles(nc, block, sems, xs, k1nat, consts, tcs, sqs, ybuf,
                       kibuf, dbuf, t1buf, nkbuf, x5_ready,
                       quant=True, sc_out=scU, sc_wcol=4 * 2 + 2, xn_nat=None)
        tt0 = sems["tt0"]
        k1_stored = kn.n

        @block.sync
        def _(sync):
            sync.wait_ge(kn.h, k1_stored)
            for dt in range(16):
                tt0.inc(sync.dma_start_transpose(
                    out=k1T[:, dt * T_LOC:(dt + 1) * T_LOC],
                    in_=k1nat[:, dt * 128:(dt + 1) * 128]), 16)
        esA.close()

        esB = contextlib.ExitStack()
        wua = esB.enter_context(nc.sbuf_tensor("wua", [128, 2 * 16 * 512], BF16))
        gab = esB.enter_context(nc.sbuf_tensor("gab", [128, 2 * 512], F32))
        gbb2 = esB.enter_context(nc.sbuf_tensor("gbb2", [128, 2 * 512], F32))
        rmaxb = esB.enter_context(nc.sbuf_tensor("rmaxb", [128, 2], F32))
        emit_ffn_up_glu(nc, block, sems, k1T, wt_w1_in, g_dram, scU, m2,
                        Fh=F_L1, glu=False, silu=False,
                        bufs={"wua": wua, "wub": wua, "ga": gab, "gb": gbb2,
                              "rmax": rmaxb})
        esB.close()

        esC = contextlib.ExitStack()
        grl = esC.enter_context(nc.sbuf_tensor("grl", [128, 2 * 1024], F32))
        y2 = esC.enter_context(nc.sbuf_tensor("y2", [128, 1024], F32))
        ki2 = esC.enter_context(nc.sbuf_tensor("ki2", [128, 1024], I32))
        d2 = esC.enter_context(nc.sbuf_tensor("d2", [128, 1024], F32))
        t12 = esC.enter_context(nc.sbuf_tensor("t12", [128, 1024], F32))
        nk2 = esC.enter_context(nc.sbuf_tensor("nk2", [128, 1024], BF16))
        k2T = esC.enter_context(nc.sbuf_tensor("k2T", [128, 2 * F_L1], BF16))
        wdn = esC.enter_context(nc.sbuf_tensor("wdn", [128, 8 * D], BF16))
        emit_quant_g(nc, block, sems, g_dram, k2nat, m2, s2c, sc2, consts,
                     grl, y2, ki2, d2, t12, nk2, dn_wi=3, Fh=F_L1,
                     extra_half=True)
        emit_ffn_down(nc, block, sems, k2nat, wt_w2_in, xs, sc2, k2T, wdn,
                      Fh=F_L1, Eo=D)
        esC.close()
        esK.close()
        x6_ready_n = sems["dn_ev"].n

        ob = sems["ob"]

        @block.gpsimd
        def _(gpsimd):
            gpsimd.wait_ge(sems["dn_ev"].h, x6_ready_n)
            for t in range(NT):
                ob.inc(gpsimd.dma_start(out=x6_out[t * 128:(t + 1) * 128, :],
                                        in_=xs[:, t * D:(t + 1) * D]), 16)

        @block.sync
        def _(sync):
            sync.wait_ge(ob.h, ob.n)
            for r in range(4):
                sync.wait_ge(sems["sd%d" % r].h, sems["sd%d" % r].n)
    return nc



def _rope_tables(core, width, scale=1.0):
    k = core % 4
    pos = np.array([(k + 4 * (r // 128)) * 128 + r % 128 for r in range(T_LOC)],
                   dtype=np.float64)
    ang = 10000.0 ** (-np.arange(64) / 64.0)
    rad = pos[:, None] * ang[None, :]
    c = np.cos(rad)
    sn = np.sin(rad)
    nh = width // 128
    C = np.tile(np.concatenate([c, c], 1), (1, nh)) * scale
    S = np.tile(np.concatenate([sn, sn], 1), (1, nh)) * scale
    return (np.ascontiguousarray(C, dtype=np.float32),
            np.ascontiguousarray(S, dtype=np.float32))


def _build_mask(core):
    k = core % 4
    m = np.zeros((128, NT * 512), dtype=np.float32)
    for j in range(NT):
        tq = k + 4 * j
        for i in range(4):
            kt = 4 * j + i
            key_abs = kt * 128 + np.arange(128)[:, None]
            q_abs = tq * 128 + np.arange(128)[None, :]
            m[:, j * 512 + i * 128:j * 512 + (i + 1) * 128] = np.where(
                key_abs <= q_abs, 0.0, -1e30).astype(np.float32)
    return m


def kernel(L0_attn_x, L0_attn_q_w, L0_attn_k_w, L0_attn_v_w, L0_attn_o_w,
           L0_attn_gate_w, L0_attn_gate_b, L0_ffn_w_gate_up, L0_ffn_w_down,
           L1_attn_wq, L1_attn_wk, L1_attn_wv, L1_attn_wo, L1_attn_gq,
           L1_attn_gk, L1_ffn_w1, L1_ffn_w2):
    f32 = lambda a: np.ascontiguousarray(np.asarray(a), dtype=np.float32)
    x_full = f32(L0_attn_x).reshape(B, SEQ, D)
    ws = {"L0gu": f32(L0_ffn_w_gate_up), "L0dn": f32(L0_ffn_w_down),
          "L1w1": f32(L1_ffn_w1), "L1w2": f32(L1_ffn_w2)}
    wqkv0 = {"w_q": f32(L0_attn_q_w), "w_k": f32(L0_attn_k_w),
             "w_v": f32(L0_attn_v_w), "w_o": f32(L0_attn_o_w)}
    wl1 = {"w1q": f32(L1_attn_wq), "w1k": f32(L1_attn_wk),
           "w1v": f32(L1_attn_wv), "w1o": f32(L1_attn_wo)}

    nc0 = build_p0()
    in_maps0 = []
    for c in range(N_CORES):
        m = {}
        for n in QW_NAMES:
            r = QW_SHAPES[n][0] // 8
            m["w_" + n] = np.ascontiguousarray(ws[n][c * r:(c + 1) * r])
        in_maps0.append(m)
    res0 = run_bass_kernel_spmd(nc0, in_maps0, core_ids=list(range(N_CORES)))
    partials = np.concatenate([res0.results[c]["partials"]
                               for c in range(N_CORES)], 0)

    nc1 = build_p1()
    in_maps1 = []
    for c in range(N_CORES):
        sq = c // 4
        tiles = owned_tiles(c)
        xc = np.concatenate([x_full[sq, t * 128:(t + 1) * 128] for t in tiles], 0)
        im = {"x": np.ascontiguousarray(xc), "partials": partials,
              "w_L0gu": ws["L0gu"], "w_L0dn": ws["L0dn"],
              "w_L1w1": ws["L1w1"], "w_L1w2": ws["L1w2"]}
        im.update(wqkv0)
        im.update(wl1)
        in_maps1.append(im)
    res1 = run_bass_kernel_spmd(nc1, in_maps1, core_ids=list(range(N_CORES)))

    ktf, vf = {}, {}
    for sq in range(B):
        ktfull = np.zeros((D, SEQ), dtype=ml_dtypes.bfloat16)
        vfull = np.zeros((SEQ, D), dtype=ml_dtypes.bfloat16)
        for k in range(4):
            c = sq * 4 + k
            for j in range(NT):
                T = k + 4 * j
                ktfull[:, T * 128:(T + 1) * 128] = \
                    res1.results[c]["kt"][:, j * 128:(j + 1) * 128]
                vfull[T * 128:(T + 1) * 128, :] = \
                    res1.results[c]["v"][j * 128:(j + 1) * 128, :]
        ktf[sq] = ktfull
        vf[sq] = vfull

    nc2 = build_p2()
    in_maps2 = []
    for c in range(N_CORES):
        sq = c // 4
        qc, qs_ = _rope_tables(c, D)
        kc, ks_ = _rope_tables(c, 512, scale=INV_SQRT_DH)
        r1 = res1.results[c]
        in_maps2.append({
            "x1": r1["x1"], "partials": partials, "qt": r1["qt"],
            "ktf": ktf[sq], "vf": vf[sq], "mask": _build_mask(c),
            "gate_w": f32(L0_attn_gate_w),
            "gate_b": f32(L0_attn_gate_b).reshape(1, 128),
            "wt_L0gu": r1["wt_L0gu"], "wt_L0dn": r1["wt_L0dn"],
            "wt_o": r1["wt_o"], "wt_L1w1": r1["wt_L1w1"],
            "wt_L1w2": r1["wt_L1w2"], "wnat_q1": r1["wnat_q1"],
            "wnat_k1": r1["wnat_k1"], "wnat_v1": r1["wnat_v1"],
            "gq_row": f32(L1_attn_gq).reshape(1, D),
            "gk_row": f32(L1_attn_gk).reshape(1, 512),
            "tab_qc": qc, "tab_qs": qs_, "tab_kc": kc, "tab_ks": ks_,
        })
    res2 = run_bass_kernel_spmd(nc2, in_maps2, core_ids=list(range(N_CORES)))

    ktf1, vf1 = {}, {}
    for sq in range(B):
        ktfull = np.zeros((512, SEQ), dtype=ml_dtypes.bfloat16)
        vfull = np.zeros((SEQ, 512), dtype=ml_dtypes.bfloat16)
        for k in range(4):
            c = sq * 4 + k
            for j in range(NT):
                T = k + 4 * j
                ktfull[:, T * 128:(T + 1) * 128] = \
                    res2.results[c]["kt1"][:, j * 128:(j + 1) * 128]
                vfull[T * 128:(T + 1) * 128, :] = \
                    res2.results[c]["v1"][j * 128:(j + 1) * 128, :]
        ktf1[sq] = ktfull
        vf1[sq] = vfull

    nc3 = build_p3()
    in_maps3 = []
    for c in range(N_CORES):
        sq = c // 4
        r1 = res1.results[c]
        r2 = res2.results[c]
        in_maps3.append({
            "x4": r2["x4"], "partials": partials, "qt1": r2["qt1"],
            "ktf1": ktf1[sq], "vf1": vf1[sq], "mask": _build_mask(c),
            "wnat_o1": r1["wnat_o1"], "wt_L1w1": r1["wt_L1w1"],
            "wt_L1w2": r1["wt_L1w2"],
        })
    res3 = run_bass_kernel_spmd(nc3, in_maps3, core_ids=list(range(N_CORES)))

    out = np.zeros((B, SEQ, D), dtype=np.float32)
    for c in range(N_CORES):
        sq = c // 4
        x6 = res3.results[c]["x6"]
        for j, t in enumerate(owned_tiles(c)):
            out[sq, t * 128:(t + 1) * 128] = x6[j * 128:(j + 1) * 128]
    return out.reshape(np.asarray(L0_attn_x).shape)
